# version 20
# baseline (speedup 1.0000x reference)
import numpy as np
from contextlib import ExitStack

import concourse.bass as bass
import concourse.mybir as mybir
from concourse.bass_utils import run_bass_kernel_spmd

F32 = mybir.dt.float32
BF16 = mybir.dt.bfloat16
I32 = mybir.dt.int32
AF = mybir.ActivationFunctionType
ALU = mybir.AluOpType
AX = mybir.AxisListType

SEQ = 8192
D = 2048
DEBUG_TAPS = False
NH = 16
KVH = 4
INTER = 5504
NIT = 43
HT = 16
DT = 16
BLK = 256
NSLOT = 4
NCORES = 8
OWN = NSLOT * BLK
EPS = 1e-5
NSUP = SEQ // 512
NSUPQ = OWN // 512

MAT_PUMP_P1 = 6
MAT_SG_P1 = 8
MAT_PUMP_P2 = (3, 5, 8, 11)
MAT_SG_P2 = 2
MAT_PUMP_FFN = 6


class Sem:
    def __init__(self, nc, es, name):
        self.h = es.enter_context(nc.semaphore(name))
        self.n = 0


class Builder:
    def __init__(self):
        self.nc = bass.Bass("TRN2", target_bir_lowering=False)
        nc = self.nc
        self.es = ExitStack()
        self.tok = {}

        def din(name, shape, dt=F32):
            return nc.dram_tensor(name, shape, dt, kind="ExternalInput").ap()

        self.x_full = din("x_full", [SEQ, D])
        self.x_own = din("x_own", [OWN, D])
        self.wqT = din("wqT", [D, D])
        self.wkT = din("wkT", [D, 512])
        self.wvT = din("wvT", [D, 512])
        self.woTr = din("woTr", [DT, 128, NH * 128])
        self.boP = din("boP", [128, DT])
        self.wvbT = din("wvbT", [D, 512])
        self.wpT = din("wpT", [D, D])
        self.lutp = din("lutp", [128, 16])
        self.gidxT = din("gidxT", [D, INTER], I32)
        self.uidxT = din("uidxT", [D, INTER], I32)
        self.didxT = din("didxT", [INTER, D], I32)
        self.cmask = din("cmask", [NSLOT, 128, 16 * BLK])
        self.ident = din("ident", [128, 128])
        self.y_own = nc.dram_tensor("y_own", [OWN, D], F32,
                                    kind="ExternalOutput").ap()
        if DEBUG_TAPS:
            self.t_x1 = nc.dram_tensor("t_x1", [OWN, D], F32,
                                       kind="ExternalOutput").ap()
            self.t_x2 = nc.dram_tensor("t_x2", [OWN, D], F32,
                                       kind="ExternalOutput").ap()
            self.t_x3 = nc.dram_tensor("t_x3", [OWN, D], F32,
                                       kind="ExternalOutput").ap()
            self.t_wg = nc.dram_tensor("t_wg", [128, HT * 128], F32,
                                       kind="ExternalOutput").ap()
            self.t_wd = nc.dram_tensor("t_wd", [128, NIT * 128], F32,
                                       kind="ExternalOutput").ap()
            self.t_wu = nc.dram_tensor("t_wu", [128, HT * 128], F32,
                                       kind="ExternalOutput").ap()
            self.t_k = nc.dram_tensor("t_k", [128, 1024], F32,
                                      kind="ExternalOutput").ap()
            self.t_v = nc.dram_tensor("t_v", [128, 1024], F32,
                                      kind="ExternalOutput").ap()
            self.t_q = nc.dram_tensor("t_q", [128, 1024], F32,
                                      kind="ExternalOutput").ap()

        self.d_K = nc.dram_tensor("d_K", [KVH, 128, SEQ], BF16, kind="Internal").ap()
        self.d_V = nc.dram_tensor("d_V", [KVH, 128, SEQ], BF16,
                                  kind="Internal").ap()
        self.d_Q = nc.dram_tensor("d_Q", [NH, 128, 1024], BF16,
                                  kind="Internal").ap()
        self.d_Wg = nc.dram_tensor("d_Wg", [NIT, 128, HT * 128], BF16,
                                   kind="Internal").ap()
        self.d_Wu = nc.dram_tensor("d_Wu", [NIT, 128, HT * 128], BF16,
                                   kind="Internal").ap()
        self.d_Wd = nc.dram_tensor("d_Wd", [DT, 128, NIT * 128], BF16,
                                   kind="Internal").ap()
        self.d_X = [nc.dram_tensor(f"d_X{i}", [OWN, D], F32, kind="Internal").ap()
                    for i in (1, 2, 3)]

        def mk(name):
            return Sem(nc, self.es, name)

        def mk2(name):
            return [Sem(nc, self.es, name + "0"), Sem(nc, self.es, name + "1")]

        self.sWS = mk("sWS")
        self.sWC = mk("sWC")
        self.sX = mk2("sX")
        self.sKV = mk2("sKV")
        self.sQ2 = mk2("sQ2")
        self.sWU = mk2("sWU")
        self.sWD = mk2("sWD")
        self.sWO = mk2("sWO")
        self.sXT = mk("sXT")
        self.sG = mk2("sG")
        self.sSg = mk2("sSg")
        self.sST = mk2("sST")
        self.sXR = mk2("sXR")
        self.sFX = mk("sFX")
        self.dM2 = mk2("dM")
        self.dMs = mk2("dMs")
        self.vS = Sem(nc, self.es, "vS")
        self.sS = Sem(nc, self.es, "sS")
        self.gS = Sem(nc, self.es, "gS")
        self.tM = Sem(nc, self.es, "tM")

        def sb(name, shape, dt):
            return nc.alloc_sbuf_tensor(name, shape, dt).ap()

        self.lutP = sb("lutP", [128, 16], F32)
        self.identb = sb("identb", [128, 128], BF16)
        self.onesb = sb("onesb", [128, 1], BF16)
        self.ones1f = sb("ones1f", [1, 128], F32)
        self.epsA = sb("epsA", [128, 1], F32)
        self.negone = sb("negone", [128, 1], F32)
        self.st6 = sb("st6", [128, 24], F32)
        self.mv2 = sb("mv2", [128, 2], F32)
        self.veps = sb("veps", [128, 1], F32)
        self.ivar4 = sb("ivar4", [128, 4], F32)
        self.negm4 = sb("negm4", [128, 4], F32)
        self.rstd4 = sb("rstd4", [128, 4], F32)
        self.lnb4 = sb("lnb4", [128, 4], F32)
        self.xb = [sb(f"xb{i}", [128, D], F32) for i in range(2)]
        self.lnt = [sb(f"lnt{i}", [128, D], BF16) for i in range(4)]
        self.ksum = sb("ksum", [128, 1], F32)
        self.negmean = sb("negmean", [128, 1], F32)
        self.rL = sb("rL", [1, 256], F32)
        self.jneg = sb("jneg", [128, 16], F32)
        self.onefF = sb("onefF", [128, 1], F32)
        self.dnbf = [sb(f"dnbf{i}", [128, 512], BF16) for i in range(2)]

        self.mat_stack = ExitStack()
        self.midx = [self.mat_stack.enter_context(
            nc.sbuf_tensor(f"midx{i}", [128, INTER], BF16, side="right")).ap()
            for i in range(2)]
        self.macc = [self.mat_stack.enter_context(
            nc.sbuf_tensor(f"macc{i}", [128, INTER], BF16, side="right")).ap()
            for i in range(2)]
        self.mtj = self.mat_stack.enter_context(
            nc.sbuf_tensor("mtj", [128, INTER], BF16, side="right")).ap()
        SGW = 1408
        self.SGW = SGW
        self.msq = self.mat_stack.enter_context(
            nc.sbuf_tensor("msq", [128, SGW], BF16, side="right")).ap()
        self.msq2 = self.mat_stack.enter_context(
            nc.sbuf_tensor("msq2", [128, SGW], BF16, side="right")).ap()
        self.mind = self.mat_stack.enter_context(
            nc.sbuf_tensor("mind", [128, 2 * SGW], BF16, side="right")).ap()
        self.midx_sg = self.mat_stack.enter_context(
            nc.sbuf_tensor("midx_sg", [128, 2 * SGW], BF16, side="right")).ap()
        self.macc_sg = self.mat_stack.enter_context(
            nc.sbuf_tensor("macc_sg", [128, 2 * SGW], BF16, side="right")).ap()

        def psu(name, shape, dt=F32):
            return nc.alloc_psum_tensor(name, shape, dt).ap()

        self.psA = psu("psA", [128, 512])
        self.psB = psu("psB", [128, 512])
        self.psC = psu("psC", [128, 512])
        self.psD = psu("psD", [128, 512])
        self.psE = psu("psE", [128, 512])
        self.psF = psu("psF", [128, 512])
        self.psL = self.psF[0:1, 0:256]
        self.psT = [psu(f"psT{i}", [128, 512]) for i in range(2)]

        self.mat_jobs = ([("g", i) for i in range(HT)]
                         + [("u", i) for i in range(HT)]
                         + [("d", i) for i in range(NIT)])
        SG_JOBS = {30, 31, 71, 72, 73, 74}
        self.vg_jobs = {ji for ji in range(39, 2 * HT + NIT)
                        if ji % 2 == 1 and ji not in SG_JOBS}
        self.mat = dict(loads=0, load_tok={}, store_tok={}, done={},
                        cap=len(self.mat_jobs), sg_jobs=SG_JOBS,
                        vq=[i for i in range(len(self.mat_jobs))
                            if i not in SG_JOBS],
                        vqi=0, vj=0, buf_of={}, vg_tok={},
                        squnits=[], squi=0, sj=0, sg_ltok={}, sg_sttok={},
                        sg_stok={})
        for ji in sorted(SG_JOBS):
            kind, idx = self.mat_jobs[ji]
            W = INTER if kind in ("g", "u") else D
            qs_ = []
            c = 0
            while c < W:
                w = min(1408, W - c)
                qs_.append((ji, c, w))
                c += w
            self.mat["squnits"].extend(qs_)

    def dma(self, q, out, in_, sem):
        q.dma_start(out=out, in_=in_).then_inc(sem.h, 16)
        sem.n += 16
        return (sem, sem.n)

    def dmaw(self, eng, tokv):
        eng.wait_ge(tokv[0].h, tokv[1])

    def store_sems(self):
        return [self.sST[0], self.sST[1], self.sXR[0], self.sXR[1], self.sFX,
                self.dM2[0], self.dM2[1], self.dMs[0], self.dMs[1]]

    def store_snapshot(self):
        return [(s, s.n) for s in self.store_sems() if s.n > 0]

    def vchain(self, v, emit):
        v.wait_ge(self.vS.h, self.vS.n)
        emit().then_inc(self.vS.h, 1)
        self.vS.n += 1
        return self.vS.n

    def vind(self, v, emit):
        emit().then_inc(self.vS.h, 1)
        self.vS.n += 1
        return self.vS.n

    def schain(self, s, emit):
        s.wait_ge(self.sS.h, self.sS.n)
        emit().then_inc(self.sS.h, 1)
        self.sS.n += 1
        return self.sS.n

    def sind(self, s, emit):
        emit().then_inc(self.sS.h, 1)
        self.sS.n += 1
        return self.sS.n

    def mat_src(self, kind, idx):
        if kind == "g":
            dst = self.d_Wg[:, :, 128 * idx:128 * (idx + 1)].rearrange(
                "it p f -> p it f")
            return (self.gidxT[128 * idx:128 * (idx + 1), :], INTER, dst)
        if kind == "u":
            dst = self.d_Wu[:, :, 128 * idx:128 * (idx + 1)].rearrange(
                "it p f -> p it f")
            return (self.uidxT[128 * idx:128 * (idx + 1), :], INTER, dst)
        dst = self.d_Wd[:, :, 128 * idx:128 * (idx + 1)].rearrange(
            "d p f -> p d f")
        return (self.didxT[128 * idx:128 * (idx + 1), :], D, dst)

    def mat_gpsimd_service(self, g):
        m = self.mat
        for ji in sorted(m["done"].keys()):
            if ji in m["store_tok"]:
                continue
            kind, idx = self.mat_jobs[ji]
            _, w, dst = self.mat_src(kind, idx)
            sem, val = m["done"][ji]
            g.wait_ge(sem.h, val)
            bp = m["buf_of"][ji] % 2
            m["store_tok"][ji] = self.dma(
                g, dst, self.macc[bp][:, 0:w].rearrange("p (a f) -> p a f",
                                                        f=128), self.dM2[bp])
        while m["loads"] < m["cap"]:
            ji = m["vq"][m["loads"]] if m["loads"] < len(m["vq"]) else None
            if ji is None:
                break
            prev2 = m["vq"][m["loads"] - 2] if m["loads"] >= 2 else None
            prev4 = m["vq"][m["loads"] - 4] if m["loads"] >= 4 else None
            if prev2 is not None and prev2 not in m["done"]:
                break
            if prev4 is not None and prev4 not in m["store_tok"]:
                break
            kind, idx = self.mat_jobs[ji]
            srcap, w, _ = self.mat_src(kind, idx)
            if prev2 is not None:
                sem, val = m["done"][prev2]
                g.wait_ge(sem.h, val)
            if m.get("stage2_guard") and not m.get("stage2_guard_done"):
                for sv in m["stage2_guard"]:
                    self.dmaw(g, sv)
                m["stage2_guard_done"] = True
            li = m["loads"]
            m["load_tok"][ji] = self.dma(g, self.midx[li % 2][:, 0:w], srcap,
                                         self.sG[li % 2])
            m["buf_of"][ji] = li % 2
            m["loads"] += 1

    def mat_pump_drv(self, blk, nj):
        m = self.mat
        left = nj
        while left > 0 and m["vqi"] < len(m["vq"]):
            ji = m["vq"][m["vqi"]]
            if ji not in m["load_tok"]:
                break
            kind, idx = self.mat_jobs[ji]
            _, w, _ = self.mat_src(kind, idx)
            b = m["buf_of"][ji]
            hw = w // 2
            vg = ji in self.vg_jobs
            j0 = m["vj"]
            j1 = min(16, j0 + min(left, 1 if vg else left))
            left -= (j1 - j0)

            @blk.vector
            def _(v, ji=ji, j0=j0, j1=j1, b=b, hw=hw, vg=vg, w=w):
                if j0 == 0:
                    self.dmaw(v, m["load_tok"][ji])
                    if "consts" in self.tok:
                        self.dmaw(v, self.tok["consts"])
                    prev = m["vq"][m["vqi"] - 1] if m["vqi"] >= 1 else None
                    pd = m["done"].get(prev) if prev is not None else None
                    if pd is not None and pd[0] is self.gS:
                        v.wait_ge(self.gS.h, pd[1])
                    if not vg:
                        prev2 = (m["vq"][m["vqi"] - 2]
                                 if m["vqi"] >= 2 else None)
                        stok = (m["store_tok"].get(prev2)
                                if prev2 is not None else None)
                        if stok is not None:
                            self.dmaw(v, stok)
                        self.vchain(v, lambda: v.memset(self.macc[b][:, 0:w],
                                                        0.0))
                for j in range(j0, j1):
                    for hf in range(2):
                        sl = slice(hw * hf, hw * (hf + 1))
                        if vg:
                            gtok = m["vg_tok"].get(("g", ji, 2 * j + hf - 2))
                            if gtok:
                                v.wait_ge(self.gS.h, gtok)
                            ms = self.mtj[:, hw * ((2 * j + hf) % 2):
                                          hw * ((2 * j + hf) % 2) + hw]
                            self.vchain(v, lambda j=j, sl=sl, ms=ms:
                                        v.tensor_scalar(
                                            ms, self.midx[b][:, sl], float(j),
                                            self.lutP[:, j:j + 1],
                                            ALU.is_equal, ALU.mult))
                            m["vg_tok"][("v", ji, 2 * j + hf)] = self.vS.n
                        else:
                            self.vchain(v, lambda j=j, sl=sl: v.tensor_scalar(
                                self.mtj[:, 0:hw], self.midx[b][:, sl],
                                float(j), self.lutP[:, j:j + 1],
                                ALU.is_equal, ALU.mult))
                            self.vchain(v, lambda sl=sl: v.tensor_tensor(
                                self.macc[b][:, sl], self.macc[b][:, sl],
                                self.mtj[:, 0:hw], ALU.add))

            if vg:
                @blk.gpsimd
                def _(g, ji=ji, j0=j0, j1=j1, b=b, hw=hw, w=w):
                    if j0 == 0:
                        self.dmaw(g, m["load_tok"][ji])
                        prev2 = m["vq"][m["vqi"] - 2] if m["vqi"] >= 2 else None
                        stok = (m["store_tok"].get(prev2)
                                if prev2 is not None else None)
                        if stok is not None:
                            self.dmaw(g, stok)
                    for j in range(j0, j1):
                        for hf in range(2):
                            sl = slice(hw * hf, hw * (hf + 1))
                            g.wait_ge(self.vS.h, m["vg_tok"][("v", ji, 2 * j + hf)])
                            g.wait_ge(self.gS.h, self.gS.n)
                            ms = self.mtj[:, hw * ((2 * j + hf) % 2):
                                          hw * ((2 * j + hf) % 2) + hw]
                            if j == 0:
                                g.tensor_copy(self.macc[b][:, sl],
                                              ms).then_inc(self.gS.h, 1)
                            else:
                                g.tensor_tensor(self.macc[b][:, sl],
                                                self.macc[b][:, sl], ms,
                                                ALU.add).then_inc(self.gS.h, 1)
                            self.gS.n += 1
                            m["vg_tok"][("g", ji, 2 * j + hf)] = self.gS.n

            m["vj"] = j1
            if j1 == 16:
                if vg:
                    m["done"][ji] = (self.gS, self.gS.n)
                else:
                    m["done"][ji] = (self.vS, self.vS.n)
                m["vqi"] += 1
                m["vj"] = 0

    def mat_pump_sg(self, blk, nj):
        m = self.mat
        left = nj
        while left > 0:
            if m["squi"] >= len(m["squnits"]):
                return
            ui = m["squi"]
            ji, c0, w = m["squnits"][ui]
            kind, idx = self.mat_jobs[ji]
            srcap, _, dstap = self.mat_src(kind, idx)
            slot = ui % 2
            if ui not in m["sg_ltok"]:
                @blk.gpsimd
                def _(g, ui=ui):
                    for u2 in (ui, ui + 1):
                        if u2 >= len(m["squnits"]) or u2 in m["sg_ltok"]:
                            continue
                        ji2, c2, w2 = m["squnits"][u2]
                        k2, i2 = self.mat_jobs[ji2]
                        src2, _, _ = self.mat_src(k2, i2)
                        st = m["sg_sttok"].get(u2 - 2)
                        if st is not None:
                            self.dmaw(g, st)
                        ra = m["sg_sttok"].get("realloc")
                        if ra is not None and not m.get(f"sg_ra_done_{u2 % 2}"):
                            for sv in ra:
                                self.dmaw(g, sv)
                            m[f"sg_ra_done_{u2 % 2}"] = True
                        m["sg_ltok"][u2] = self.dma(
                            g, self.midx_sg[:, self.SGW * (u2 % 2):
                                            self.SGW * (u2 % 2) + w2],
                            src2[:, c2:c2 + w2], self.sSg[u2 % 2])
            j0 = m["sj"]
            j1 = min(16, j0 + min(left, 2))
            left -= (j1 - j0)

            @blk.scalar
            def _(s, ui=ui, ji=ji, j0=j0, j1=j1, w=w, slot=slot):
                for j in range(j0, j1):
                    if j == 0 and j0 == 0:
                        self.dmaw(s, m["sg_ltok"][ui])
                        s.wait_ge(self.vS.h, self.tok["consts_v"])
                        self.dmaw(s, self.tok["consts"])
                        s.wait_ge(self.gS.h, self.gS.n)
                    gtok = m["sg_stok"].get(("g", ui, j - 2))
                    if gtok:
                        s.wait_ge(self.gS.h, gtok)
                    mi = self.midx_sg[:, self.SGW * slot:self.SGW * slot + w]
                    self.schain(s, lambda j=j, mi=mi: s.activation(
                        self.msq[:, 0:w], mi, AF.Square,
                        bias=self.jneg[:, j:j + 1]))
                    self.schain(s, lambda: s.activation(
                        self.msq2[:, 0:w], self.msq[:, 0:w],
                        AF.Relu, bias=self.onefF[:, :], scale=-1.0))
                    self.schain(s, lambda j=j: s.activation(
                        self.mind[:, self.SGW * (j % 2):self.SGW * (j % 2) + w],
                        self.msq2[:, 0:w], AF.Identity,
                        scale=self.lutP[:, j:j + 1]))
                    m["sg_stok"][("s", ui, j)] = self.sS.n

            @blk.gpsimd
            def _(g, ui=ui, j0=j0, j1=j1, w=w, slot=slot):
                if j0 == 0:
                    self.dmaw(g, m["sg_ltok"][ui])
                    st = m["sg_sttok"].get(ui - 2)
                    if st is not None:
                        self.dmaw(g, st)
                    g.wait_ge(self.gS.h, self.gS.n)
                    g.memset(self.macc_sg[:, self.SGW * slot:
                                          self.SGW * slot + w],
                             0.0).then_inc(self.gS.h, 1)
                    self.gS.n += 1
                for j in range(j0, j1):
                    g.wait_ge(self.sS.h, m["sg_stok"][("s", ui, j)])
                    g.wait_ge(self.gS.h, self.gS.n)
                    g.tensor_tensor(
                        self.macc_sg[:, self.SGW * slot:self.SGW * slot + w],
                        self.macc_sg[:, self.SGW * slot:self.SGW * slot + w],
                        self.mind[:, self.SGW * (j % 2):self.SGW * (j % 2) + w],
                        ALU.add).then_inc(self.gS.h, 1)
                    self.gS.n += 1
                    m["sg_stok"][("g", ui, j)] = self.gS.n
                if j1 == 16:
                    g.wait_ge(self.gS.h, self.gS.n)
                    dst = dstap[:, c0 // 128:(c0 + w) // 128, :] \
                        if False else None
                    kindq, idxq = self.mat_jobs[ji]
                    _, _, dfull = self.mat_src(kindq, idxq)
                    m["sg_sttok"][ui] = self.dma(
                        g, dfull[:, c0 // 128:(c0 + w) // 128, :],
                        self.macc_sg[:, self.SGW * slot:self.SGW * slot + w
                                     ].rearrange("p (a f) -> p a f", f=128),
                        self.dMs[ui % 2])
                    if ui + 1 >= len(m["squnits"]) or \
                            m["squnits"][ui + 1][0] != ji:
                        m["store_tok"][ji] = m["sg_sttok"][ui]

            m["sj"] = j1
            if j1 == 16:
                m["squi"] += 1
                m["sj"] = 0

    def mat_pump_section(self, blk, nj, nsg=0):
        @blk.gpsimd
        def _(g):
            self.mat_gpsimd_service(g)

        if nsg:
            self.mat_pump_sg(blk, nsg)
        self.mat_pump_drv(blk, nj)

    def mat_drain(self, blk, upto=None):
        if upto is None:
            upto = len(self.mat_jobs)
        guard = 0
        while any(ji not in self.mat["store_tok"] for ji in range(upto)):
            self.mat_pump_section(blk, 48, nsg=16)
            guard += 1
            assert guard < 900, "mat drain stuck"

    def mat_stage2(self, blk):
        m = self.mat
        self.mat["cap"] = max(self.mat["loads"], 1)
        guard = 0
        gu_jobs = [ji for ji in range(2 * HT)]
        loaded = set(list(m["load_tok"].keys()))
        sg_started = set(u[0] for i, u in enumerate(m["squnits"])
                         if i <= m["squi"] or i in m["sg_ltok"])
        need = set(gu_jobs) | loaded | sg_started
        while any(ji not in m["store_tok"] for ji in need):
            self.mat_pump_section(blk, 48, nsg=16)
            loaded = set(list(m["load_tok"].keys()))
            sg_started = set(u[0] for i, u in enumerate(m["squnits"])
                             if i <= m["squi"] or i in m["sg_ltok"])
            need = set(gu_jobs) | loaded | sg_started
            guard += 1
            assert guard < 900, "mat stage2 drain stuck"
        m["stage2_guard"] = [(self.dM2[0], self.dM2[0].n),
                             (self.dM2[1], self.dM2[1].n),
                             (self.dMs[0], self.dMs[0].n),
                             (self.dMs[1], self.dMs[1].n)]
        m["stage2_guard_done"] = False
        self.mat_stack.close()
        nc = self.nc
        self.mat_stack = ExitStack()
        self.midx = [self.mat_stack.enter_context(
            nc.sbuf_tensor(f"midxB{i}", [128, D], BF16, side="right")).ap()
            for i in range(2)]
        self.macc = [self.mat_stack.enter_context(
            nc.sbuf_tensor(f"maccB{i}", [128, D], BF16, side="right")).ap()
            for i in range(2)]
        self.mtj = self.mat_stack.enter_context(
            nc.sbuf_tensor("mtjB", [128, D], BF16, side="right")).ap()
        SGW = 512
        self.SGW = SGW
        self.msq = self.mat_stack.enter_context(
            nc.sbuf_tensor("msqB", [128, SGW], BF16, side="right")).ap()
        self.msq2 = self.mat_stack.enter_context(
            nc.sbuf_tensor("msq2B", [128, SGW], BF16, side="right")).ap()
        self.mind = self.mat_stack.enter_context(
            nc.sbuf_tensor("mindB", [128, 2 * SGW], BF16, side="right")).ap()
        self.midx_sg = self.mat_stack.enter_context(
            nc.sbuf_tensor("midx_sgB", [128, 2 * SGW], BF16, side="right")).ap()
        self.macc_sg = self.mat_stack.enter_context(
            nc.sbuf_tensor("macc_sgB", [128, 2 * SGW], BF16, side="right")).ap()
        self.mat["cap"] = len(self.mat_jobs)

    def mat_store_tok(self, last_job):
        best = {}
        for ji in range(last_job + 1):
            t = self.mat["store_tok"].get(ji)
            assert t is not None, "mat job not stored before use"
            sem, val = t
            best[id(sem)] = (sem, max(val, best.get(id(sem), (sem, 0))[1]))
        return list(best.values())

    def ln_vector(self, v, xin, col, extra_waits=()):
        for sem, val in extra_waits:
            if val:
                v.wait_ge(sem.h, val)
        self.vchain(v, lambda: v.bn_stats(self.st6[:, 0:6], xin[:, 0:512]))
        for q in range(1, 4):
            self.vind(v, lambda q=q: v.bn_stats(
                self.st6[:, 6 * q:6 * q + 6], xin[:, 512 * q:512 * (q + 1)]))
        self.vchain(v, lambda: v.bn_aggr(self.mv2[:, :], self.st6[:, :]))
        self.vchain(v, lambda: v.tensor_scalar(
            self.ivar4[:, col:col + 1], self.mv2[:, 1:2], self.epsA[:, :],
            None, ALU.add))
        return self.vchain(v, lambda: v.tensor_scalar(
            self.negm4[:, col:col + 1], self.mv2[:, 0:1], self.negone[:, :],
            None, ALU.mult))

    def ln_scalar(self, s, xin, lnout, col, vtok, extra_waits=()):
        s.wait_ge(self.vS.h, vtok)
        for sem, val in extra_waits:
            if val:
                s.wait_ge(sem.h, val)
        self.schain(s, lambda: s.activation(
            self.veps[:, col % 1:col % 1 + 1], self.ivar4[:, col:col + 1], AF.Ln))
        self.schain(s, lambda: s.activation(
            self.rstd4[:, col:col + 1], self.veps[:, 0:1], AF.Exp, scale=-0.5))
        self.schain(s, lambda: s.activation(
            self.lnb4[:, col:col + 1], self.negm4[:, col:col + 1], AF.Identity,
            scale=self.rstd4[:, col:col + 1]))
        return self.schain(s, lambda: s.activation(
            lnout[:, :], xin[:, :], AF.Identity,
            bias=self.lnb4[:, col:col + 1], scale=self.rstd4[:, col:col + 1]))

    def emit_ln_tile(self, blk, pre, t, xsrc_fn, dstT, dst_stride, first_waits):
        tok = self.tok

        @blk.sync
        def _(sy, t=t):
            wt = tok.get(f"{pre}_lnap_{t - 2}")
            if wt:
                sy.wait_ge(self.sS.h, wt)
            for sem, val in first_waits:
                sy.wait_ge(sem.h, val)
            tok[f"{pre}_xl_{t}"] = self.dma(sy, self.xb[t % 2][:, :],
                                            xsrc_fn(t), self.sX[t % 2])

        @blk.vector
        def _(v, t=t):
            waits = [tok[f"{pre}_xl_{t}"],
                     (self.sS, tok.get(f"{pre}_lnap_{t - 4}", 0))]
            tok[f"{pre}_lnv_{t}"] = self.ln_vector(v, self.xb[t % 2], t % 4,
                                                   extra_waits=waits)

        @blk.scalar
        def _(s, t=t):
            waits = [(self.tM, tok.get(f"{pre}_tp_{t - 4}", 0))]
            tok[f"{pre}_lnap_{t}"] = self.ln_scalar(
                s, self.xb[t % 2], self.lnt[t % 4], t % 4,
                tok[f"{pre}_lnv_{t}"], extra_waits=waits)

        t_off = t % 4 if dst_stride == 512 else t
        tb = [self.psT[0], self.psT[1], self.psA, self.psB]

        @blk.tensor
        def _(tn, t=t):
            tn.wait_ge(self.sS.h, tok[f"{pre}_lnap_{t}"])
            if t == 0:
                self.dmaw(tn, tok["identb"])
            for gb in range(4):
                wt = tok.get(f"{pre}_tpev_prev_{gb}")
                if wt:
                    tn.wait_ge(self.vS.h, wt)
                for k in range(4):
                    h = 4 * gb + k
                    ins = tn.matmul(tb[gb][:, 128 * k:128 * (k + 1)],
                                    self.lnt[t % 4][:, 128 * h:128 * (h + 1)],
                                    self.identb[:, :], start=True, stop=True)
                ins.then_inc(self.tM.h, 1)
                self.tM.n += 1
                tok[f"{pre}_tp_{t}_{gb}"] = self.tM.n

        @blk.vector
        def _(v, t=t, t_off=t_off):
            for gb in range(4):
                v.wait_ge(self.tM.h, tok[f"{pre}_tp_{t}_{gb}"])
                dst = dstT[:, :].rearrange("p (h f) -> p h f", f=dst_stride)[
                    :, 4 * gb:4 * gb + 4, 128 * t_off:128 * (t_off + 1)]
                tok[f"{pre}_tpev_prev_{gb}"] = self.vind(
                    v, lambda dst=dst, gb=gb: v.tensor_copy(
                        dst, tb[gb][:, :].rearrange("p (a b) -> p a b", b=128)))
        tok[f"{pre}_tp_{t}"] = self.tM.n
        tok[f"{pre}_tpev_{t}"] = self.vS.n

    def build(self):
        nc = self.nc
        tok = self.tok
        blk_cm = nc.Block()
        blk = blk_cm.__enter__()

        p1 = ExitStack()
        wkTb = p1.enter_context(nc.sbuf_tensor("wkTb", [128, HT * 512], BF16)).ap()
        wvTb = p1.enter_context(nc.sbuf_tensor("wvTb", [128, HT * 512], BF16)).ap()
        wqTb = p1.enter_context(nc.sbuf_tensor("wqTb", [128, HT * 512], BF16)).ap()
        lnT = [p1.enter_context(nc.sbuf_tensor(f"lnT{i}", [128, HT * 512],
                                               BF16)).ap() for i in range(2)]
        kvst = [p1.enter_context(nc.sbuf_tensor(f"kvst{i}", [128, 512],
                                                BF16)).ap() for i in range(2)]
        self.lnT, self.wkTb, self.wvTb, self.wqTb = lnT, wkTb, wvTb, wqTb
        self.kvst = kvst

        @blk.gpsimd
        def _(g):
            self.dma(g, self.identb[:, :], self.ident, self.sWS)
            self.dma(g, wkTb[:, :].rearrange("p (h f) -> p h f", f=512),
                     self.wkT.rearrange("(h p) f -> p h f", p=128), self.sWS)
            self.dma(g, wvTb[:, :].rearrange("p (h f) -> p h f", f=512),
                     self.wvT.rearrange("(h p) f -> p h f", p=128), self.sWS)
            self.dma(g, wqTb[:, :].rearrange("p (h f) -> p h f", f=512),
                     self.wqT[:, 0:512].rearrange("(h p) f -> p h f", p=128),
                     self.sWS)
            tok["identb"] = (self.sWS, self.sWS.n)
            tok["w_kv"] = (self.sWS, self.sWS.n)
            tok["wq_q0"] = (self.sWS, self.sWS.n)
            self.mat_gpsimd_service(g)

        @blk.sync
        def _(sy):
            tok["consts"] = self.dma(sy, self.lutP[:, :], self.lutp, self.sWC)

        @blk.vector
        def _(v):
            self.dmaw(v, tok["consts"])
            v.memset(self.epsA[:, :], EPS)
            v.memset(self.negone[:, :], -1.0)
            self.vind(v, lambda: v.memset(self.onesb[:, :], 1.0))
            self.vind(v, lambda: v.memset(self.ones1f[:, :], 1.0))
            self.vind(v, lambda: v.memset(self.onefF[:, :], 1.0))
            for j in range(16):
                self.vind(v, lambda j=j: v.memset(self.jneg[:, j:j + 1],
                                                  float(-j)))
            tok["consts_v"] = self.vS.n

        NT_FULL = NSUP * 4

        def x_src(t):
            if t < NT_FULL:
                return self.x_full[128 * t:128 * (t + 1), :]
            tt = t - NT_FULL
            return self.x_own[128 * tt:128 * (tt + 1), :]

        for su in range(NSUP + NSUPQ):
            for i in range(4):
                t = 4 * su + i
                fw = [tok["consts"]] if t == 0 else []
                self.emit_ln_tile(blk, "p1", t, x_src, lnT[su % 2], 512, fw)
            if su >= 1:
                self.emit_proj_supertile(blk, su - 1)
            self.mat_pump_section(blk, MAT_PUMP_P1, nsg=MAT_SG_P1)
        self.emit_proj_supertile(blk, NSUP + NSUPQ - 1)

        self.emit_q_pass(blk, wqTb)
        self.mat_pump_section(blk, MAT_PUMP_P1)

        p1.close()
        p2 = ExitStack()
        Ksl = [p2.enter_context(nc.sbuf_tensor(f"Ksl{i}", [128, SEQ], BF16)).ap()
               for i in range(2)]
        Vsl = [p2.enter_context(nc.sbuf_tensor(f"Vsl{i}", [128, SEQ], BF16)).ap()
               for i in range(2)]
        PT = [p2.enter_context(nc.sbuf_tensor(f"PT{i}", [128, 512], BF16)).ap()
              for i in range(2)]
        cmb = p2.enter_context(nc.sbuf_tensor("cmb", [128, 16 * BLK], BF16)).ap()
        aoT = p2.enter_context(nc.sbuf_tensor("aoT", [128, NH * 256], BF16)).ap()
        woS = [p2.enter_context(nc.sbuf_tensor(f"woS{i}", [128, HT * 128],
                                               BF16)).ap() for i in range(2)]
        qs = [p2.enter_context(nc.sbuf_tensor(f"qs{i}", [128, 256], BF16)).ap()
              for i in range(2)]
        self.avf = p2.enter_context(nc.sbuf_tensor("avf", [128, 256], F32)).ap()
        self.boS = p2.enter_context(nc.sbuf_tensor("boS", [128, DT], F32)).ap()
        self.opbf = p2.enter_context(nc.sbuf_tensor("opbf", [128, 256],
                                                    BF16)).ap()
        self.Ksl, self.Vsl, self.PT, self.cmb, self.aoT, self.woS = \
            Ksl, Vsl, PT, cmb, aoT, woS
        self.qs = qs
        self.hseq = 0
        self.slast = {}

        j_t, j_s, j_v = self.tM.n, self.sS.n, self.vS.n
        self.p2_join = (j_t, j_s, j_v)
        self.p2_jdO = self.store_snapshot()
        prev_ao = None

        @blk.sync
        def _(sy):
            sy.wait_ge(self.sS.h, j_s)
            for sv in self.p2_jdO:
                self.dmaw(sy, sv)
            tok["bo_ld"] = self.dma(sy, self.boS[:, :], self.boP, self.sWC)

        for slot in range(NSLOT):
            nkb = 16 * (slot + 1)

            @blk.gpsimd
            def _(g, slot=slot):
                if slot > 0:
                    g.wait_ge(self.tM.h, tok[f"att_done_{4 * slot - 1}"])
                else:
                    g.wait_ge(self.tM.h, j_t)
                    g.wait_ge(self.sS.h, j_s)
                tok[f"cmask_{slot}"] = self.dma(g, cmb[:, :],
                                                self.cmask[slot, :, :], self.sWS)
                self.mat_gpsimd_service(g)

            for kv in range(KVH):
                gidx = slot * KVH + kv
                par = gidx % 2

                @blk.sync
                def _(sy, kv=kv, gidx=gidx, par=par, nkb=nkb):
                    wt = tok.get(f"att_done_{gidx - 2}")
                    if wt:
                        sy.wait_ge(self.tM.h, wt)
                    else:
                        sy.wait_ge(self.tM.h, j_t)
                        sy.wait_ge(self.sS.h, j_s)
                        for sv in self.p2_jdO:
                            self.dmaw(sy, sv)
                    self.dma(sy, Ksl[par][:, 0:128 * nkb],
                             self.d_K[kv, :, 0:128 * nkb], self.sKV[par])
                    tok[f"kvload_{gidx}"] = self.dma(
                        sy, Vsl[par][:, 0:128 * nkb],
                        self.d_V[kv, :, 0:128 * nkb], self.sKV[par])

                @blk.vector
                def _(v, gidx=gidx, par=par, nkb=nkb):
                    self.dmaw(v, tok[f"kvload_{gidx}"])
                    self.vchain(v, lambda: v.reduce_sum(
                        self.ksum[:, :], Ksl[par][:, 0:128 * nkb], axis=AX.X))
                    self.vchain(v, lambda: v.tensor_scalar(
                        self.negmean[:, :], self.ksum[:, :],
                        -1.0 / (128 * nkb), None, ALU.mult))
                    tok[f"kshift_{gidx}"] = self.vchain(
                        v, lambda: v.tensor_scalar(
                            Ksl[par][:, 0:128 * nkb], Ksl[par][:, 0:128 * nkb],
                            self.negmean[:, :], None, ALU.add))

                for hh in range(4):
                    h = 4 * kv + hh
                    prev_ao = self.emit_attention_head(
                        blk, slot, kv, h, gidx, par, nkb, prev_ao)
                    @blk.gpsimd
                    def _(g):
                        self.mat_gpsimd_service(g)

                    if MAT_PUMP_P2[slot]:
                        self.mat_pump_drv(blk, MAT_PUMP_P2[slot])
                    if MAT_SG_P2:
                        self.mat_pump_sg(blk, MAT_SG_P2)
                tok[f"att_done_{gidx}"] = self.tM.n

            self.emit_attn_outproj(blk, slot)

        p2.close()
        self.mat_stage2(blk)
        self.emit_ffn(blk, self.d_X[0], self.d_X[1], 1)

        self.emit_attn_b(blk)

        self.emit_ffn(blk, self.d_X[2], None, 2)

        if DEBUG_TAPS:
            @blk.gpsimd
            def _(g):
                for sv in self.store_snapshot():
                    self.dmaw(g, sv)
                for (dst, srcap) in [
                        (self.t_x1, self.d_X[0]), (self.t_x2, self.d_X[1]),
                        (self.t_x3, self.d_X[2]),
                        (self.t_wg, self.d_Wg[0, :, :]),
                        (self.t_wd, self.d_Wd[0, :, :]),
                        (self.t_wu, self.d_Wu[0, :, :]),
                        (self.t_k, self.d_K[0, :, 0:1024]),
                        (self.t_v, self.d_V[0, :, 0:1024]),
                        (self.t_q, self.d_Q[0, :, 0:1024])]:
                    self.dma(g, dst, srcap, self.sFX)

        @blk.sync
        def _(sy):
            for sv in self.store_snapshot():
                self.dmaw(sy, sv)

        blk_cm.__exit__(None, None, None)
        self.es.close()
        return nc

    def emit_proj_supertile(self, blk, su):
        tok = self.tok
        own = su >= NSUP
        if own:
            return
        src = self.lnT[su % 2]
        groups = [("k", f) for f in range(4)] + [("v", i) for i in range(4)]

        for gi in range(0, len(groups), 2):
            pair = groups[gi:gi + 2]

            @blk.tensor
            def _(tn, pair=pair, su=su, gi=gi, src=src, own=own):
                tn.wait_ge(self.vS.h, tok[f"p1_tpev_{4 * su + 3}"])
                if gi == 0 and su == 0:
                    self.dmaw(tn, tok["w_kv"])
                for pi, (kind, f) in enumerate(pair):
                    psd = (self.psC, self.psD)[pi]
                    wt = tok.get(f"pj_ev_{su}_{gi - 2 + pi}")
                    if wt is None and su > 0:
                        wt = tok.get(f"pj_ev_{su - 1}_{len(groups) - 2 + pi}")
                    if wt:
                        tn.wait_ge(self.sS.h, wt)
                    for ht in range(HT):
                        if kind == "k":
                            lhs = self.wkTb[:, 512 * ht + 128 * f:
                                            512 * ht + 128 * (f + 1)]
                            rhs = src[:, 512 * ht:512 * (ht + 1)]
                        else:
                            lhs = src[:, 512 * ht + 128 * f:512 * ht + 128 * (f + 1)]
                            rhs = self.wvTb[:, 512 * ht:512 * (ht + 1)]
                        ins = tn.matmul(psd[:, :], lhs, rhs, start=(ht == 0),
                                        stop=(ht == HT - 1))
                    ins.then_inc(self.tM.h, 1)
                    self.tM.n += 1
                    tok[f"pj_mm_{su}_{gi + pi}"] = self.tM.n

            @blk.scalar
            def _(s, pair=pair, su=su, gi=gi):
                for pi, (kind, f) in enumerate(pair):
                    psd = (self.psC, self.psD)[pi]
                    s.wait_ge(self.tM.h, tok[f"pj_mm_{su}_{gi + pi}"])
                    st = self.kvst[(gi + pi) % 2]
                    wt = tok.get(f"pj_st_{su}_{gi - 2 + pi}")
                    if wt is None and su > 0:
                        wt = tok.get(f"pj_st_{su - 1}_{gi + pi + 6}")
                    if wt:
                        self.dmaw(s, wt)
                    tok[f"pj_ev_{su}_{gi + pi}"] = self.sind(
                        s, lambda st=st, psd=psd: s.copy(st[:, :], psd[:, :]))

            if True:
                @blk.sync
                def _(sy, pair=pair, su=su, gi=gi):
                    for pi, (kind, f) in enumerate(pair):
                        sy.wait_ge(self.sS.h, tok[f"pj_ev_{su}_{gi + pi}"])
                        st = self.kvst[(gi + pi) % 2]
                        sem = self.sST[(gi + pi) % 2]
                        if kind == "k":
                            tok[f"pj_st_{su}_{gi + pi}"] = self.dma(
                                sy, self.d_K[f, :, 512 * su:512 * (su + 1)],
                                st[:, :], sem)
                        else:
                            t = 4 * su + f
                            tok[f"pj_st_{su}_{gi + pi}"] = self.dma(
                                sy,
                                self.d_V[:, :, 128 * t:128 * (t + 1)].rearrange(
                                    "kv p f -> p kv f"),
                                st[:, :].rearrange("p (kv f) -> p kv f", f=128),
                                sem)



    def emit_q_pass(self, blk, wqTb):
        tok = self.tok
        for q in range(4):
            if q > 0:
                @blk.gpsimd
                def _(g, q=q):
                    g.wait_ge(self.tM.h, tok[f"q_mm_{q - 1}_last"])
                    tok[f"wq_q{q}"] = self.dma(
                        g, wqTb[:, :].rearrange("p (h f) -> p h f", f=512),
                        self.wqT[:, 512 * q:512 * (q + 1)].rearrange(
                            "(h p) f -> p h f", p=128), self.sWS)
                    self.mat_gpsimd_service(g)
            for su2 in range(2):
                su = NSUP + su2
                for jr in range(2):
                    @blk.tensor
                    def _(tn, q=q, su2=su2, su=su, jr=jr):
                        if su2 == 0 and jr == 0:
                            self.dmaw(tn, tok[f"wq_q{q}"])
                            if q == 0:
                                tn.wait_ge(self.vS.h,
                                           tok[f"p1_tpev_{4 * (NSUP + 1) + 3}"])
                                tn.wait_ge(self.sS.h,
                                           tok[f"pj_ev_{NSUP - 1}_7"])
                        src = self.lnT[su % 2]
                        for pi in range(2):
                            j = 2 * jr + pi
                            psd = (self.psC, self.psD)[pi]
                            wt = tok.get(f"qp_ev_{q}_{su2}_{jr - 1}_{pi}")
                            if wt is None:
                                wt = tok.get(f"qp_ev_{q}_{su2 - 1}_{1}_{pi}")
                            if wt is None:
                                wt = tok.get(f"qp_ev_{q - 1}_1_1_{pi}")
                            if wt:
                                tn.wait_ge(self.sS.h, wt)
                            for ht in range(HT):
                                lhs = wqTb[:, 512 * ht + 128 * j:
                                           512 * ht + 128 * (j + 1)]
                                rhs = src[:, 512 * ht:512 * (ht + 1)]
                                ins = tn.matmul(psd[:, :], lhs, rhs,
                                                start=(ht == 0),
                                                stop=(ht == HT - 1))
                            ins.then_inc(self.tM.h, 1)
                            self.tM.n += 1
                            tok[f"qp_mm_{q}_{su2}_{jr}_{pi}"] = self.tM.n
                        tok[f"q_mm_{q}_last"] = self.tM.n

                    @blk.scalar
                    def _(s, q=q, su2=su2, jr=jr):
                        for pi in range(2):
                            j = 2 * jr + pi
                            psd = (self.psC, self.psD)[pi]
                            s.wait_ge(self.tM.h, tok[f"qp_mm_{q}_{su2}_{jr}_{pi}"])
                            st = self.kvst[pi]
                            wt = tok.get(f"qp_st_{q}_{su2}_{jr - 1}_{pi}")
                            if wt is None:
                                wt = tok.get(f"qp_st_{q}_{su2 - 1}_1_{pi}")
                            if wt is None:
                                wt = tok.get(f"qp_st_{q - 1}_1_1_{pi}")
                            if wt is None:
                                wt = tok.get(f"pj_st_{NSUP - 1}_{6 + pi}")
                            if wt:
                                self.dmaw(s, wt)
                            tok[f"qp_ev_{q}_{su2}_{jr}_{pi}"] = self.sind(
                                s, lambda st=st, psd=psd: s.copy(st[:, :], psd[:, :]))

                    @blk.sync
                    def _(sy, q=q, su2=su2, jr=jr):
                        for pi in range(2):
                            j = 2 * jr + pi
                            h = 4 * q + j
                            sy.wait_ge(self.sS.h, tok[f"qp_ev_{q}_{su2}_{jr}_{pi}"])
                            tok[f"qp_st_{q}_{su2}_{jr}_{pi}"] = self.dma(
                                sy, self.d_Q[h, :, 512 * su2:512 * (su2 + 1)],
                                self.kvst[pi][:, :], self.sST[pi])

    def emit_attention_head(self, blk, slot, kv, h, gidx, par, nkb, prev_ao):
        tok = self.tok
        npair = nkb // 2
        tail0 = npair - 8
        Ksl, Vsl, PT, cmb, aoT = (self.Ksl, self.Vsl, self.PT, self.cmb,
                                  self.aoT)
        hseq = self.hseq
        self.hseq += 1
        qb = self.qs[hseq % 2]

        @blk.sync
        def _(sy):
            wt = self.slast.get(hseq - 2)
            if wt:
                sy.wait_ge(self.tM.h, wt)
            elif hseq < 2:
                for sv in self.p2_jdO:
                    self.dmaw(sy, sv)
            tok[f"qld_{hseq}"] = self.dma(
                sy, qb[:, :], self.d_Q[h, :, 256 * slot:256 * (slot + 1)],
                self.sQ2[hseq % 2])

        base = self.tM.n

        def s_tok(p):
            return base + (1 if p == 0 else 2 * p)

        def a_tok(p):
            return base + (2 * npair if p == npair - 1 else 3 + 2 * p)

        bc_tok = base + 2 * npair + 1

        exp_toks = []

        @blk.scalar
        def _(s):
            for p in range(npair):
                s.wait_ge(self.tM.h, s_tok(p))
                ps = (self.psA, self.psB)[p % 2]
                ins = s.activation(PT[p % 2][:, :], ps[:, :], AF.Exp)
                ins.then_inc(self.sS.h, 1)
                self.sS.n += 1
                exp_toks.append(self.sS.n)

        msk_toks = {}
        r_tok = [None]
        av_tok = [None]
        ao_tok = [None]

        @blk.gpsimd
        def _(g):
            for p in range(tail0, npair):
                g.wait_ge(self.sS.h, exp_toks[p])
                m = p - tail0
                for half in range(2):
                    g.wait_ge(self.gS.h, self.gS.n)
                    g.tensor_tensor(
                        PT[p % 2][:, 256 * half:256 * (half + 1)],
                        PT[p % 2][:, 256 * half:256 * (half + 1)],
                        cmb[:, 256 * (2 * m + half):256 * (2 * m + half + 1)],
                        ALU.mult).then_inc(self.gS.h, 1)
                    self.gS.n += 1
                msk_toks[p] = self.gS.n

        @blk.vector
        def _(v):
            v.wait_ge(self.tM.h, a_tok(npair - 1))
            r_tok[0] = self.vchain(v, lambda: v.reciprocal(self.rL[:, :],
                                                           self.psL))
            av_tok[0] = self.vchain(v, lambda: v.tensor_copy(
                self.avf[:, :], self.psC[:, 0:256]))
            v.wait_ge(self.tM.h, bc_tok)
            ao_tok[0] = self.vchain(v, lambda: v.tensor_tensor(
                aoT[:, 256 * h:256 * (h + 1)], self.avf[:, :],
                self.psD[:, 0:256], ALU.mult))
            tok[f"ao_{slot}_{h}"] = ao_tok[0]

        @blk.tensor
        def _(tn):
            if prev_ao is None:
                tn.wait_ge(self.sS.h, self.p2_join[1])
                tn.wait_ge(self.vS.h, self.p2_join[2])

            def s_pair(p):
                ps = (self.psA, self.psB)[p % 2]
                if p == 0:
                    tn.wait_ge(self.vS.h, tok[f"kshift_{gidx}"])
                    self.dmaw(tn, tok[f"kvload_{gidx}"])
                    self.dmaw(tn, tok[f"qld_{hseq}"])
                    if h == 4 * kv:
                        self.dmaw(tn, tok[f"cmask_{slot}"])
                if p >= 2:
                    tn.wait_ge(self.sS.h, exp_toks[p - 2])
                for half in range(2):
                    kb = 2 * p + half
                    ins = tn.matmul(ps[:, 256 * half:256 * (half + 1)],
                                    Ksl[par][:, 128 * kb:128 * (kb + 1)],
                                    qb[:, :], start=True, stop=True)
                ins.then_inc(self.tM.h, 1)
                self.tM.n += 1
                assert self.tM.n == s_tok(p), (self.tM.n, s_tok(p))

            def lav_pair(p):
                if p == 0 and prev_ao is not None:
                    tn.wait_ge(self.vS.h, prev_ao)
                if p == 0 and h == 0 and slot > 0:
                    tn.wait_ge(self.sS.h, tok[f"op_ev_{slot - 1}_{DT - 1}"])
                if p >= tail0:
                    tn.wait_ge(self.gS.h, msk_toks[p])
                else:
                    tn.wait_ge(self.sS.h, exp_toks[p])
                for half in range(2):
                    kb = 2 * p + half
                    tn.matmul(self.psL, self.onesb[:, :],
                              PT[p % 2][:, 256 * half:256 * (half + 1)],
                              start=(kb == 0), stop=(kb == nkb - 1))
                    ins = tn.matmul(self.psC[:, 0:256],
                                    Vsl[par][:, 128 * kb:128 * (kb + 1)],
                                    PT[p % 2][:, 256 * half:256 * (half + 1)],
                                    start=(kb == 0), stop=(kb == nkb - 1))
                ins.then_inc(self.tM.h, 1)
                self.tM.n += 1
                assert self.tM.n == a_tok(p), (self.tM.n, a_tok(p))

            s_pair(0)
            if npair > 1:
                s_pair(1)
            for p in range(2, npair):
                lav_pair(p - 2)
                s_pair(p)
            if npair > 1:
                lav_pair(npair - 2)
            lav_pair(npair - 1)
            tn.wait_ge(self.vS.h, r_tok[0])
            ins = tn.matmul(self.psD[:, 0:256], self.ones1f[:, :],
                            self.rL[:, :], start=True, stop=True)
            ins.then_inc(self.tM.h, 1)
            self.tM.n += 1
            assert self.tM.n == bc_tok

        self.slast[hseq] = base + (1 if npair == 1 else 2 * (npair - 1))
        return ao_tok[0]

    def emit_attn_outproj(self, blk, slot):
        tok = self.tok

        @blk.sync
        def _(sy, slot=slot):
            for c in range(2):
                wt = tok.get(f"x1st_{slot - 1}_{c}")
                if wt:
                    self.dmaw(sy, wt)
                if slot == 0:
                    sy.wait_ge(self.sS.h, self.p2_join[1])
                tok[f"xown_{slot}_{c}"] = self.dma(
                    sy, self.xb[c][:, :],
                    self.x_own[256 * slot + 128 * c:256 * slot + 128 * (c + 1),
                               :], self.sX[c])

        for d in range(DT):
            @blk.gpsimd
            def _(g, d=d, slot=slot):
                wt = tok.get(f"op_mm_{slot}_{d - 1}")
                if wt is None and slot > 0:
                    wt = tok.get(f"op_mm_{slot - 1}_{DT - 1}")
                if wt:
                    g.wait_ge(self.tM.h, wt)
                elif slot == 0 and d < 2:
                    for sv in self.p2_jdO:
                        self.dmaw(g, sv)
                tok[f"op_w_{slot}_{d}"] = self.dma(
                    g, self.woS[d % 2][:, :], self.woTr[d, :, :],
                    self.sWO[d % 2])

            @blk.tensor
            def _(tn, d=d, slot=slot):
                self.dmaw(tn, tok[f"op_w_{slot}_{d}"])
                if d == 0:
                    tn.wait_ge(self.vS.h, tok[f"ao_{slot}_{15}"])
                wt = tok.get(f"op_ev_{slot}_{d - 1}")
                if wt:
                    tn.wait_ge(self.sS.h, wt)
                for h in range(NH):
                    ins = tn.matmul(self.psF[:, 0:256],
                                    self.woS[d % 2][:, 128 * h:128 * (h + 1)],
                                    self.aoT[:, 256 * h:256 * (h + 1)],
                                    start=(h == 0), stop=(h == NH - 1))
                ins.then_inc(self.tM.h, 1)
                self.tM.n += 1
                tok[f"op_mm_{slot}_{d}"] = self.tM.n

            @blk.scalar
            def _(s, d=d, slot=slot):
                s.wait_ge(self.tM.h, tok[f"op_mm_{slot}_{d}"])
                if slot == 0 and d == 0:
                    self.dmaw(s, tok["bo_ld"])
                wt = tok.get(f"op_tp_{slot}_{d - 1}")
                if wt:
                    s.wait_ge(self.tM.h, wt)
                tok[f"op_ev_{slot}_{d}"] = self.sind(
                    s, lambda d=d: s.activation(
                        self.opbf[:, :], self.psF[:, 0:256], AF.Identity,
                        bias=self.boS[:, d:d + 1]))

            @blk.tensor
            def _(tn, d=d, slot=slot):
                tn.wait_ge(self.sS.h, tok[f"op_ev_{slot}_{d}"])
                wt = tok.get(f"op_add_{slot}_{d - 2}")
                if wt:
                    tn.wait_ge(self.vS.h, wt)
                for c in range(2):
                    ins = tn.matmul(self.psT[d % 2][:, 128 * c:128 * (c + 1)],
                                    self.opbf[:, 128 * c:128 * (c + 1)],
                                    self.identb[:, :], start=True, stop=True)
                ins.then_inc(self.tM.h, 1)
                self.tM.n += 1
                tok[f"op_tp_{slot}_{d}"] = self.tM.n

            @blk.vector
            def _(v, d=d, slot=slot):
                v.wait_ge(self.tM.h, tok[f"op_tp_{slot}_{d}"])
                if d == 0:
                    self.dmaw(v, tok[f"xown_{slot}_0"])
                    self.dmaw(v, tok[f"xown_{slot}_1"])
                for c in range(2):
                    self.vind(v, lambda d=d, c=c: v.tensor_tensor(
                        self.xb[c][:, 128 * d:128 * (d + 1)],
                        self.xb[c][:, 128 * d:128 * (d + 1)],
                        self.psT[d % 2][:, 128 * c:128 * (c + 1)], ALU.add))
                tok[f"op_add_{slot}_{d}"] = self.vS.n

        @blk.sync
        def _(sy, slot=slot):
            sy.wait_ge(self.vS.h, tok[f"op_add_{slot}_{DT - 1}"])
            for c in range(2):
                tok[f"x1st_{slot}_{c}"] = self.dma(
                    sy, self.d_X[0][256 * slot + 128 * c:
                                    256 * slot + 128 * (c + 1), :],
                    self.xb[c][:, :], self.sXR[c])
        self.mat_pump_section(blk, 6, nsg=4)

    def emit_ffn(self, blk, xin_d, xout_d, phase):
        tok = self.tok
        nc = self.nc
        pre = f"f{phase}"
        pfh = ExitStack()
        hT = pfh.enter_context(nc.sbuf_tensor(f"hT_{phase}",
                                              [128, NIT * 1024], BF16)).ap()
        pf1 = ExitStack()
        ln2T = pf1.enter_context(nc.sbuf_tensor(f"ln2T_{phase}",
                                                [128, HT * 1024], BF16)).ap()
        wgS = [pf1.enter_context(nc.sbuf_tensor(f"wgS{i}_{phase}",
                                                [128, HT * 128], BF16)).ap()
               for i in range(2)]
        wuS = [pf1.enter_context(nc.sbuf_tensor(f"wuS{i}_{phase}",
                                                [128, HT * 128], BF16)).ap()
               for i in range(2)]
        gsb = [pf1.enter_context(nc.sbuf_tensor(f"gsb{i}_{phase}",
                                                [128, 512], BF16)).ap()
               for i in range(2)]
        jt, js, jv = self.tM.n, self.sS.n, self.vS.n

        def xsrc(t):
            return xin_d[128 * t:128 * (t + 1), :]

        fw = [(self.tM, jt), (self.sS, js), (self.vS, jv)]
        if phase == 1:
            fw.append(tok["x1st_3_0"])
            fw.append(tok["x1st_3_1"])
            fw.extend(self.mat["stage2_guard"])
        else:
            fw.append((self.sFX, self.sFX.n))
        for t in range(8):
            self.emit_ln_tile(blk, pre, t, xsrc, ln2T, 1024,
                              fw if t < 2 else [])
            if phase == 1:
                self.mat_pump_section(blk, MAT_PUMP_FFN, nsg=6)
        ln_done = self.sS.n

        for it in range(NIT):
            pair = it % 2

            @blk.sync
            def _(sy, it=it, pair=pair):
                wt = tok.get(f"{pre}_gu_{it - 2}_1")
                if wt:
                    sy.wait_ge(self.tM.h, wt)
                if it == 0 and phase == 1:
                    for sv in self.mat_store_tok(2 * HT - 1):
                        self.dmaw(sy, sv)
                self.dma(sy, wgS[pair][:, :], self.d_Wg[it, :, :],
                         self.sWU[pair])
                tok[f"{pre}_wgu_{it}"] = self.dma(
                    sy, wuS[pair][:, :], self.d_Wu[it, :, :], self.sWU[pair])

            @blk.tensor
            def _(tn, it=it, pair=pair):
                self.dmaw(tn, tok[f"{pre}_wgu_{it}"])
                if it == 0:
                    tn.wait_ge(self.sS.h, ln_done)
                for q2 in range(2):
                    psg = (self.psC, self.psE)[q2]
                    psq = (self.psD, self.psF)[q2]
                    wt = tok.get(f"{pre}_sil_{it - 1}_{q2}")
                    if wt:
                        tn.wait_ge(self.sS.h, wt)
                    wt2 = tok.get(f"{pre}_hml_{it - 1}_{q2}")
                    if wt2:
                        tn.wait_ge(self.vS.h, wt2)
                    for ht in range(HT):
                        tn.matmul(psg[:, :],
                                  wgS[pair][:, 128 * ht:128 * (ht + 1)],
                                  ln2T[:, 1024 * ht + 512 * q2:
                                       1024 * ht + 512 * (q2 + 1)],
                                  start=(ht == 0), stop=(ht == HT - 1))
                    for ht in range(HT):
                        ins = tn.matmul(psq[:, :],
                                        wuS[pair][:, 128 * ht:128 * (ht + 1)],
                                        ln2T[:, 1024 * ht + 512 * q2:
                                             1024 * ht + 512 * (q2 + 1)],
                                        start=(ht == 0), stop=(ht == HT - 1))
                    ins.then_inc(self.tM.h, 1)
                    self.tM.n += 1
                    tok[f"{pre}_gu_{it}_{q2}"] = self.tM.n

            @blk.scalar
            def _(s, it=it):
                for q2 in range(2):
                    psg = (self.psC, self.psE)[q2]
                    s.wait_ge(self.tM.h, tok[f"{pre}_gu_{it}_{q2}"])
                    wt = tok.get(f"{pre}_hml_{it - 1}_{q2}")
                    if wt:
                        s.wait_ge(self.vS.h, wt)
                    tok[f"{pre}_sil_{it}_{q2}"] = self.sind(
                        s, lambda q2=q2, psg=psg: s.activation(
                            gsb[q2][:, :], psg[:, :], AF.Silu))

            @blk.vector
            def _(v, it=it):
                for q2 in range(2):
                    psq = (self.psD, self.psF)[q2]
                    v.wait_ge(self.sS.h, tok[f"{pre}_sil_{it}_{q2}"])
                    self.vind(v, lambda it=it, q2=q2, psq=psq: v.tensor_tensor(
                        hT[:, 1024 * it + 512 * q2:1024 * it + 512 * (q2 + 1)],
                        gsb[q2][:, :], psq[:, :], ALU.mult))
                    tok[f"{pre}_hml_{it}_{q2}"] = self.vS.n

            if phase == 1:
                self.mat_pump_section(blk, MAT_PUMP_FFN, nsg=4)

        if phase == 1:
            self.mat_drain(blk)
            self.mat_stack.close()
        gu_done_t = self.tM.n
        hml_done_v = self.vS.n
        pf1.close()

        pf2 = ExitStack()
        xst = [pf2.enter_context(nc.sbuf_tensor(f"xst{i}_{phase}",
                                                [128, D], F32, side="right")).ap()
               for i in range(8)]
        wdS = [pf2.enter_context(nc.sbuf_tensor(f"wdS{i}_{phase}",
                                                [128, NIT * 128], BF16)).ap()
               for i in range(2)]

        @blk.sync
        def _(sy):
            sy.wait_ge(self.tM.h, gu_done_t)
            sy.wait_ge(self.vS.h, hml_done_v)
            if phase == 1:
                for sv in self.mat_store_tok(len(self.mat_jobs) - 1):
                    self.dmaw(sy, sv)
            for t in range(8):
                self.dma(sy, xst[t][:, :], xin_d[128 * t:128 * (t + 1), :],
                         self.sXT)
            tok[f"{pre}_xst_ld"] = (self.sXT, self.sXT.n)

        for d in range(DT):
            pair = d % 2

            @blk.sync
            def _(sy, d=d, pair=pair):
                wt = tok.get(f"{pre}_dn_{d - 2}_1")
                if wt:
                    sy.wait_ge(self.tM.h, wt)
                tok[f"{pre}_wd_{d}"] = self.dma(
                    sy, wdS[pair][:, :], self.d_Wd[d, :, :], self.sWD[pair])

            @blk.tensor
            def _(tn, d=d, pair=pair):
                self.dmaw(tn, tok[f"{pre}_wd_{d}"])
                if d == 0:
                    tn.wait_ge(self.vS.h, tok[f"{pre}_tpev_7"])
                for q2 in range(2):
                    psd = (self.psA, self.psB)[q2]
                    wt = tok.get(f"{pre}_dnev_{d - 1}_{q2}")
                    if wt:
                        tn.wait_ge(self.sS.h, wt)
                    for it in range(NIT):
                        ins = tn.matmul(
                            psd[:, :], wdS[pair][:, 128 * it:128 * (it + 1)],
                            hT[:, 1024 * it + 512 * q2:1024 * it + 512 * (q2 + 1)],
                            start=(it == 0), stop=(it == NIT - 1))
                    ins.then_inc(self.tM.h, 1)
                    self.tM.n += 1
                    tok[f"{pre}_dn_{d}_{q2}"] = self.tM.n

            @blk.scalar
            def _(s, d=d):
                for q2 in range(2):
                    psd = (self.psA, self.psB)[q2]
                    s.wait_ge(self.tM.h, tok[f"{pre}_dn_{d}_{q2}"])
                    wt = tok.get(f"{pre}_dtp_{d - 1}_{q2}")
                    if wt:
                        s.wait_ge(self.tM.h, wt)
                    tok[f"{pre}_dnev_{d}_{q2}"] = self.sind(
                        s, lambda q2=q2, psd=psd: s.copy(
                            self.dnbf[q2][:, :], psd[:, :]))

            @blk.tensor
            def _(tn, d=d):
                for q2 in range(2):
                    tn.wait_ge(self.sS.h, tok[f"{pre}_dnev_{d}_{q2}"])
                    wt = tok.get(f"{pre}_dadd_{d - 1}_{q2}")
                    if wt:
                        tn.wait_ge(self.vS.h, wt)
                    for c in range(4):
                        ins = tn.matmul(
                            self.psT[q2][:, 128 * c:128 * (c + 1)],
                            self.dnbf[q2][:, 128 * c:128 * (c + 1)],
                            self.identb[:, :], start=True, stop=True)
                    ins.then_inc(self.tM.h, 1)
                    self.tM.n += 1
                    tok[f"{pre}_dtp_{d}_{q2}"] = self.tM.n

            @blk.vector
            def _(v, d=d):
                if d == 0:
                    self.dmaw(v, tok[f"{pre}_xst_ld"])
                for q2 in range(2):
                    v.wait_ge(self.tM.h, tok[f"{pre}_dtp_{d}_{q2}"])
                    for c in range(4):
                        t = 4 * q2 + c
                        self.vind(v, lambda t=t, d=d, q2=q2, c=c: v.tensor_tensor(
                            xst[t][:, 128 * d:128 * (d + 1)],
                            xst[t][:, 128 * d:128 * (d + 1)],
                            self.psT[q2][:, 128 * c:128 * (c + 1)], ALU.add))
                    tok[f"{pre}_dadd_{d}_{q2}"] = self.vS.n

        @blk.sync
        def _(sy):
            dst = xout_d if xout_d is not None else self.y_own
            for t in range(8):
                sy.wait_ge(self.vS.h, tok[f"{pre}_dadd_{DT - 1}_{t // 4}"])
                tok[f"{pre}_xst_{t}"] = self.dma(
                    sy, dst[128 * t:128 * (t + 1), :], xst[t][:, :], self.sFX)

        pf2.close()
        pfh.close()

    def emit_attn_b(self, blk):
        tok = self.tok
        nc = self.nc
        pre = "b"
        pb = ExitStack()
        ln3T = pb.enter_context(nc.sbuf_tensor("ln3T", [128, HT * 1024],
                                               BF16)).ap()
        wvbS = pb.enter_context(nc.sbuf_tensor("wvbS", [128, HT * 512],
                                               BF16)).ap()
        wpA = pb.enter_context(nc.sbuf_tensor("wpA", [128, KVH * D], BF16)).ap()
        wpB = pb.enter_context(nc.sbuf_tensor("wpB", [128, KVH * D], BF16)).ap()
        vbT = pb.enter_context(nc.sbuf_tensor("vbT", [128, KVH * 1024],
                                              BF16)).ap()
        xst = [pb.enter_context(nc.sbuf_tensor(f"bxst{i}", [128, D], F32,
                                               side="right")).ap()
               for i in range(8)]

        jt, js, jv = self.tM.n, self.sS.n, self.vS.n

        @blk.gpsimd
        def _(g):
            g.wait_ge(self.tM.h, jt)
            g.wait_ge(self.sS.h, js)
            g.wait_ge(self.vS.h, jv)
            self.dma(g, wvbS[:, :].rearrange("p (h f) -> p h f", f=512),
                     self.wvbT.rearrange("(h p) f -> p h f", p=128), self.sWS)
            tok["wvb_ld"] = (self.sWS, self.sWS.n)
            wp4 = self.wpT.rearrange("(kv j p) d -> j p kv d", kv=4, p=128)
            tok["wp_0"] = self.dma(
                g, wpA[:, :].rearrange("p (kv d) -> p kv d", d=D),
                wp4[0, :, :, :], self.sWS)

        for j in range(1, 4):
            @blk.gpsimd
            def _(g, j=j):
                if j > 1:
                    g.wait_ge(self.vS.h, tok[f"wp_add_{j - 1}"])
                tok[f"wp_l{j}"] = self.dma(
                    g, wpB[:, :].rearrange("p (kv d) -> p kv d", d=D),
                    self.wpT.rearrange("(kv j p) d -> j p kv d",
                                       kv=4, p=128)[j, :, :, :], self.sWS)

            @blk.vector
            def _(v, j=j):
                self.dmaw(v, tok[f"wp_l{j}"])
                tok[f"wp_add_{j}"] = self.vchain(v, lambda: v.tensor_tensor(
                    wpA[:, :], wpA[:, :], wpB[:, :], ALU.add))

        def xsrc(t):
            return self.d_X[1][128 * t:128 * (t + 1), :]

        fw = [(self.tM, jt), (self.sS, js), (self.vS, jv),
              (self.sFX, self.sFX.n)]
        for t in range(8):
            self.emit_ln_tile(blk, pre, t, xsrc, ln3T, 1024,
                              fw if t < 2 else [])
        ln_done = self.sS.n

        for f in range(4):
            @blk.tensor
            def _(tn, f=f):
                if f == 0:
                    tn.wait_ge(self.sS.h, ln_done)
                    self.dmaw(tn, (self.sWS, self.sWS.n))
                for q2 in range(2):
                    psd = (self.psC, self.psD)[q2]
                    wt = tok.get(f"vb_ev_{f - 1}_{q2}")
                    if wt:
                        tn.wait_ge(self.sS.h, wt)
                    for ht in range(HT):
                        ins = tn.matmul(psd[:, :],
                                        wvbS[:, 512 * ht + 128 * f:
                                             512 * ht + 128 * (f + 1)],
                                        ln3T[:, 1024 * ht + 512 * q2:
                                             1024 * ht + 512 * (q2 + 1)],
                                        start=(ht == 0), stop=(ht == HT - 1))
                    ins.then_inc(self.tM.h, 1)
                    self.tM.n += 1
                    tok[f"vb_mm_{f}_{q2}"] = self.tM.n

            @blk.scalar
            def _(s, f=f):
                for q2 in range(2):
                    psd = (self.psC, self.psD)[q2]
                    s.wait_ge(self.tM.h, tok[f"vb_mm_{f}_{q2}"])
                    dst = vbT[:, 1024 * f + 512 * q2:1024 * f + 512 * (q2 + 1)]
                    tok[f"vb_ev_{f}_{q2}"] = self.sind(
                        s, lambda dst=dst, psd=psd: s.copy(dst, psd[:, :]))

        @blk.sync
        def _(sy):
            sy.wait_ge(self.sS.h, tok["vb_ev_3_1"])
            for t in range(8):
                self.dma(sy, xst[t][:, :],
                         self.d_X[1][128 * t:128 * (t + 1), :], self.sXT)
            tok["b_xst_ld"] = (self.sXT, self.sXT.n)

        for d in range(DT):
            @blk.tensor
            def _(tn, d=d):
                if d == 0:
                    tn.wait_ge(self.vS.h, tok["wp_add_3"])
                    tn.wait_ge(self.sS.h, tok["vb_ev_3_1"])
                for q2 in range(2):
                    psd = (self.psA, self.psB)[q2]
                    wt = tok.get(f"b_dnev_{d - 1}_{q2}")
                    if wt:
                        tn.wait_ge(self.sS.h, wt)
                    for f in range(4):
                        ins = tn.matmul(psd[:, :],
                                        wpA[:, D * f + 128 * d:
                                            D * f + 128 * (d + 1)],
                                        vbT[:, 1024 * f + 512 * q2:
                                            1024 * f + 512 * (q2 + 1)],
                                        start=(f == 0), stop=(f == 3))
                    ins.then_inc(self.tM.h, 1)
                    self.tM.n += 1
                    tok[f"b_dn_{d}_{q2}"] = self.tM.n

            @blk.scalar
            def _(s, d=d):
                for q2 in range(2):
                    psd = (self.psA, self.psB)[q2]
                    s.wait_ge(self.tM.h, tok[f"b_dn_{d}_{q2}"])
                    wt = tok.get(f"b_dtp_{d - 1}_{q2}")
                    if wt:
                        s.wait_ge(self.tM.h, wt)
                    tok[f"b_dnev_{d}_{q2}"] = self.sind(
                        s, lambda q2=q2, psd=psd: s.copy(
                            self.dnbf[q2][:, :], psd[:, :]))

            @blk.tensor
            def _(tn, d=d):
                for q2 in range(2):
                    tn.wait_ge(self.sS.h, tok[f"b_dnev_{d}_{q2}"])
                    wt = tok.get(f"b_dadd_{d - 1}_{q2}")
                    if wt:
                        tn.wait_ge(self.vS.h, wt)
                    for c in range(4):
                        ins = tn.matmul(
                            self.psT[q2][:, 128 * c:128 * (c + 1)],
                            self.dnbf[q2][:, 128 * c:128 * (c + 1)],
                            self.identb[:, :], start=True, stop=True)
                    ins.then_inc(self.tM.h, 1)
                    self.tM.n += 1
                    tok[f"b_dtp_{d}_{q2}"] = self.tM.n

            @blk.vector
            def _(v, d=d):
                if d == 0:
                    self.dmaw(v, tok["b_xst_ld"])
                for q2 in range(2):
                    v.wait_ge(self.tM.h, tok[f"b_dtp_{d}_{q2}"])
                    for c in range(4):
                        t = 4 * q2 + c
                        self.vind(v, lambda t=t, d=d, q2=q2, c=c: v.tensor_tensor(
                            xst[t][:, 128 * d:128 * (d + 1)],
                            xst[t][:, 128 * d:128 * (d + 1)],
                            self.psT[q2][:, 128 * c:128 * (c + 1)], ALU.add))
                    tok[f"b_dadd_{d}_{q2}"] = self.vS.n

        @blk.sync
        def _(sy):
            for t in range(8):
                sy.wait_ge(self.vS.h, tok[f"b_dadd_{DT - 1}_{t // 4}"])
                tok[f"b_xst_{t}"] = self.dma(
                    sy, self.d_X[2][128 * t:128 * (t + 1), :], xst[t][:, :],
                    self.sFX)

        pb.close()


_prog_cache = {}


def _get_prog():
    if "nc" not in _prog_cache:
        b = Builder()
        _prog_cache["nc"] = b.build()
    return _prog_cache["nc"]


def kernel(L0_attn_a_hidden_states, L0_attn_a_Wq, L0_attn_a_Wk, L0_attn_a_Wv,
           L0_attn_a_Wo, L0_attn_a_bo, L0_attn_b_weight_qkv,
           L0_attn_b_weight_proj, L0_ffn_lut, L0_ffn_gate_idx, L0_ffn_up_idx,
           L0_ffn_down_idx):
    f32 = np.float32
    x = np.ascontiguousarray(np.asarray(L0_attn_a_hidden_states, f32)
                             .reshape(SEQ, D))
    wqT = np.ascontiguousarray(np.asarray(L0_attn_a_Wq, f32).T)
    wkT = np.ascontiguousarray(np.asarray(L0_attn_a_Wk, f32).T)
    wvT = np.ascontiguousarray(np.asarray(L0_attn_a_Wv, f32).T)
    wo = np.asarray(L0_attn_a_Wo, f32)
    woTr = np.ascontiguousarray(
        wo.reshape(DT, 128, NH, 128).transpose(0, 3, 2, 1)).reshape(
            DT, 128, NH * 128)
    boP = np.ascontiguousarray(np.asarray(L0_attn_a_bo, f32)
                               .reshape(DT, 128).T)
    qkv = np.asarray(L0_attn_b_weight_qkv, f32)
    wvb = np.concatenate([qkv[768 * g + 640:768 * (g + 1)] for g in range(4)],
                         axis=0)
    wvbT = np.ascontiguousarray(wvb.T)
    wpT = np.ascontiguousarray(np.asarray(L0_attn_b_weight_proj, f32).T)
    lutp = np.ascontiguousarray(
        np.broadcast_to(np.asarray(L0_ffn_lut, f32), (128, 16)))
    gidxT = np.ascontiguousarray(np.asarray(L0_ffn_gate_idx, np.int32).T)
    uidxT = np.ascontiguousarray(np.asarray(L0_ffn_up_idx, np.int32).T)
    didxT = np.ascontiguousarray(np.asarray(L0_ffn_down_idx, np.int32).T)
    ident = np.eye(128, dtype=f32)

    pp = np.arange(128)[:, None]
    cq = np.arange(BLK)[None, :]
    cmasks = []
    for c in range(NCORES):
        cm = np.zeros((NSLOT, 128, 16 * BLK), f32)
        for k in range(NSLOT):
            for m in range(16):
                valid = (m * 128 + pp) <= (2 * c * 128 + cq)
                cm[k, :, BLK * m:BLK * (m + 1)] = valid.astype(f32)
        cmasks.append(cm)

    nc = _get_prog()
    in_maps = []
    for c in range(NCORES):
        rows = np.concatenate([np.arange(256 * (8 * k + c), 256 * (8 * k + c) + 256)
                               for k in range(NSLOT)])
        in_maps.append(dict(
            x_full=x, x_own=np.ascontiguousarray(x[rows]), wqT=wqT, wkT=wkT,
            wvT=wvT, woTr=woTr, boP=boP, wvbT=wvbT, wpT=wpT, lutp=lutp,
            gidxT=gidxT, uidxT=uidxT, didxT=didxT, cmask=cmasks[c],
            ident=ident))
    res = run_bass_kernel_spmd(nc, in_maps, core_ids=list(range(NCORES)))
    _prog_cache["taps"] = res.results
    y = np.empty((SEQ, D), f32)
    for c in range(NCORES):
        yo = np.asarray(res.results[c]["y_own"])
        for k in range(NSLOT):
            b = 8 * k + c
            y[256 * b:256 * (b + 1)] = yo[256 * k:256 * (k + 1)]
    return y.reshape(1, SEQ, D)


if __name__ == "__main__":
    b = Builder()
    nc = b.build()
    print("program built OK")


# version 21
# speedup vs baseline: 1.0554x; 1.0554x over previous
import numpy as np
from contextlib import ExitStack

import concourse.bass as bass
import concourse.mybir as mybir
from concourse.bass_utils import run_bass_kernel_spmd

F32 = mybir.dt.float32
BF16 = mybir.dt.bfloat16
I32 = mybir.dt.int32
AF = mybir.ActivationFunctionType
ALU = mybir.AluOpType
AX = mybir.AxisListType

SEQ = 8192
D = 2048
DEBUG_TAPS = False
NH = 16
KVH = 4
INTER = 5504
NIT = 43
HT = 16
DT = 16
BLK = 256
NSLOT = 4
NCORES = 8
OWN = NSLOT * BLK
EPS = 1e-5
NSUP = SEQ // 512
NSUPQ = OWN // 512

MAT_PUMP_P1 = 6
MAT_SG_P1 = 8
MAT_PUMP_P2 = (3, 5, 8, 11)
MAT_SG_P2 = 2
MAT_PUMP_FFN = 6


class Sem:
    def __init__(self, nc, es, name):
        self.h = es.enter_context(nc.semaphore(name))
        self.n = 0


class Builder:
    def __init__(self):
        self.nc = bass.Bass("TRN2", target_bir_lowering=False)
        nc = self.nc
        self.es = ExitStack()
        self.tok = {}

        def din(name, shape, dt=F32):
            return nc.dram_tensor(name, shape, dt, kind="ExternalInput").ap()

        self.x_full = din("x_full", [SEQ, D])
        self.x_own = din("x_own", [OWN, D])
        self.wqT = din("wqT", [D, D])
        self.wkT = din("wkT", [D, 512])
        self.wvT = din("wvT", [D, 512])
        self.woTr = din("woTr", [DT, 128, NH * 128])
        self.boP = din("boP", [128, DT])
        self.wvbT = din("wvbT", [D, 512])
        self.wpT = din("wpT", [D, D])
        self.lutp = din("lutp", [128, 16])
        self.gidxT = din("gidxT", [D, INTER], I32)
        self.uidxT = din("uidxT", [D, INTER], I32)
        self.didxT = din("didxT", [INTER, D], I32)
        self.cmask = din("cmask", [NSLOT, 128, 16 * BLK])
        self.ident = din("ident", [128, 128])
        self.y_own = nc.dram_tensor("y_own", [OWN, D], F32,
                                    kind="ExternalOutput").ap()
        if DEBUG_TAPS:
            self.t_x1 = nc.dram_tensor("t_x1", [OWN, D], F32,
                                       kind="ExternalOutput").ap()
            self.t_x2 = nc.dram_tensor("t_x2", [OWN, D], F32,
                                       kind="ExternalOutput").ap()
            self.t_x3 = nc.dram_tensor("t_x3", [OWN, D], F32,
                                       kind="ExternalOutput").ap()
            self.t_wg = nc.dram_tensor("t_wg", [128, HT * 128], F32,
                                       kind="ExternalOutput").ap()
            self.t_wd = nc.dram_tensor("t_wd", [128, NIT * 128], F32,
                                       kind="ExternalOutput").ap()
            self.t_wu = nc.dram_tensor("t_wu", [128, HT * 128], F32,
                                       kind="ExternalOutput").ap()
            self.t_k = nc.dram_tensor("t_k", [128, 1024], F32,
                                      kind="ExternalOutput").ap()
            self.t_v = nc.dram_tensor("t_v", [128, 1024], F32,
                                      kind="ExternalOutput").ap()
            self.t_q = nc.dram_tensor("t_q", [128, 1024], F32,
                                      kind="ExternalOutput").ap()

        self.d_K = nc.dram_tensor("d_K", [KVH, 128, SEQ], BF16, kind="Internal").ap()
        self.d_V = nc.dram_tensor("d_V", [KVH, 128, SEQ], BF16,
                                  kind="Internal").ap()
        self.d_Q = nc.dram_tensor("d_Q", [NH, 128, 1024], BF16,
                                  kind="Internal").ap()
        self.d_Wg = nc.dram_tensor("d_Wg", [NIT, 128, HT * 128], BF16,
                                   kind="Internal").ap()
        self.d_Wu = nc.dram_tensor("d_Wu", [NIT, 128, HT * 128], BF16,
                                   kind="Internal").ap()
        self.d_Wd = nc.dram_tensor("d_Wd", [DT, 128, NIT * 128], BF16,
                                   kind="Internal").ap()
        self.d_X = [nc.dram_tensor(f"d_X{i}", [OWN, D], F32, kind="Internal").ap()
                    for i in (1, 2, 3)]

        def mk(name):
            return Sem(nc, self.es, name)

        def mk2(name):
            return [Sem(nc, self.es, name + "0"), Sem(nc, self.es, name + "1")]

        self.sWS = mk("sWS")
        self.sWC = mk("sWC")
        self.sX = mk2("sX")
        self.sKV = mk2("sKV")
        self.sQ2 = mk2("sQ2")
        self.sWU = mk2("sWU")
        self.sWD = mk2("sWD")
        self.sWO = mk2("sWO")
        self.sXT = mk("sXT")
        self.sG = mk2("sG")
        self.sSg = mk2("sSg")
        self.sST = mk2("sST")
        self.sXR = mk2("sXR")
        self.sFX = mk("sFX")
        self.dM2 = mk2("dM")
        self.dMs = mk2("dMs")
        self.vS = Sem(nc, self.es, "vS")
        self.sS = Sem(nc, self.es, "sS")
        self.gS = Sem(nc, self.es, "gS")
        self.tM = Sem(nc, self.es, "tM")

        def sb(name, shape, dt):
            return nc.alloc_sbuf_tensor(name, shape, dt).ap()

        self.lutP = sb("lutP", [128, 16], F32)
        self.identb = sb("identb", [128, 128], BF16)
        self.onesb = sb("onesb", [128, 1], BF16)
        self.ones1f = sb("ones1f", [1, 128], F32)
        self.epsA = sb("epsA", [128, 1], F32)
        self.negone = sb("negone", [128, 1], F32)
        self.st6 = sb("st6", [128, 24], F32)
        self.mv2 = sb("mv2", [128, 2], F32)
        self.veps = sb("veps", [128, 1], F32)
        self.ivar4 = sb("ivar4", [128, 4], F32)
        self.negm4 = sb("negm4", [128, 4], F32)
        self.rstd4 = sb("rstd4", [128, 4], F32)
        self.lnb4 = sb("lnb4", [128, 4], F32)
        self.xb = [sb(f"xb{i}", [128, D], F32) for i in range(2)]
        self.lnt = [sb(f"lnt{i}", [128, D], BF16) for i in range(4)]
        self.ksum = sb("ksum", [128, 1], F32)
        self.negmean = sb("negmean", [128, 1], F32)
        self.rL = sb("rL", [1, 256], F32)
        self.jneg = sb("jneg", [128, 16], F32)
        self.onefF = sb("onefF", [128, 1], F32)
        self.dnbf = [sb(f"dnbf{i}", [128, 512], BF16) for i in range(2)]

        self.mat_stack = ExitStack()
        self.midx = [self.mat_stack.enter_context(
            nc.sbuf_tensor(f"midx{i}", [128, INTER], BF16, side="right")).ap()
            for i in range(2)]
        self.macc = [self.mat_stack.enter_context(
            nc.sbuf_tensor(f"macc{i}", [128, INTER], BF16, side="right")).ap()
            for i in range(2)]
        self.mtj = self.mat_stack.enter_context(
            nc.sbuf_tensor("mtj", [128, INTER], BF16, side="right")).ap()
        SGW = 1408
        self.SGW = SGW
        self.msq = self.mat_stack.enter_context(
            nc.sbuf_tensor("msq", [128, SGW], BF16, side="right")).ap()
        self.msq2 = self.mat_stack.enter_context(
            nc.sbuf_tensor("msq2", [128, SGW], BF16, side="right")).ap()
        self.mind = self.mat_stack.enter_context(
            nc.sbuf_tensor("mind", [128, 2 * SGW], BF16, side="right")).ap()
        self.midx_sg = self.mat_stack.enter_context(
            nc.sbuf_tensor("midx_sg", [128, 2 * SGW], BF16, side="right")).ap()
        self.macc_sg = self.mat_stack.enter_context(
            nc.sbuf_tensor("macc_sg", [128, 2 * SGW], BF16, side="right")).ap()

        def psu(name, shape, dt=F32):
            return nc.alloc_psum_tensor(name, shape, dt).ap()

        self.psA = psu("psA", [128, 512])
        self.psB = psu("psB", [128, 512])
        self.psC = psu("psC", [128, 512])
        self.psD = psu("psD", [128, 512])
        self.psE = psu("psE", [128, 512])
        self.psF = psu("psF", [128, 512])
        self.psL = self.psF[0:1, 0:256]
        self.psT = [psu(f"psT{i}", [128, 512]) for i in range(2)]

        self.mat_jobs = ([("g", i) for i in range(HT)]
                         + [("u", i) for i in range(HT)]
                         + [("d", i) for i in range(NIT)])
        SG_JOBS = {30, 31, 71, 72, 73, 74}
        self.vg_jobs = set()
        self.mat = dict(loads=0, load_tok={}, store_tok={}, done={},
                        cap=len(self.mat_jobs), sg_jobs=SG_JOBS,
                        vq=[i for i in range(len(self.mat_jobs))
                            if i not in SG_JOBS],
                        vqi=0, vj=0, buf_of={}, vg_tok={},
                        squnits=[], squi=0, sj=0, sg_ltok={}, sg_sttok={},
                        sg_stok={})
        for ji in sorted(SG_JOBS):
            kind, idx = self.mat_jobs[ji]
            W = INTER if kind in ("g", "u") else D
            qs_ = []
            c = 0
            while c < W:
                w = min(1408, W - c)
                qs_.append((ji, c, w))
                c += w
            self.mat["squnits"].extend(qs_)

    def dma(self, q, out, in_, sem):
        q.dma_start(out=out, in_=in_).then_inc(sem.h, 16)
        sem.n += 16
        return (sem, sem.n)

    def dmaw(self, eng, tokv):
        eng.wait_ge(tokv[0].h, tokv[1])

    def store_sems(self):
        return [self.sST[0], self.sST[1], self.sXR[0], self.sXR[1], self.sFX,
                self.dM2[0], self.dM2[1], self.dMs[0], self.dMs[1]]

    def store_snapshot(self):
        return [(s, s.n) for s in self.store_sems() if s.n > 0]

    def vchain(self, v, emit):
        v.wait_ge(self.vS.h, self.vS.n)
        emit().then_inc(self.vS.h, 1)
        self.vS.n += 1
        return self.vS.n

    def vind(self, v, emit):
        emit().then_inc(self.vS.h, 1)
        self.vS.n += 1
        return self.vS.n

    def schain(self, s, emit):
        s.wait_ge(self.sS.h, self.sS.n)
        emit().then_inc(self.sS.h, 1)
        self.sS.n += 1
        return self.sS.n

    def sind(self, s, emit):
        emit().then_inc(self.sS.h, 1)
        self.sS.n += 1
        return self.sS.n

    def mat_src(self, kind, idx):
        if kind == "g":
            dst = self.d_Wg[:, :, 128 * idx:128 * (idx + 1)].rearrange(
                "it p f -> p it f")
            return (self.gidxT[128 * idx:128 * (idx + 1), :], INTER, dst)
        if kind == "u":
            dst = self.d_Wu[:, :, 128 * idx:128 * (idx + 1)].rearrange(
                "it p f -> p it f")
            return (self.uidxT[128 * idx:128 * (idx + 1), :], INTER, dst)
        dst = self.d_Wd[:, :, 128 * idx:128 * (idx + 1)].rearrange(
            "d p f -> p d f")
        return (self.didxT[128 * idx:128 * (idx + 1), :], D, dst)

    def mat_gpsimd_service(self, g):
        m = self.mat
        for ji in sorted(m["done"].keys()):
            if ji in m["store_tok"]:
                continue
            kind, idx = self.mat_jobs[ji]
            _, w, dst = self.mat_src(kind, idx)
            sem, val = m["done"][ji]
            g.wait_ge(sem.h, val)
            bp = m["buf_of"][ji] % 2
            m["store_tok"][ji] = self.dma(
                g, dst, self.macc[bp][:, 0:w].rearrange("p (a f) -> p a f",
                                                        f=128), self.dM2[bp])
        while m["loads"] < m["cap"]:
            ji = m["vq"][m["loads"]] if m["loads"] < len(m["vq"]) else None
            if ji is None:
                break
            prev2 = m["vq"][m["loads"] - 2] if m["loads"] >= 2 else None
            prev4 = m["vq"][m["loads"] - 4] if m["loads"] >= 4 else None
            if prev2 is not None and prev2 not in m["done"]:
                break
            if prev4 is not None and prev4 not in m["store_tok"]:
                break
            kind, idx = self.mat_jobs[ji]
            srcap, w, _ = self.mat_src(kind, idx)
            if prev2 is not None:
                sem, val = m["done"][prev2]
                g.wait_ge(sem.h, val)
            if m.get("stage2_guard") and not m.get("stage2_guard_done"):
                for sv in m["stage2_guard"]:
                    self.dmaw(g, sv)
                m["stage2_guard_done"] = True
            li = m["loads"]
            m["load_tok"][ji] = self.dma(g, self.midx[li % 2][:, 0:w], srcap,
                                         self.sG[li % 2])
            m["buf_of"][ji] = li % 2
            m["loads"] += 1

    def mat_pump_drv(self, blk, nj):
        m = self.mat
        left = nj
        while left > 0 and m["vqi"] < len(m["vq"]):
            ji = m["vq"][m["vqi"]]
            if ji not in m["load_tok"]:
                break
            kind, idx = self.mat_jobs[ji]
            _, w, _ = self.mat_src(kind, idx)
            b = m["buf_of"][ji]
            hw = w // 2
            vg = ji in self.vg_jobs
            j0 = m["vj"]
            j1 = min(16, j0 + min(left, 1 if vg else left))
            left -= (j1 - j0)

            @blk.vector
            def _(v, ji=ji, j0=j0, j1=j1, b=b, hw=hw, vg=vg, w=w):
                if j0 == 0:
                    self.dmaw(v, m["load_tok"][ji])
                    if "consts" in self.tok:
                        self.dmaw(v, self.tok["consts"])
                    prev = m["vq"][m["vqi"] - 1] if m["vqi"] >= 1 else None
                    pd = m["done"].get(prev) if prev is not None else None
                    if pd is not None and pd[0] is self.gS:
                        v.wait_ge(self.gS.h, pd[1])
                    if not vg:
                        prev2 = (m["vq"][m["vqi"] - 2]
                                 if m["vqi"] >= 2 else None)
                        stok = (m["store_tok"].get(prev2)
                                if prev2 is not None else None)
                        if stok is not None:
                            self.dmaw(v, stok)
                        self.vchain(v, lambda: v.memset(self.macc[b][:, 0:w],
                                                        0.0))
                for j in range(j0, j1):
                    for hf in range(2):
                        sl = slice(hw * hf, hw * (hf + 1))
                        if vg:
                            gtok = m["vg_tok"].get(("g", ji, 2 * j + hf - 2))
                            if gtok:
                                v.wait_ge(self.gS.h, gtok)
                            ms = self.mtj[:, hw * ((2 * j + hf) % 2):
                                          hw * ((2 * j + hf) % 2) + hw]
                            self.vchain(v, lambda j=j, sl=sl, ms=ms:
                                        v.tensor_scalar(
                                            ms, self.midx[b][:, sl], float(j),
                                            self.lutP[:, j:j + 1],
                                            ALU.is_equal, ALU.mult))
                            m["vg_tok"][("v", ji, 2 * j + hf)] = self.vS.n
                        else:
                            self.vchain(v, lambda j=j, sl=sl: v.tensor_scalar(
                                self.mtj[:, 0:hw], self.midx[b][:, sl],
                                float(j), self.lutP[:, j:j + 1],
                                ALU.is_equal, ALU.mult))
                            self.vchain(v, lambda sl=sl: v.tensor_tensor(
                                self.macc[b][:, sl], self.macc[b][:, sl],
                                self.mtj[:, 0:hw], ALU.add))

            if vg:
                @blk.gpsimd
                def _(g, ji=ji, j0=j0, j1=j1, b=b, hw=hw, w=w):
                    if j0 == 0:
                        self.dmaw(g, m["load_tok"][ji])
                        prev2 = m["vq"][m["vqi"] - 2] if m["vqi"] >= 2 else None
                        stok = (m["store_tok"].get(prev2)
                                if prev2 is not None else None)
                        if stok is not None:
                            self.dmaw(g, stok)
                    for j in range(j0, j1):
                        for hf in range(2):
                            sl = slice(hw * hf, hw * (hf + 1))
                            g.wait_ge(self.vS.h, m["vg_tok"][("v", ji, 2 * j + hf)])
                            g.wait_ge(self.gS.h, self.gS.n)
                            ms = self.mtj[:, hw * ((2 * j + hf) % 2):
                                          hw * ((2 * j + hf) % 2) + hw]
                            if j == 0:
                                g.tensor_copy(self.macc[b][:, sl],
                                              ms).then_inc(self.gS.h, 1)
                            else:
                                g.tensor_tensor(self.macc[b][:, sl],
                                                self.macc[b][:, sl], ms,
                                                ALU.add).then_inc(self.gS.h, 1)
                            self.gS.n += 1
                            m["vg_tok"][("g", ji, 2 * j + hf)] = self.gS.n

            m["vj"] = j1
            if j1 == 16:
                if vg:
                    m["done"][ji] = (self.gS, self.gS.n)
                else:
                    m["done"][ji] = (self.vS, self.vS.n)
                m["vqi"] += 1
                m["vj"] = 0

    def mat_pump_sg(self, blk, nj):
        m = self.mat
        left = nj
        while left > 0:
            if m["squi"] >= len(m["squnits"]):
                return
            ui = m["squi"]
            ji, c0, w = m["squnits"][ui]
            kind, idx = self.mat_jobs[ji]
            srcap, _, dstap = self.mat_src(kind, idx)
            slot = ui % 2
            if ui not in m["sg_ltok"]:
                @blk.gpsimd
                def _(g, ui=ui):
                    for u2 in (ui, ui + 1):
                        if u2 >= len(m["squnits"]) or u2 in m["sg_ltok"]:
                            continue
                        ji2, c2, w2 = m["squnits"][u2]
                        k2, i2 = self.mat_jobs[ji2]
                        src2, _, _ = self.mat_src(k2, i2)
                        st = m["sg_sttok"].get(u2 - 2)
                        if st is not None:
                            self.dmaw(g, st)
                        ra = m["sg_sttok"].get("realloc")
                        if ra is not None and not m.get(f"sg_ra_done_{u2 % 2}"):
                            for sv in ra:
                                self.dmaw(g, sv)
                            m[f"sg_ra_done_{u2 % 2}"] = True
                        m["sg_ltok"][u2] = self.dma(
                            g, self.midx_sg[:, self.SGW * (u2 % 2):
                                            self.SGW * (u2 % 2) + w2],
                            src2[:, c2:c2 + w2], self.sSg[u2 % 2])
            j0 = m["sj"]
            j1 = min(16, j0 + min(left, 2))
            left -= (j1 - j0)

            @blk.scalar
            def _(s, ui=ui, ji=ji, j0=j0, j1=j1, w=w, slot=slot):
                for j in range(j0, j1):
                    if j == 0 and j0 == 0:
                        self.dmaw(s, m["sg_ltok"][ui])
                        s.wait_ge(self.vS.h, self.tok["consts_v"])
                        self.dmaw(s, self.tok["consts"])
                        s.wait_ge(self.gS.h, self.gS.n)
                    gtok = m["sg_stok"].get(("g", ui, j - 2))
                    if gtok:
                        s.wait_ge(self.gS.h, gtok)
                    mi = self.midx_sg[:, self.SGW * slot:self.SGW * slot + w]
                    self.schain(s, lambda j=j, mi=mi: s.activation(
                        self.msq[:, 0:w], mi, AF.Square,
                        bias=self.jneg[:, j:j + 1]))
                    self.schain(s, lambda: s.activation(
                        self.msq2[:, 0:w], self.msq[:, 0:w],
                        AF.Relu, bias=self.onefF[:, :], scale=-1.0))
                    self.schain(s, lambda j=j: s.activation(
                        self.mind[:, self.SGW * (j % 2):self.SGW * (j % 2) + w],
                        self.msq2[:, 0:w], AF.Identity,
                        scale=self.lutP[:, j:j + 1]))
                    m["sg_stok"][("s", ui, j)] = self.sS.n

            @blk.gpsimd
            def _(g, ui=ui, j0=j0, j1=j1, w=w, slot=slot):
                if j0 == 0:
                    self.dmaw(g, m["sg_ltok"][ui])
                    st = m["sg_sttok"].get(ui - 2)
                    if st is not None:
                        self.dmaw(g, st)
                    g.wait_ge(self.gS.h, self.gS.n)
                    g.memset(self.macc_sg[:, self.SGW * slot:
                                          self.SGW * slot + w],
                             0.0).then_inc(self.gS.h, 1)
                    self.gS.n += 1
                for j in range(j0, j1):
                    g.wait_ge(self.sS.h, m["sg_stok"][("s", ui, j)])
                    g.wait_ge(self.gS.h, self.gS.n)
                    g.tensor_tensor(
                        self.macc_sg[:, self.SGW * slot:self.SGW * slot + w],
                        self.macc_sg[:, self.SGW * slot:self.SGW * slot + w],
                        self.mind[:, self.SGW * (j % 2):self.SGW * (j % 2) + w],
                        ALU.add).then_inc(self.gS.h, 1)
                    self.gS.n += 1
                    m["sg_stok"][("g", ui, j)] = self.gS.n
                if j1 == 16:
                    g.wait_ge(self.gS.h, self.gS.n)
                    dst = dstap[:, c0 // 128:(c0 + w) // 128, :] \
                        if False else None
                    kindq, idxq = self.mat_jobs[ji]
                    _, _, dfull = self.mat_src(kindq, idxq)
                    m["sg_sttok"][ui] = self.dma(
                        g, dfull[:, c0 // 128:(c0 + w) // 128, :],
                        self.macc_sg[:, self.SGW * slot:self.SGW * slot + w
                                     ].rearrange("p (a f) -> p a f", f=128),
                        self.dMs[ui % 2])
                    if ui + 1 >= len(m["squnits"]) or \
                            m["squnits"][ui + 1][0] != ji:
                        m["store_tok"][ji] = m["sg_sttok"][ui]

            m["sj"] = j1
            if j1 == 16:
                m["squi"] += 1
                m["sj"] = 0

    def mat_pump_section(self, blk, nj, nsg=0):
        @blk.gpsimd
        def _(g):
            self.mat_gpsimd_service(g)

        if nsg:
            self.mat_pump_sg(blk, nsg)
        self.mat_pump_drv(blk, nj)

    def mat_drain(self, blk, upto=None):
        if upto is None:
            upto = len(self.mat_jobs)
        guard = 0
        while any(ji not in self.mat["store_tok"] for ji in range(upto)):
            self.mat_pump_section(blk, 48, nsg=16)
            guard += 1
            assert guard < 900, "mat drain stuck"

    def mat_stage2(self, blk):
        m = self.mat
        self.mat["cap"] = max(self.mat["loads"], 1)
        guard = 0
        gu_jobs = [ji for ji in range(2 * HT)]
        loaded = set(list(m["load_tok"].keys()))
        sg_started = set(u[0] for i, u in enumerate(m["squnits"])
                         if i <= m["squi"] or i in m["sg_ltok"])
        need = set(gu_jobs) | loaded | sg_started
        while any(ji not in m["store_tok"] for ji in need):
            self.mat_pump_section(blk, 48, nsg=16)
            loaded = set(list(m["load_tok"].keys()))
            sg_started = set(u[0] for i, u in enumerate(m["squnits"])
                             if i <= m["squi"] or i in m["sg_ltok"])
            need = set(gu_jobs) | loaded | sg_started
            guard += 1
            assert guard < 900, "mat stage2 drain stuck"
        m["stage2_guard"] = [(self.dM2[0], self.dM2[0].n),
                             (self.dM2[1], self.dM2[1].n),
                             (self.dMs[0], self.dMs[0].n),
                             (self.dMs[1], self.dMs[1].n)]
        m["stage2_guard_done"] = False
        self.mat_stack.close()
        nc = self.nc
        self.mat_stack = ExitStack()
        self.midx = [self.mat_stack.enter_context(
            nc.sbuf_tensor(f"midxB{i}", [128, D], BF16, side="right")).ap()
            for i in range(2)]
        self.macc = [self.mat_stack.enter_context(
            nc.sbuf_tensor(f"maccB{i}", [128, D], BF16, side="right")).ap()
            for i in range(2)]
        self.mtj = self.mat_stack.enter_context(
            nc.sbuf_tensor("mtjB", [128, D], BF16, side="right")).ap()
        SGW = 512
        self.SGW = SGW
        self.msq = self.mat_stack.enter_context(
            nc.sbuf_tensor("msqB", [128, SGW], BF16, side="right")).ap()
        self.msq2 = self.mat_stack.enter_context(
            nc.sbuf_tensor("msq2B", [128, SGW], BF16, side="right")).ap()
        self.mind = self.mat_stack.enter_context(
            nc.sbuf_tensor("mindB", [128, 2 * SGW], BF16, side="right")).ap()
        self.midx_sg = self.mat_stack.enter_context(
            nc.sbuf_tensor("midx_sgB", [128, 2 * SGW], BF16, side="right")).ap()
        self.macc_sg = self.mat_stack.enter_context(
            nc.sbuf_tensor("macc_sgB", [128, 2 * SGW], BF16, side="right")).ap()
        self.mat["cap"] = len(self.mat_jobs)

    def mat_store_tok(self, last_job):
        best = {}
        for ji in range(last_job + 1):
            t = self.mat["store_tok"].get(ji)
            assert t is not None, "mat job not stored before use"
            sem, val = t
            best[id(sem)] = (sem, max(val, best.get(id(sem), (sem, 0))[1]))
        return list(best.values())

    def ln_vector(self, v, xin, col, extra_waits=()):
        for sem, val in extra_waits:
            if val:
                v.wait_ge(sem.h, val)
        self.vchain(v, lambda: v.bn_stats(self.st6[:, 0:6], xin[:, 0:512]))
        for q in range(1, 4):
            self.vind(v, lambda q=q: v.bn_stats(
                self.st6[:, 6 * q:6 * q + 6], xin[:, 512 * q:512 * (q + 1)]))
        self.vchain(v, lambda: v.bn_aggr(self.mv2[:, :], self.st6[:, :]))
        self.vchain(v, lambda: v.tensor_scalar(
            self.ivar4[:, col:col + 1], self.mv2[:, 1:2], self.epsA[:, :],
            None, ALU.add))
        return self.vchain(v, lambda: v.tensor_scalar(
            self.negm4[:, col:col + 1], self.mv2[:, 0:1], self.negone[:, :],
            None, ALU.mult))

    def ln_scalar(self, s, xin, lnout, col, vtok, extra_waits=()):
        s.wait_ge(self.vS.h, vtok)
        for sem, val in extra_waits:
            if val:
                s.wait_ge(sem.h, val)
        self.schain(s, lambda: s.activation(
            self.veps[:, col % 1:col % 1 + 1], self.ivar4[:, col:col + 1], AF.Ln))
        self.schain(s, lambda: s.activation(
            self.rstd4[:, col:col + 1], self.veps[:, 0:1], AF.Exp, scale=-0.5))
        self.schain(s, lambda: s.activation(
            self.lnb4[:, col:col + 1], self.negm4[:, col:col + 1], AF.Identity,
            scale=self.rstd4[:, col:col + 1]))
        return self.schain(s, lambda: s.activation(
            lnout[:, :], xin[:, :], AF.Identity,
            bias=self.lnb4[:, col:col + 1], scale=self.rstd4[:, col:col + 1]))

    def emit_ln_tile(self, blk, pre, t, xsrc_fn, dstT, dst_stride, first_waits):
        tok = self.tok

        @blk.sync
        def _(sy, t=t):
            wt = tok.get(f"{pre}_lnap_{t - 2}")
            if wt:
                sy.wait_ge(self.sS.h, wt)
            for sem, val in first_waits:
                sy.wait_ge(sem.h, val)
            tok[f"{pre}_xl_{t}"] = self.dma(sy, self.xb[t % 2][:, :],
                                            xsrc_fn(t), self.sX[t % 2])

        @blk.vector
        def _(v, t=t):
            waits = [tok[f"{pre}_xl_{t}"],
                     (self.sS, tok.get(f"{pre}_lnap_{t - 4}", 0))]
            tok[f"{pre}_lnv_{t}"] = self.ln_vector(v, self.xb[t % 2], t % 4,
                                                   extra_waits=waits)

        @blk.scalar
        def _(s, t=t):
            waits = [(self.tM, tok.get(f"{pre}_tp_{t - 4}", 0))]
            tok[f"{pre}_lnap_{t}"] = self.ln_scalar(
                s, self.xb[t % 2], self.lnt[t % 4], t % 4,
                tok[f"{pre}_lnv_{t}"], extra_waits=waits)

        t_off = t % 4 if dst_stride == 512 else t
        tb = [self.psT[0], self.psT[1], self.psA, self.psB]

        @blk.tensor
        def _(tn, t=t):
            tn.wait_ge(self.sS.h, tok[f"{pre}_lnap_{t}"])
            if t == 0:
                self.dmaw(tn, tok["identb"])
            for gb in range(4):
                wt = tok.get(f"{pre}_tpev_prev_{gb}")
                if wt:
                    tn.wait_ge(self.vS.h, wt)
                for k in range(4):
                    h = 4 * gb + k
                    ins = tn.matmul(tb[gb][:, 128 * k:128 * (k + 1)],
                                    self.lnt[t % 4][:, 128 * h:128 * (h + 1)],
                                    self.identb[:, :], start=True, stop=True)
                ins.then_inc(self.tM.h, 1)
                self.tM.n += 1
                tok[f"{pre}_tp_{t}_{gb}"] = self.tM.n

        @blk.vector
        def _(v, t=t, t_off=t_off):
            for gb in range(4):
                v.wait_ge(self.tM.h, tok[f"{pre}_tp_{t}_{gb}"])
                dst = dstT[:, :].rearrange("p (h f) -> p h f", f=dst_stride)[
                    :, 4 * gb:4 * gb + 4, 128 * t_off:128 * (t_off + 1)]
                tok[f"{pre}_tpev_prev_{gb}"] = self.vind(
                    v, lambda dst=dst, gb=gb: v.tensor_copy(
                        dst, tb[gb][:, :].rearrange("p (a b) -> p a b", b=128)))
        tok[f"{pre}_tp_{t}"] = self.tM.n
        tok[f"{pre}_tpev_{t}"] = self.vS.n

    def build(self):
        nc = self.nc
        tok = self.tok
        blk_cm = nc.Block()
        blk = blk_cm.__enter__()

        p1 = ExitStack()
        wkTb = p1.enter_context(nc.sbuf_tensor("wkTb", [128, HT * 512], BF16)).ap()
        wvTb = p1.enter_context(nc.sbuf_tensor("wvTb", [128, HT * 512], BF16)).ap()
        wqTb = p1.enter_context(nc.sbuf_tensor("wqTb", [128, HT * 512], BF16)).ap()
        lnT = [p1.enter_context(nc.sbuf_tensor(f"lnT{i}", [128, HT * 512],
                                               BF16)).ap() for i in range(2)]
        kvst = [p1.enter_context(nc.sbuf_tensor(f"kvst{i}", [128, 512],
                                                BF16)).ap() for i in range(2)]
        self.lnT, self.wkTb, self.wvTb, self.wqTb = lnT, wkTb, wvTb, wqTb
        self.kvst = kvst

        @blk.gpsimd
        def _(g):
            self.dma(g, self.identb[:, :], self.ident, self.sWS)
            self.dma(g, wkTb[:, :].rearrange("p (h f) -> p h f", f=512),
                     self.wkT.rearrange("(h p) f -> p h f", p=128), self.sWS)
            self.dma(g, wvTb[:, :].rearrange("p (h f) -> p h f", f=512),
                     self.wvT.rearrange("(h p) f -> p h f", p=128), self.sWS)
            self.dma(g, wqTb[:, :].rearrange("p (h f) -> p h f", f=512),
                     self.wqT[:, 0:512].rearrange("(h p) f -> p h f", p=128),
                     self.sWS)
            tok["identb"] = (self.sWS, self.sWS.n)
            tok["w_kv"] = (self.sWS, self.sWS.n)
            tok["wq_q0"] = (self.sWS, self.sWS.n)
            self.mat_gpsimd_service(g)

        @blk.sync
        def _(sy):
            tok["consts"] = self.dma(sy, self.lutP[:, :], self.lutp, self.sWC)

        @blk.vector
        def _(v):
            self.dmaw(v, tok["consts"])
            v.memset(self.epsA[:, :], EPS)
            v.memset(self.negone[:, :], -1.0)
            self.vind(v, lambda: v.memset(self.onesb[:, :], 1.0))
            self.vind(v, lambda: v.memset(self.ones1f[:, :], 1.0))
            self.vind(v, lambda: v.memset(self.onefF[:, :], 1.0))
            for j in range(16):
                self.vind(v, lambda j=j: v.memset(self.jneg[:, j:j + 1],
                                                  float(-j)))
            tok["consts_v"] = self.vS.n

        NT_FULL = NSUP * 4

        def x_src(t):
            if t < NT_FULL:
                return self.x_full[128 * t:128 * (t + 1), :]
            tt = t - NT_FULL
            return self.x_own[128 * tt:128 * (tt + 1), :]

        for su in range(NSUP + NSUPQ):
            for i in range(4):
                t = 4 * su + i
                fw = [tok["consts"]] if t == 0 else []
                self.emit_ln_tile(blk, "p1", t, x_src, lnT[su % 2], 512, fw)
            if su >= 1:
                self.emit_proj_supertile(blk, su - 1)
            self.mat_pump_section(blk, MAT_PUMP_P1, nsg=MAT_SG_P1)
        self.emit_proj_supertile(blk, NSUP + NSUPQ - 1)

        self.emit_q_pass(blk, wqTb)
        self.mat_pump_section(blk, MAT_PUMP_P1)

        p1.close()
        p2 = ExitStack()
        Ksl = [p2.enter_context(nc.sbuf_tensor(f"Ksl{i}", [128, SEQ], BF16)).ap()
               for i in range(2)]
        Vsl = [p2.enter_context(nc.sbuf_tensor(f"Vsl{i}", [128, SEQ], BF16)).ap()
               for i in range(2)]
        PT = [p2.enter_context(nc.sbuf_tensor(f"PT{i}", [128, 512], BF16)).ap()
              for i in range(2)]
        cmb = p2.enter_context(nc.sbuf_tensor("cmb", [128, 16 * BLK], BF16)).ap()
        aoT = p2.enter_context(nc.sbuf_tensor("aoT", [128, NH * 256], BF16)).ap()
        woS = [p2.enter_context(nc.sbuf_tensor(f"woS{i}", [128, HT * 128],
                                               BF16)).ap() for i in range(2)]
        qs = [p2.enter_context(nc.sbuf_tensor(f"qs{i}", [128, 256], BF16)).ap()
              for i in range(2)]
        self.avf = p2.enter_context(nc.sbuf_tensor("avf", [128, 256], F32)).ap()
        self.boS = p2.enter_context(nc.sbuf_tensor("boS", [128, DT], F32)).ap()
        self.opbf = p2.enter_context(nc.sbuf_tensor("opbf", [128, 256],
                                                    BF16)).ap()
        self.Ksl, self.Vsl, self.PT, self.cmb, self.aoT, self.woS = \
            Ksl, Vsl, PT, cmb, aoT, woS
        self.qs = qs
        self.hseq = 0
        self.slast = {}

        j_t, j_s, j_v = self.tM.n, self.sS.n, self.vS.n
        self.p2_join = (j_t, j_s, j_v)
        self.p2_jdO = self.store_snapshot()
        prev_ao = None

        @blk.sync
        def _(sy):
            sy.wait_ge(self.sS.h, j_s)
            for sv in self.p2_jdO:
                self.dmaw(sy, sv)
            tok["bo_ld"] = self.dma(sy, self.boS[:, :], self.boP, self.sWC)

        for slot in range(NSLOT):
            nkb = 16 * (slot + 1)

            @blk.gpsimd
            def _(g, slot=slot):
                if slot > 0:
                    g.wait_ge(self.tM.h, tok[f"att_done_{4 * slot - 1}"])
                else:
                    g.wait_ge(self.tM.h, j_t)
                    g.wait_ge(self.sS.h, j_s)
                tok[f"cmask_{slot}"] = self.dma(g, cmb[:, :],
                                                self.cmask[slot, :, :], self.sWS)
                self.mat_gpsimd_service(g)

            for kv in range(KVH):
                gidx = slot * KVH + kv
                par = gidx % 2

                @blk.sync
                def _(sy, kv=kv, gidx=gidx, par=par, nkb=nkb):
                    wt = tok.get(f"att_done_{gidx - 2}")
                    if wt:
                        sy.wait_ge(self.tM.h, wt)
                    else:
                        sy.wait_ge(self.tM.h, j_t)
                        sy.wait_ge(self.sS.h, j_s)
                        for sv in self.p2_jdO:
                            self.dmaw(sy, sv)
                    self.dma(sy, Ksl[par][:, 0:128 * nkb],
                             self.d_K[kv, :, 0:128 * nkb], self.sKV[par])
                    tok[f"kvload_{gidx}"] = self.dma(
                        sy, Vsl[par][:, 0:128 * nkb],
                        self.d_V[kv, :, 0:128 * nkb], self.sKV[par])

                @blk.vector
                def _(v, gidx=gidx, par=par, nkb=nkb):
                    self.dmaw(v, tok[f"kvload_{gidx}"])
                    self.vchain(v, lambda: v.reduce_sum(
                        self.ksum[:, :], Ksl[par][:, 0:128 * nkb], axis=AX.X))
                    self.vchain(v, lambda: v.tensor_scalar(
                        self.negmean[:, :], self.ksum[:, :],
                        -1.0 / (128 * nkb), None, ALU.mult))
                    tok[f"kshift_{gidx}"] = self.vchain(
                        v, lambda: v.tensor_scalar(
                            Ksl[par][:, 0:128 * nkb], Ksl[par][:, 0:128 * nkb],
                            self.negmean[:, :], None, ALU.add))

                for hh in range(4):
                    h = 4 * kv + hh
                    prev_ao = self.emit_attention_head(
                        blk, slot, kv, h, gidx, par, nkb, prev_ao)
                    @blk.gpsimd
                    def _(g):
                        self.mat_gpsimd_service(g)

                    if MAT_PUMP_P2[slot]:
                        self.mat_pump_drv(blk, MAT_PUMP_P2[slot])
                    if MAT_SG_P2:
                        self.mat_pump_sg(blk, MAT_SG_P2)
                tok[f"att_done_{gidx}"] = self.tM.n

            self.emit_attn_outproj(blk, slot)

        p2.close()
        self.mat_stage2(blk)
        self.emit_ffn(blk, self.d_X[0], self.d_X[1], 1)

        self.emit_attn_b(blk)

        self.emit_ffn(blk, self.d_X[2], None, 2)

        if DEBUG_TAPS:
            @blk.gpsimd
            def _(g):
                for sv in self.store_snapshot():
                    self.dmaw(g, sv)
                for (dst, srcap) in [
                        (self.t_x1, self.d_X[0]), (self.t_x2, self.d_X[1]),
                        (self.t_x3, self.d_X[2]),
                        (self.t_wg, self.d_Wg[0, :, :]),
                        (self.t_wd, self.d_Wd[0, :, :]),
                        (self.t_wu, self.d_Wu[0, :, :]),
                        (self.t_k, self.d_K[0, :, 0:1024]),
                        (self.t_v, self.d_V[0, :, 0:1024]),
                        (self.t_q, self.d_Q[0, :, 0:1024])]:
                    self.dma(g, dst, srcap, self.sFX)

        @blk.sync
        def _(sy):
            for sv in self.store_snapshot():
                self.dmaw(sy, sv)

        blk_cm.__exit__(None, None, None)
        self.es.close()
        return nc

    def emit_proj_supertile(self, blk, su):
        tok = self.tok
        own = su >= NSUP
        if own:
            return
        src = self.lnT[su % 2]
        groups = [("k", f) for f in range(4)] + [("v", i) for i in range(4)]

        for gi in range(0, len(groups), 2):
            pair = groups[gi:gi + 2]

            @blk.tensor
            def _(tn, pair=pair, su=su, gi=gi, src=src, own=own):
                tn.wait_ge(self.vS.h, tok[f"p1_tpev_{4 * su + 3}"])
                if gi == 0 and su == 0:
                    self.dmaw(tn, tok["w_kv"])
                for pi, (kind, f) in enumerate(pair):
                    psd = (self.psC, self.psD)[pi]
                    wt = tok.get(f"pj_ev_{su}_{gi - 2 + pi}")
                    if wt is None and su > 0:
                        wt = tok.get(f"pj_ev_{su - 1}_{len(groups) - 2 + pi}")
                    if wt:
                        tn.wait_ge(self.sS.h, wt)
                    for ht in range(HT):
                        if kind == "k":
                            lhs = self.wkTb[:, 512 * ht + 128 * f:
                                            512 * ht + 128 * (f + 1)]
                            rhs = src[:, 512 * ht:512 * (ht + 1)]
                        else:
                            lhs = src[:, 512 * ht + 128 * f:512 * ht + 128 * (f + 1)]
                            rhs = self.wvTb[:, 512 * ht:512 * (ht + 1)]
                        ins = tn.matmul(psd[:, :], lhs, rhs, start=(ht == 0),
                                        stop=(ht == HT - 1))
                    ins.then_inc(self.tM.h, 1)
                    self.tM.n += 1
                    tok[f"pj_mm_{su}_{gi + pi}"] = self.tM.n

            @blk.scalar
            def _(s, pair=pair, su=su, gi=gi):
                for pi, (kind, f) in enumerate(pair):
                    psd = (self.psC, self.psD)[pi]
                    s.wait_ge(self.tM.h, tok[f"pj_mm_{su}_{gi + pi}"])
                    st = self.kvst[(gi + pi) % 2]
                    wt = tok.get(f"pj_st_{su}_{gi - 2 + pi}")
                    if wt is None and su > 0:
                        wt = tok.get(f"pj_st_{su - 1}_{gi + pi + 6}")
                    if wt:
                        self.dmaw(s, wt)
                    tok[f"pj_ev_{su}_{gi + pi}"] = self.sind(
                        s, lambda st=st, psd=psd: s.copy(st[:, :], psd[:, :]))

            if True:
                @blk.sync
                def _(sy, pair=pair, su=su, gi=gi):
                    for pi, (kind, f) in enumerate(pair):
                        sy.wait_ge(self.sS.h, tok[f"pj_ev_{su}_{gi + pi}"])
                        st = self.kvst[(gi + pi) % 2]
                        sem = self.sST[(gi + pi) % 2]
                        if kind == "k":
                            tok[f"pj_st_{su}_{gi + pi}"] = self.dma(
                                sy, self.d_K[f, :, 512 * su:512 * (su + 1)],
                                st[:, :], sem)
                        else:
                            t = 4 * su + f
                            tok[f"pj_st_{su}_{gi + pi}"] = self.dma(
                                sy,
                                self.d_V[:, :, 128 * t:128 * (t + 1)].rearrange(
                                    "kv p f -> p kv f"),
                                st[:, :].rearrange("p (kv f) -> p kv f", f=128),
                                sem)



    def emit_q_pass(self, blk, wqTb):
        tok = self.tok
        for q in range(4):
            if q > 0:
                @blk.gpsimd
                def _(g, q=q):
                    g.wait_ge(self.tM.h, tok[f"q_mm_{q - 1}_last"])
                    tok[f"wq_q{q}"] = self.dma(
                        g, wqTb[:, :].rearrange("p (h f) -> p h f", f=512),
                        self.wqT[:, 512 * q:512 * (q + 1)].rearrange(
                            "(h p) f -> p h f", p=128), self.sWS)
                    self.mat_gpsimd_service(g)
            for su2 in range(2):
                su = NSUP + su2
                for jr in range(2):
                    @blk.tensor
                    def _(tn, q=q, su2=su2, su=su, jr=jr):
                        if su2 == 0 and jr == 0:
                            self.dmaw(tn, tok[f"wq_q{q}"])
                            if q == 0:
                                tn.wait_ge(self.vS.h,
                                           tok[f"p1_tpev_{4 * (NSUP + 1) + 3}"])
                                tn.wait_ge(self.sS.h,
                                           tok[f"pj_ev_{NSUP - 1}_7"])
                        src = self.lnT[su % 2]
                        for pi in range(2):
                            j = 2 * jr + pi
                            psd = (self.psC, self.psD)[pi]
                            wt = tok.get(f"qp_ev_{q}_{su2}_{jr - 1}_{pi}")
                            if wt is None:
                                wt = tok.get(f"qp_ev_{q}_{su2 - 1}_{1}_{pi}")
                            if wt is None:
                                wt = tok.get(f"qp_ev_{q - 1}_1_1_{pi}")
                            if wt:
                                tn.wait_ge(self.sS.h, wt)
                            for ht in range(HT):
                                lhs = wqTb[:, 512 * ht + 128 * j:
                                           512 * ht + 128 * (j + 1)]
                                rhs = src[:, 512 * ht:512 * (ht + 1)]
                                ins = tn.matmul(psd[:, :], lhs, rhs,
                                                start=(ht == 0),
                                                stop=(ht == HT - 1))
                            ins.then_inc(self.tM.h, 1)
                            self.tM.n += 1
                            tok[f"qp_mm_{q}_{su2}_{jr}_{pi}"] = self.tM.n
                        tok[f"q_mm_{q}_last"] = self.tM.n

                    @blk.scalar
                    def _(s, q=q, su2=su2, jr=jr):
                        for pi in range(2):
                            j = 2 * jr + pi
                            psd = (self.psC, self.psD)[pi]
                            s.wait_ge(self.tM.h, tok[f"qp_mm_{q}_{su2}_{jr}_{pi}"])
                            st = self.kvst[pi]
                            wt = tok.get(f"qp_st_{q}_{su2}_{jr - 1}_{pi}")
                            if wt is None:
                                wt = tok.get(f"qp_st_{q}_{su2 - 1}_1_{pi}")
                            if wt is None:
                                wt = tok.get(f"qp_st_{q - 1}_1_1_{pi}")
                            if wt is None:
                                wt = tok.get(f"pj_st_{NSUP - 1}_{6 + pi}")
                            if wt:
                                self.dmaw(s, wt)
                            tok[f"qp_ev_{q}_{su2}_{jr}_{pi}"] = self.sind(
                                s, lambda st=st, psd=psd: s.copy(st[:, :], psd[:, :]))

                    @blk.sync
                    def _(sy, q=q, su2=su2, jr=jr):
                        for pi in range(2):
                            j = 2 * jr + pi
                            h = 4 * q + j
                            sy.wait_ge(self.sS.h, tok[f"qp_ev_{q}_{su2}_{jr}_{pi}"])
                            tok[f"qp_st_{q}_{su2}_{jr}_{pi}"] = self.dma(
                                sy, self.d_Q[h, :, 512 * su2:512 * (su2 + 1)],
                                self.kvst[pi][:, :], self.sST[pi])

    def emit_attention_head(self, blk, slot, kv, h, gidx, par, nkb, prev_ao):
        tok = self.tok
        npair = nkb // 2
        tail0 = npair - 8
        Ksl, Vsl, PT, cmb, aoT = (self.Ksl, self.Vsl, self.PT, self.cmb,
                                  self.aoT)
        hseq = self.hseq
        self.hseq += 1
        qb = self.qs[hseq % 2]

        @blk.sync
        def _(sy):
            wt = self.slast.get(hseq - 2)
            if wt:
                sy.wait_ge(self.tM.h, wt)
            elif hseq < 2:
                for sv in self.p2_jdO:
                    self.dmaw(sy, sv)
            tok[f"qld_{hseq}"] = self.dma(
                sy, qb[:, :], self.d_Q[h, :, 256 * slot:256 * (slot + 1)],
                self.sQ2[hseq % 2])

        base = self.tM.n

        def s_tok(p):
            return base + (1 if p == 0 else 2 * p)

        def a_tok(p):
            return base + (2 * npair if p == npair - 1 else 3 + 2 * p)

        bc_tok = base + 2 * npair + 1

        exp_toks = []

        @blk.scalar
        def _(s):
            for p in range(npair):
                s.wait_ge(self.tM.h, s_tok(p))
                ps = (self.psA, self.psB)[p % 2]
                ins = s.activation(PT[p % 2][:, :], ps[:, :], AF.Exp)
                ins.then_inc(self.sS.h, 1)
                self.sS.n += 1
                exp_toks.append(self.sS.n)

        msk_toks = {}
        r_tok = [None]
        av_tok = [None]
        ao_tok = [None]

        @blk.gpsimd
        def _(g):
            for p in range(tail0, npair):
                g.wait_ge(self.sS.h, exp_toks[p])
                m = p - tail0
                for half in range(2):
                    g.wait_ge(self.gS.h, self.gS.n)
                    g.tensor_tensor(
                        PT[p % 2][:, 256 * half:256 * (half + 1)],
                        PT[p % 2][:, 256 * half:256 * (half + 1)],
                        cmb[:, 256 * (2 * m + half):256 * (2 * m + half + 1)],
                        ALU.mult).then_inc(self.gS.h, 1)
                    self.gS.n += 1
                msk_toks[p] = self.gS.n

        @blk.vector
        def _(v):
            v.wait_ge(self.tM.h, a_tok(npair - 1))
            r_tok[0] = self.vchain(v, lambda: v.reciprocal(self.rL[:, :],
                                                           self.psL))
            av_tok[0] = self.vchain(v, lambda: v.tensor_copy(
                self.avf[:, :], self.psC[:, 0:256]))
            v.wait_ge(self.tM.h, bc_tok)
            ao_tok[0] = self.vchain(v, lambda: v.tensor_tensor(
                aoT[:, 256 * h:256 * (h + 1)], self.avf[:, :],
                self.psD[:, 0:256], ALU.mult))
            tok[f"ao_{slot}_{h}"] = ao_tok[0]

        @blk.tensor
        def _(tn):
            if prev_ao is None:
                tn.wait_ge(self.sS.h, self.p2_join[1])
                tn.wait_ge(self.vS.h, self.p2_join[2])

            def s_pair(p):
                ps = (self.psA, self.psB)[p % 2]
                if p == 0:
                    tn.wait_ge(self.vS.h, tok[f"kshift_{gidx}"])
                    self.dmaw(tn, tok[f"kvload_{gidx}"])
                    self.dmaw(tn, tok[f"qld_{hseq}"])
                    if h == 4 * kv:
                        self.dmaw(tn, tok[f"cmask_{slot}"])
                if p >= 2:
                    tn.wait_ge(self.sS.h, exp_toks[p - 2])
                for half in range(2):
                    kb = 2 * p + half
                    ins = tn.matmul(ps[:, 256 * half:256 * (half + 1)],
                                    Ksl[par][:, 128 * kb:128 * (kb + 1)],
                                    qb[:, :], start=True, stop=True)
                ins.then_inc(self.tM.h, 1)
                self.tM.n += 1
                assert self.tM.n == s_tok(p), (self.tM.n, s_tok(p))

            def lav_pair(p):
                if p == 0 and prev_ao is not None:
                    tn.wait_ge(self.vS.h, prev_ao)
                if p == 0 and h == 0 and slot > 0:
                    tn.wait_ge(self.sS.h, tok[f"op_ev_{slot - 1}_{DT - 1}"])
                if p >= tail0:
                    tn.wait_ge(self.gS.h, msk_toks[p])
                else:
                    tn.wait_ge(self.sS.h, exp_toks[p])
                for half in range(2):
                    kb = 2 * p + half
                    tn.matmul(self.psL, self.onesb[:, :],
                              PT[p % 2][:, 256 * half:256 * (half + 1)],
                              start=(kb == 0), stop=(kb == nkb - 1))
                    ins = tn.matmul(self.psC[:, 0:256],
                                    Vsl[par][:, 128 * kb:128 * (kb + 1)],
                                    PT[p % 2][:, 256 * half:256 * (half + 1)],
                                    start=(kb == 0), stop=(kb == nkb - 1))
                ins.then_inc(self.tM.h, 1)
                self.tM.n += 1
                assert self.tM.n == a_tok(p), (self.tM.n, a_tok(p))

            s_pair(0)
            if npair > 1:
                s_pair(1)
            for p in range(2, npair):
                lav_pair(p - 2)
                s_pair(p)
            if npair > 1:
                lav_pair(npair - 2)
            lav_pair(npair - 1)
            tn.wait_ge(self.vS.h, r_tok[0])
            ins = tn.matmul(self.psD[:, 0:256], self.ones1f[:, :],
                            self.rL[:, :], start=True, stop=True)
            ins.then_inc(self.tM.h, 1)
            self.tM.n += 1
            assert self.tM.n == bc_tok

        self.slast[hseq] = base + (1 if npair == 1 else 2 * (npair - 1))
        return ao_tok[0]

    def emit_attn_outproj(self, blk, slot):
        tok = self.tok

        @blk.sync
        def _(sy, slot=slot):
            for c in range(2):
                wt = tok.get(f"x1st_{slot - 1}_{c}")
                if wt:
                    self.dmaw(sy, wt)
                if slot == 0:
                    sy.wait_ge(self.sS.h, self.p2_join[1])
                tok[f"xown_{slot}_{c}"] = self.dma(
                    sy, self.xb[c][:, :],
                    self.x_own[256 * slot + 128 * c:256 * slot + 128 * (c + 1),
                               :], self.sX[c])

        for d in range(DT):
            @blk.gpsimd
            def _(g, d=d, slot=slot):
                wt = tok.get(f"op_mm_{slot}_{d - 1}")
                if wt is None and slot > 0:
                    wt = tok.get(f"op_mm_{slot - 1}_{DT - 1}")
                if wt:
                    g.wait_ge(self.tM.h, wt)
                elif slot == 0 and d < 2:
                    for sv in self.p2_jdO:
                        self.dmaw(g, sv)
                tok[f"op_w_{slot}_{d}"] = self.dma(
                    g, self.woS[d % 2][:, :], self.woTr[d, :, :],
                    self.sWO[d % 2])

            @blk.tensor
            def _(tn, d=d, slot=slot):
                self.dmaw(tn, tok[f"op_w_{slot}_{d}"])
                if d == 0:
                    tn.wait_ge(self.vS.h, tok[f"ao_{slot}_{15}"])
                wt = tok.get(f"op_ev_{slot}_{d - 1}")
                if wt:
                    tn.wait_ge(self.sS.h, wt)
                for h in range(NH):
                    ins = tn.matmul(self.psF[:, 0:256],
                                    self.woS[d % 2][:, 128 * h:128 * (h + 1)],
                                    self.aoT[:, 256 * h:256 * (h + 1)],
                                    start=(h == 0), stop=(h == NH - 1))
                ins.then_inc(self.tM.h, 1)
                self.tM.n += 1
                tok[f"op_mm_{slot}_{d}"] = self.tM.n

            @blk.scalar
            def _(s, d=d, slot=slot):
                s.wait_ge(self.tM.h, tok[f"op_mm_{slot}_{d}"])
                if slot == 0 and d == 0:
                    self.dmaw(s, tok["bo_ld"])
                wt = tok.get(f"op_tp_{slot}_{d - 1}")
                if wt:
                    s.wait_ge(self.tM.h, wt)
                tok[f"op_ev_{slot}_{d}"] = self.sind(
                    s, lambda d=d: s.activation(
                        self.opbf[:, :], self.psF[:, 0:256], AF.Identity,
                        bias=self.boS[:, d:d + 1]))

            @blk.tensor
            def _(tn, d=d, slot=slot):
                tn.wait_ge(self.sS.h, tok[f"op_ev_{slot}_{d}"])
                wt = tok.get(f"op_add_{slot}_{d - 2}")
                if wt:
                    tn.wait_ge(self.vS.h, wt)
                for c in range(2):
                    ins = tn.matmul(self.psT[d % 2][:, 128 * c:128 * (c + 1)],
                                    self.opbf[:, 128 * c:128 * (c + 1)],
                                    self.identb[:, :], start=True, stop=True)
                ins.then_inc(self.tM.h, 1)
                self.tM.n += 1
                tok[f"op_tp_{slot}_{d}"] = self.tM.n

            @blk.vector
            def _(v, d=d, slot=slot):
                v.wait_ge(self.tM.h, tok[f"op_tp_{slot}_{d}"])
                if d == 0:
                    self.dmaw(v, tok[f"xown_{slot}_0"])
                    self.dmaw(v, tok[f"xown_{slot}_1"])
                for c in range(2):
                    self.vind(v, lambda d=d, c=c: v.tensor_tensor(
                        self.xb[c][:, 128 * d:128 * (d + 1)],
                        self.xb[c][:, 128 * d:128 * (d + 1)],
                        self.psT[d % 2][:, 128 * c:128 * (c + 1)], ALU.add))
                tok[f"op_add_{slot}_{d}"] = self.vS.n

        @blk.sync
        def _(sy, slot=slot):
            sy.wait_ge(self.vS.h, tok[f"op_add_{slot}_{DT - 1}"])
            for c in range(2):
                tok[f"x1st_{slot}_{c}"] = self.dma(
                    sy, self.d_X[0][256 * slot + 128 * c:
                                    256 * slot + 128 * (c + 1), :],
                    self.xb[c][:, :], self.sXR[c])
        self.mat_pump_section(blk, 6, nsg=4)

    def emit_ffn(self, blk, xin_d, xout_d, phase):
        tok = self.tok
        nc = self.nc
        pre = f"f{phase}"
        pfh = ExitStack()
        hT = pfh.enter_context(nc.sbuf_tensor(f"hT_{phase}",
                                              [128, NIT * 1024], BF16)).ap()
        pf1 = ExitStack()
        ln2T = pf1.enter_context(nc.sbuf_tensor(f"ln2T_{phase}",
                                                [128, HT * 1024], BF16)).ap()
        wgS = [pf1.enter_context(nc.sbuf_tensor(f"wgS{i}_{phase}",
                                                [128, HT * 128], BF16)).ap()
               for i in range(2)]
        wuS = [pf1.enter_context(nc.sbuf_tensor(f"wuS{i}_{phase}",
                                                [128, HT * 128], BF16)).ap()
               for i in range(2)]
        gsb = [pf1.enter_context(nc.sbuf_tensor(f"gsb{i}_{phase}",
                                                [128, 512], BF16)).ap()
               for i in range(2)]
        jt, js, jv = self.tM.n, self.sS.n, self.vS.n

        def xsrc(t):
            return xin_d[128 * t:128 * (t + 1), :]

        fw = [(self.tM, jt), (self.sS, js), (self.vS, jv)]
        if phase == 1:
            fw.append(tok["x1st_3_0"])
            fw.append(tok["x1st_3_1"])
            fw.extend(self.mat["stage2_guard"])
        else:
            fw.append((self.sFX, self.sFX.n))
        for t in range(8):
            self.emit_ln_tile(blk, pre, t, xsrc, ln2T, 1024,
                              fw if t < 2 else [])
            if phase == 1:
                self.mat_pump_section(blk, MAT_PUMP_FFN, nsg=6)
        ln_done = self.sS.n

        for it in range(NIT):
            pair = it % 2

            @blk.sync
            def _(sy, it=it, pair=pair):
                wt = tok.get(f"{pre}_gu_{it - 2}_1")
                if wt:
                    sy.wait_ge(self.tM.h, wt)
                if it == 0 and phase == 1:
                    for sv in self.mat_store_tok(2 * HT - 1):
                        self.dmaw(sy, sv)
                self.dma(sy, wgS[pair][:, :], self.d_Wg[it, :, :],
                         self.sWU[pair])
                tok[f"{pre}_wgu_{it}"] = self.dma(
                    sy, wuS[pair][:, :], self.d_Wu[it, :, :], self.sWU[pair])

            @blk.tensor
            def _(tn, it=it, pair=pair):
                self.dmaw(tn, tok[f"{pre}_wgu_{it}"])
                if it == 0:
                    tn.wait_ge(self.sS.h, ln_done)
                for q2 in range(2):
                    psg = (self.psC, self.psE)[q2]
                    psq = (self.psD, self.psF)[q2]
                    wt = tok.get(f"{pre}_sil_{it - 1}_{q2}")
                    if wt:
                        tn.wait_ge(self.sS.h, wt)
                    wt2 = tok.get(f"{pre}_hml_{it - 1}_{q2}")
                    if wt2:
                        tn.wait_ge(self.vS.h, wt2)
                    for ht in range(HT):
                        tn.matmul(psg[:, :],
                                  wgS[pair][:, 128 * ht:128 * (ht + 1)],
                                  ln2T[:, 1024 * ht + 512 * q2:
                                       1024 * ht + 512 * (q2 + 1)],
                                  start=(ht == 0), stop=(ht == HT - 1))
                    for ht in range(HT):
                        ins = tn.matmul(psq[:, :],
                                        wuS[pair][:, 128 * ht:128 * (ht + 1)],
                                        ln2T[:, 1024 * ht + 512 * q2:
                                             1024 * ht + 512 * (q2 + 1)],
                                        start=(ht == 0), stop=(ht == HT - 1))
                    ins.then_inc(self.tM.h, 1)
                    self.tM.n += 1
                    tok[f"{pre}_gu_{it}_{q2}"] = self.tM.n

            @blk.scalar
            def _(s, it=it):
                for q2 in range(2):
                    psg = (self.psC, self.psE)[q2]
                    s.wait_ge(self.tM.h, tok[f"{pre}_gu_{it}_{q2}"])
                    wt = tok.get(f"{pre}_hml_{it - 1}_{q2}")
                    if wt:
                        s.wait_ge(self.vS.h, wt)
                    tok[f"{pre}_sil_{it}_{q2}"] = self.sind(
                        s, lambda q2=q2, psg=psg: s.activation(
                            gsb[q2][:, :], psg[:, :], AF.Silu))

            @blk.vector
            def _(v, it=it):
                for q2 in range(2):
                    psq = (self.psD, self.psF)[q2]
                    v.wait_ge(self.sS.h, tok[f"{pre}_sil_{it}_{q2}"])
                    self.vind(v, lambda it=it, q2=q2, psq=psq: v.tensor_tensor(
                        hT[:, 1024 * it + 512 * q2:1024 * it + 512 * (q2 + 1)],
                        gsb[q2][:, :], psq[:, :], ALU.mult))
                    tok[f"{pre}_hml_{it}_{q2}"] = self.vS.n

            if phase == 1:
                self.mat_pump_section(blk, MAT_PUMP_FFN, nsg=4)

        if phase == 1:
            self.mat_drain(blk)
            self.mat_stack.close()
        gu_done_t = self.tM.n
        hml_done_v = self.vS.n
        pf1.close()

        pf2 = ExitStack()
        xst = [pf2.enter_context(nc.sbuf_tensor(f"xst{i}_{phase}",
                                                [128, D], F32, side="right")).ap()
               for i in range(8)]
        wdS = [pf2.enter_context(nc.sbuf_tensor(f"wdS{i}_{phase}",
                                                [128, NIT * 128], BF16)).ap()
               for i in range(2)]

        @blk.sync
        def _(sy):
            sy.wait_ge(self.tM.h, gu_done_t)
            sy.wait_ge(self.vS.h, hml_done_v)
            if phase == 1:
                for sv in self.mat_store_tok(len(self.mat_jobs) - 1):
                    self.dmaw(sy, sv)
            for t in range(8):
                self.dma(sy, xst[t][:, :], xin_d[128 * t:128 * (t + 1), :],
                         self.sXT)
            tok[f"{pre}_xst_ld"] = (self.sXT, self.sXT.n)

        for d in range(DT):
            pair = d % 2

            @blk.sync
            def _(sy, d=d, pair=pair):
                wt = tok.get(f"{pre}_dn_{d - 2}_1")
                if wt:
                    sy.wait_ge(self.tM.h, wt)
                tok[f"{pre}_wd_{d}"] = self.dma(
                    sy, wdS[pair][:, :], self.d_Wd[d, :, :], self.sWD[pair])

            @blk.tensor
            def _(tn, d=d, pair=pair):
                self.dmaw(tn, tok[f"{pre}_wd_{d}"])
                if d == 0:
                    tn.wait_ge(self.vS.h, tok[f"{pre}_tpev_7"])
                for q2 in range(2):
                    psd = (self.psA, self.psB)[q2]
                    wt = tok.get(f"{pre}_dnev_{d - 1}_{q2}")
                    if wt:
                        tn.wait_ge(self.sS.h, wt)
                    for it in range(NIT):
                        ins = tn.matmul(
                            psd[:, :], wdS[pair][:, 128 * it:128 * (it + 1)],
                            hT[:, 1024 * it + 512 * q2:1024 * it + 512 * (q2 + 1)],
                            start=(it == 0), stop=(it == NIT - 1))
                    ins.then_inc(self.tM.h, 1)
                    self.tM.n += 1
                    tok[f"{pre}_dn_{d}_{q2}"] = self.tM.n

            @blk.scalar
            def _(s, d=d):
                for q2 in range(2):
                    psd = (self.psA, self.psB)[q2]
                    s.wait_ge(self.tM.h, tok[f"{pre}_dn_{d}_{q2}"])
                    wt = tok.get(f"{pre}_dtp_{d - 1}_{q2}")
                    if wt:
                        s.wait_ge(self.tM.h, wt)
                    tok[f"{pre}_dnev_{d}_{q2}"] = self.sind(
                        s, lambda q2=q2, psd=psd: s.copy(
                            self.dnbf[q2][:, :], psd[:, :]))

            @blk.tensor
            def _(tn, d=d):
                for q2 in range(2):
                    tn.wait_ge(self.sS.h, tok[f"{pre}_dnev_{d}_{q2}"])
                    wt = tok.get(f"{pre}_dadd_{d - 1}_{q2}")
                    if wt:
                        tn.wait_ge(self.vS.h, wt)
                    for c in range(4):
                        ins = tn.matmul(
                            self.psT[q2][:, 128 * c:128 * (c + 1)],
                            self.dnbf[q2][:, 128 * c:128 * (c + 1)],
                            self.identb[:, :], start=True, stop=True)
                    ins.then_inc(self.tM.h, 1)
                    self.tM.n += 1
                    tok[f"{pre}_dtp_{d}_{q2}"] = self.tM.n

            @blk.vector
            def _(v, d=d):
                if d == 0:
                    self.dmaw(v, tok[f"{pre}_xst_ld"])
                for q2 in range(2):
                    v.wait_ge(self.tM.h, tok[f"{pre}_dtp_{d}_{q2}"])
                    for c in range(4):
                        t = 4 * q2 + c
                        self.vind(v, lambda t=t, d=d, q2=q2, c=c: v.tensor_tensor(
                            xst[t][:, 128 * d:128 * (d + 1)],
                            xst[t][:, 128 * d:128 * (d + 1)],
                            self.psT[q2][:, 128 * c:128 * (c + 1)], ALU.add))
                    tok[f"{pre}_dadd_{d}_{q2}"] = self.vS.n

        @blk.sync
        def _(sy):
            dst = xout_d if xout_d is not None else self.y_own
            for t in range(8):
                sy.wait_ge(self.vS.h, tok[f"{pre}_dadd_{DT - 1}_{t // 4}"])
                tok[f"{pre}_xst_{t}"] = self.dma(
                    sy, dst[128 * t:128 * (t + 1), :], xst[t][:, :], self.sFX)

        pf2.close()
        pfh.close()

    def emit_attn_b(self, blk):
        tok = self.tok
        nc = self.nc
        pre = "b"
        pb = ExitStack()
        ln3T = pb.enter_context(nc.sbuf_tensor("ln3T", [128, HT * 1024],
                                               BF16)).ap()
        wvbS = pb.enter_context(nc.sbuf_tensor("wvbS", [128, HT * 512],
                                               BF16)).ap()
        wpA = pb.enter_context(nc.sbuf_tensor("wpA", [128, KVH * D], BF16)).ap()
        wpB = pb.enter_context(nc.sbuf_tensor("wpB", [128, KVH * D], BF16)).ap()
        vbT = pb.enter_context(nc.sbuf_tensor("vbT", [128, KVH * 1024],
                                              BF16)).ap()
        xst = [pb.enter_context(nc.sbuf_tensor(f"bxst{i}", [128, D], F32,
                                               side="right")).ap()
               for i in range(8)]

        jt, js, jv = self.tM.n, self.sS.n, self.vS.n

        @blk.gpsimd
        def _(g):
            g.wait_ge(self.tM.h, jt)
            g.wait_ge(self.sS.h, js)
            g.wait_ge(self.vS.h, jv)
            self.dma(g, wvbS[:, :].rearrange("p (h f) -> p h f", f=512),
                     self.wvbT.rearrange("(h p) f -> p h f", p=128), self.sWS)
            tok["wvb_ld"] = (self.sWS, self.sWS.n)
            wp4 = self.wpT.rearrange("(kv j p) d -> j p kv d", kv=4, p=128)
            tok["wp_0"] = self.dma(
                g, wpA[:, :].rearrange("p (kv d) -> p kv d", d=D),
                wp4[0, :, :, :], self.sWS)

        for j in range(1, 4):
            @blk.gpsimd
            def _(g, j=j):
                if j > 1:
                    g.wait_ge(self.vS.h, tok[f"wp_add_{j - 1}"])
                tok[f"wp_l{j}"] = self.dma(
                    g, wpB[:, :].rearrange("p (kv d) -> p kv d", d=D),
                    self.wpT.rearrange("(kv j p) d -> j p kv d",
                                       kv=4, p=128)[j, :, :, :], self.sWS)

            @blk.vector
            def _(v, j=j):
                self.dmaw(v, tok[f"wp_l{j}"])
                tok[f"wp_add_{j}"] = self.vchain(v, lambda: v.tensor_tensor(
                    wpA[:, :], wpA[:, :], wpB[:, :], ALU.add))

        def xsrc(t):
            return self.d_X[1][128 * t:128 * (t + 1), :]

        fw = [(self.tM, jt), (self.sS, js), (self.vS, jv),
              (self.sFX, self.sFX.n)]
        for t in range(8):
            self.emit_ln_tile(blk, pre, t, xsrc, ln3T, 1024,
                              fw if t < 2 else [])
        ln_done = self.sS.n

        for f in range(4):
            @blk.tensor
            def _(tn, f=f):
                if f == 0:
                    tn.wait_ge(self.sS.h, ln_done)
                    self.dmaw(tn, (self.sWS, self.sWS.n))
                for q2 in range(2):
                    psd = (self.psC, self.psD)[q2]
                    wt = tok.get(f"vb_ev_{f - 1}_{q2}")
                    if wt:
                        tn.wait_ge(self.sS.h, wt)
                    for ht in range(HT):
                        ins = tn.matmul(psd[:, :],
                                        wvbS[:, 512 * ht + 128 * f:
                                             512 * ht + 128 * (f + 1)],
                                        ln3T[:, 1024 * ht + 512 * q2:
                                             1024 * ht + 512 * (q2 + 1)],
                                        start=(ht == 0), stop=(ht == HT - 1))
                    ins.then_inc(self.tM.h, 1)
                    self.tM.n += 1
                    tok[f"vb_mm_{f}_{q2}"] = self.tM.n

            @blk.scalar
            def _(s, f=f):
                for q2 in range(2):
                    psd = (self.psC, self.psD)[q2]
                    s.wait_ge(self.tM.h, tok[f"vb_mm_{f}_{q2}"])
                    dst = vbT[:, 1024 * f + 512 * q2:1024 * f + 512 * (q2 + 1)]
                    tok[f"vb_ev_{f}_{q2}"] = self.sind(
                        s, lambda dst=dst, psd=psd: s.copy(dst, psd[:, :]))

        @blk.sync
        def _(sy):
            sy.wait_ge(self.sS.h, tok["vb_ev_3_1"])
            for t in range(8):
                self.dma(sy, xst[t][:, :],
                         self.d_X[1][128 * t:128 * (t + 1), :], self.sXT)
            tok["b_xst_ld"] = (self.sXT, self.sXT.n)

        for d in range(DT):
            @blk.tensor
            def _(tn, d=d):
                if d == 0:
                    tn.wait_ge(self.vS.h, tok["wp_add_3"])
                    tn.wait_ge(self.sS.h, tok["vb_ev_3_1"])
                for q2 in range(2):
                    psd = (self.psA, self.psB)[q2]
                    wt = tok.get(f"b_dnev_{d - 1}_{q2}")
                    if wt:
                        tn.wait_ge(self.sS.h, wt)
                    for f in range(4):
                        ins = tn.matmul(psd[:, :],
                                        wpA[:, D * f + 128 * d:
                                            D * f + 128 * (d + 1)],
                                        vbT[:, 1024 * f + 512 * q2:
                                            1024 * f + 512 * (q2 + 1)],
                                        start=(f == 0), stop=(f == 3))
                    ins.then_inc(self.tM.h, 1)
                    self.tM.n += 1
                    tok[f"b_dn_{d}_{q2}"] = self.tM.n

            @blk.scalar
            def _(s, d=d):
                for q2 in range(2):
                    psd = (self.psA, self.psB)[q2]
                    s.wait_ge(self.tM.h, tok[f"b_dn_{d}_{q2}"])
                    wt = tok.get(f"b_dtp_{d - 1}_{q2}")
                    if wt:
                        s.wait_ge(self.tM.h, wt)
                    tok[f"b_dnev_{d}_{q2}"] = self.sind(
                        s, lambda q2=q2, psd=psd: s.copy(
                            self.dnbf[q2][:, :], psd[:, :]))

            @blk.tensor
            def _(tn, d=d):
                for q2 in range(2):
                    tn.wait_ge(self.sS.h, tok[f"b_dnev_{d}_{q2}"])
                    wt = tok.get(f"b_dadd_{d - 1}_{q2}")
                    if wt:
                        tn.wait_ge(self.vS.h, wt)
                    for c in range(4):
                        ins = tn.matmul(
                            self.psT[q2][:, 128 * c:128 * (c + 1)],
                            self.dnbf[q2][:, 128 * c:128 * (c + 1)],
                            self.identb[:, :], start=True, stop=True)
                    ins.then_inc(self.tM.h, 1)
                    self.tM.n += 1
                    tok[f"b_dtp_{d}_{q2}"] = self.tM.n

            @blk.vector
            def _(v, d=d):
                if d == 0:
                    self.dmaw(v, tok["b_xst_ld"])
                for q2 in range(2):
                    v.wait_ge(self.tM.h, tok[f"b_dtp_{d}_{q2}"])
                    for c in range(4):
                        t = 4 * q2 + c
                        self.vind(v, lambda t=t, d=d, q2=q2, c=c: v.tensor_tensor(
                            xst[t][:, 128 * d:128 * (d + 1)],
                            xst[t][:, 128 * d:128 * (d + 1)],
                            self.psT[q2][:, 128 * c:128 * (c + 1)], ALU.add))
                    tok[f"b_dadd_{d}_{q2}"] = self.vS.n

        @blk.sync
        def _(sy):
            for t in range(8):
                sy.wait_ge(self.vS.h, tok[f"b_dadd_{DT - 1}_{t // 4}"])
                tok[f"b_xst_{t}"] = self.dma(
                    sy, self.d_X[2][128 * t:128 * (t + 1), :], xst[t][:, :],
                    self.sFX)

        pb.close()


_prog_cache = {}


def _get_prog():
    if "nc" not in _prog_cache:
        b = Builder()
        _prog_cache["nc"] = b.build()
    return _prog_cache["nc"]


def kernel(L0_attn_a_hidden_states, L0_attn_a_Wq, L0_attn_a_Wk, L0_attn_a_Wv,
           L0_attn_a_Wo, L0_attn_a_bo, L0_attn_b_weight_qkv,
           L0_attn_b_weight_proj, L0_ffn_lut, L0_ffn_gate_idx, L0_ffn_up_idx,
           L0_ffn_down_idx):
    f32 = np.float32
    x = np.ascontiguousarray(np.asarray(L0_attn_a_hidden_states, f32)
                             .reshape(SEQ, D))
    wqT = np.ascontiguousarray(np.asarray(L0_attn_a_Wq, f32).T)
    wkT = np.ascontiguousarray(np.asarray(L0_attn_a_Wk, f32).T)
    wvT = np.ascontiguousarray(np.asarray(L0_attn_a_Wv, f32).T)
    wo = np.asarray(L0_attn_a_Wo, f32)
    woTr = np.ascontiguousarray(
        wo.reshape(DT, 128, NH, 128).transpose(0, 3, 2, 1)).reshape(
            DT, 128, NH * 128)
    boP = np.ascontiguousarray(np.asarray(L0_attn_a_bo, f32)
                               .reshape(DT, 128).T)
    qkv = np.asarray(L0_attn_b_weight_qkv, f32)
    wvb = np.concatenate([qkv[768 * g + 640:768 * (g + 1)] for g in range(4)],
                         axis=0)
    wvbT = np.ascontiguousarray(wvb.T)
    wpT = np.ascontiguousarray(np.asarray(L0_attn_b_weight_proj, f32).T)
    lutp = np.ascontiguousarray(
        np.broadcast_to(np.asarray(L0_ffn_lut, f32), (128, 16)))
    gidxT = np.ascontiguousarray(np.asarray(L0_ffn_gate_idx, np.int32).T)
    uidxT = np.ascontiguousarray(np.asarray(L0_ffn_up_idx, np.int32).T)
    didxT = np.ascontiguousarray(np.asarray(L0_ffn_down_idx, np.int32).T)
    ident = np.eye(128, dtype=f32)

    pp = np.arange(128)[:, None]
    cq = np.arange(BLK)[None, :]
    cmasks = []
    for c in range(NCORES):
        cm = np.zeros((NSLOT, 128, 16 * BLK), f32)
        for k in range(NSLOT):
            for m in range(16):
                valid = (m * 128 + pp) <= (2 * c * 128 + cq)
                cm[k, :, BLK * m:BLK * (m + 1)] = valid.astype(f32)
        cmasks.append(cm)

    nc = _get_prog()
    in_maps = []
    for c in range(NCORES):
        rows = np.concatenate([np.arange(256 * (8 * k + c), 256 * (8 * k + c) + 256)
                               for k in range(NSLOT)])
        in_maps.append(dict(
            x_full=x, x_own=np.ascontiguousarray(x[rows]), wqT=wqT, wkT=wkT,
            wvT=wvT, woTr=woTr, boP=boP, wvbT=wvbT, wpT=wpT, lutp=lutp,
            gidxT=gidxT, uidxT=uidxT, didxT=didxT, cmask=cmasks[c],
            ident=ident))
    res = run_bass_kernel_spmd(nc, in_maps, core_ids=list(range(NCORES)))
    _prog_cache["taps"] = res.results
    y = np.empty((SEQ, D), f32)
    for c in range(NCORES):
        yo = np.asarray(res.results[c]["y_own"])
        for k in range(NSLOT):
            b = 8 * k + c
            y[256 * b:256 * (b + 1)] = yo[256 * k:256 * (k + 1)]
    return y.reshape(1, SEQ, D)


if __name__ == "__main__":
    b = Builder()
    nc = b.build()
    print("program built OK")


# version 22
# speedup vs baseline: 1.0556x; 1.0002x over previous
import numpy as np
from contextlib import ExitStack

import concourse.bass as bass
import concourse.mybir as mybir
from concourse.bass_utils import run_bass_kernel_spmd

F32 = mybir.dt.float32
BF16 = mybir.dt.bfloat16
I32 = mybir.dt.int32
AF = mybir.ActivationFunctionType
ALU = mybir.AluOpType
AX = mybir.AxisListType

SEQ = 8192
D = 2048
DEBUG_TAPS = False
NH = 16
KVH = 4
INTER = 5504
NIT = 43
HT = 16
DT = 16
BLK = 256
NSLOT = 4
NCORES = 8
OWN = NSLOT * BLK
EPS = 1e-5
NSUP = SEQ // 512
NSUPQ = OWN // 512

MAT_PUMP_P1 = 6
MAT_SG_P1 = 5
MAT_PUMP_P2 = (3, 5, 8, 11)
MAT_SG_P2 = 3
MAT_PUMP_FFN = 6


class Sem:
    def __init__(self, nc, es, name):
        self.h = es.enter_context(nc.semaphore(name))
        self.n = 0


class Builder:
    def __init__(self):
        self.nc = bass.Bass("TRN2", target_bir_lowering=False)
        nc = self.nc
        self.es = ExitStack()
        self.tok = {}

        def din(name, shape, dt=F32):
            return nc.dram_tensor(name, shape, dt, kind="ExternalInput").ap()

        self.x_full = din("x_full", [SEQ, D])
        self.x_own = din("x_own", [OWN, D])
        self.wqT = din("wqT", [D, D])
        self.wkT = din("wkT", [D, 512])
        self.wvT = din("wvT", [D, 512])
        self.woTr = din("woTr", [DT, 128, NH * 128])
        self.boP = din("boP", [128, DT])
        self.wvbT = din("wvbT", [D, 512])
        self.wpT = din("wpT", [D, D])
        self.lutp = din("lutp", [128, 16])
        self.gidxT = din("gidxT", [D, INTER], I32)
        self.uidxT = din("uidxT", [D, INTER], I32)
        self.didxT = din("didxT", [INTER, D], I32)
        self.cmask = din("cmask", [NSLOT, 128, 16 * BLK])
        self.ident = din("ident", [128, 128])
        self.y_own = nc.dram_tensor("y_own", [OWN, D], F32,
                                    kind="ExternalOutput").ap()
        if DEBUG_TAPS:
            self.t_x1 = nc.dram_tensor("t_x1", [OWN, D], F32,
                                       kind="ExternalOutput").ap()
            self.t_x2 = nc.dram_tensor("t_x2", [OWN, D], F32,
                                       kind="ExternalOutput").ap()
            self.t_x3 = nc.dram_tensor("t_x3", [OWN, D], F32,
                                       kind="ExternalOutput").ap()
            self.t_wg = nc.dram_tensor("t_wg", [128, HT * 128], F32,
                                       kind="ExternalOutput").ap()
            self.t_wd = nc.dram_tensor("t_wd", [128, NIT * 128], F32,
                                       kind="ExternalOutput").ap()
            self.t_wu = nc.dram_tensor("t_wu", [128, HT * 128], F32,
                                       kind="ExternalOutput").ap()
            self.t_k = nc.dram_tensor("t_k", [128, 1024], F32,
                                      kind="ExternalOutput").ap()
            self.t_v = nc.dram_tensor("t_v", [128, 1024], F32,
                                      kind="ExternalOutput").ap()
            self.t_q = nc.dram_tensor("t_q", [128, 1024], F32,
                                      kind="ExternalOutput").ap()

        self.d_K = nc.dram_tensor("d_K", [KVH, 128, SEQ], BF16, kind="Internal").ap()
        self.d_V = nc.dram_tensor("d_V", [KVH, 128, SEQ], BF16,
                                  kind="Internal").ap()
        self.d_Q = nc.dram_tensor("d_Q", [NH, 128, 1024], BF16,
                                  kind="Internal").ap()
        self.d_Wg = nc.dram_tensor("d_Wg", [NIT, 128, HT * 128], BF16,
                                   kind="Internal").ap()
        self.d_Wu = nc.dram_tensor("d_Wu", [NIT, 128, HT * 128], BF16,
                                   kind="Internal").ap()
        self.d_Wd = nc.dram_tensor("d_Wd", [DT, 128, NIT * 128], BF16,
                                   kind="Internal").ap()
        self.d_X = [nc.dram_tensor(f"d_X{i}", [OWN, D], F32, kind="Internal").ap()
                    for i in (1, 2, 3)]

        def mk(name):
            return Sem(nc, self.es, name)

        def mk2(name):
            return [Sem(nc, self.es, name + "0"), Sem(nc, self.es, name + "1")]

        self.sWS = mk("sWS")
        self.sWC = mk("sWC")
        self.sX = mk2("sX")
        self.sKV = mk2("sKV")
        self.sQ2 = mk2("sQ2")
        self.sWU = mk2("sWU")
        self.sWD = mk2("sWD")
        self.sWO = mk2("sWO")
        self.sXT = mk("sXT")
        self.sG = mk2("sG")
        self.sSg = mk2("sSg")
        self.sST = mk2("sST")
        self.sXR = mk2("sXR")
        self.sFX = mk("sFX")
        self.dM2 = mk2("dM")
        self.dMs = mk2("dMs")
        self.vS = Sem(nc, self.es, "vS")
        self.sS = Sem(nc, self.es, "sS")
        self.gS = Sem(nc, self.es, "gS")
        self.tM = Sem(nc, self.es, "tM")

        def sb(name, shape, dt):
            return nc.alloc_sbuf_tensor(name, shape, dt).ap()

        self.lutP = sb("lutP", [128, 16], F32)
        self.identb = sb("identb", [128, 128], BF16)
        self.onesb = sb("onesb", [128, 1], BF16)
        self.ones1f = sb("ones1f", [1, 128], F32)
        self.epsA = sb("epsA", [128, 1], F32)
        self.negone = sb("negone", [128, 1], F32)
        self.st6 = sb("st6", [128, 24], F32)
        self.mv2 = sb("mv2", [128, 2], F32)
        self.veps = sb("veps", [128, 1], F32)
        self.ivar4 = sb("ivar4", [128, 4], F32)
        self.negm4 = sb("negm4", [128, 4], F32)
        self.rstd4 = sb("rstd4", [128, 4], F32)
        self.lnb4 = sb("lnb4", [128, 4], F32)
        self.xb = [sb(f"xb{i}", [128, D], F32) for i in range(2)]
        self.lnt = [sb(f"lnt{i}", [128, D], BF16) for i in range(4)]
        self.ksum = sb("ksum", [128, 1], F32)
        self.negmean = sb("negmean", [128, 1], F32)
        self.rL = sb("rL", [1, 256], F32)
        self.jneg = sb("jneg", [128, 16], F32)
        self.onefF = sb("onefF", [128, 1], F32)
        self.dnbf = [sb(f"dnbf{i}", [128, 512], BF16) for i in range(2)]

        self.mat_stack = ExitStack()
        self.midx = [self.mat_stack.enter_context(
            nc.sbuf_tensor(f"midx{i}", [128, INTER], BF16, side="right")).ap()
            for i in range(2)]
        self.macc = [self.mat_stack.enter_context(
            nc.sbuf_tensor(f"macc{i}", [128, INTER], BF16, side="right")).ap()
            for i in range(2)]
        self.mtj = self.mat_stack.enter_context(
            nc.sbuf_tensor("mtj", [128, INTER], BF16, side="right")).ap()
        SGW = 1408
        self.SGW = SGW
        self.msq = self.mat_stack.enter_context(
            nc.sbuf_tensor("msq", [128, SGW], BF16, side="right")).ap()
        self.msq2 = self.mat_stack.enter_context(
            nc.sbuf_tensor("msq2", [128, SGW], BF16, side="right")).ap()
        self.mind = self.mat_stack.enter_context(
            nc.sbuf_tensor("mind", [128, 2 * SGW], BF16, side="right")).ap()
        self.midx_sg = self.mat_stack.enter_context(
            nc.sbuf_tensor("midx_sg", [128, 2 * SGW], BF16, side="right")).ap()
        self.macc_sg = self.mat_stack.enter_context(
            nc.sbuf_tensor("macc_sg", [128, 2 * SGW], BF16, side="right")).ap()

        def psu(name, shape, dt=F32):
            return nc.alloc_psum_tensor(name, shape, dt).ap()

        self.psA = psu("psA", [128, 512])
        self.psB = psu("psB", [128, 512])
        self.psC = psu("psC", [128, 512])
        self.psD = psu("psD", [128, 512])
        self.psE = psu("psE", [128, 512])
        self.psF = psu("psF", [128, 512])
        self.psL = self.psF[0:1, 0:256]
        self.psT = [psu(f"psT{i}", [128, 512]) for i in range(2)]

        self.mat_jobs = ([("g", i) for i in range(HT)]
                         + [("u", i) for i in range(HT)]
                         + [("d", i) for i in range(NIT)])
        SG_JOBS = {30, 31, 71, 72, 73, 74}
        self.vg_jobs = set()
        self.mat = dict(loads=0, load_tok={}, store_tok={}, done={},
                        cap=len(self.mat_jobs), sg_jobs=SG_JOBS,
                        vq=[i for i in range(len(self.mat_jobs))
                            if i not in SG_JOBS],
                        vqi=0, vj=0, buf_of={}, vg_tok={},
                        squnits=[], squi=0, sj=0, sg_ltok={}, sg_sttok={},
                        sg_stok={})
        for ji in sorted(SG_JOBS):
            kind, idx = self.mat_jobs[ji]
            W = INTER if kind in ("g", "u") else D
            qs_ = []
            c = 0
            while c < W:
                w = min(1408, W - c)
                qs_.append((ji, c, w))
                c += w
            self.mat["squnits"].extend(qs_)

    def dma(self, q, out, in_, sem):
        q.dma_start(out=out, in_=in_).then_inc(sem.h, 16)
        sem.n += 16
        return (sem, sem.n)

    def dmaw(self, eng, tokv):
        eng.wait_ge(tokv[0].h, tokv[1])

    def store_sems(self):
        return [self.sST[0], self.sST[1], self.sXR[0], self.sXR[1], self.sFX,
                self.dM2[0], self.dM2[1], self.dMs[0], self.dMs[1]]

    def store_snapshot(self):
        return [(s, s.n) for s in self.store_sems() if s.n > 0]

    def vchain(self, v, emit):
        v.wait_ge(self.vS.h, self.vS.n)
        emit().then_inc(self.vS.h, 1)
        self.vS.n += 1
        return self.vS.n

    def vind(self, v, emit):
        emit().then_inc(self.vS.h, 1)
        self.vS.n += 1
        return self.vS.n

    def schain(self, s, emit):
        s.wait_ge(self.sS.h, self.sS.n)
        emit().then_inc(self.sS.h, 1)
        self.sS.n += 1
        return self.sS.n

    def sind(self, s, emit):
        emit().then_inc(self.sS.h, 1)
        self.sS.n += 1
        return self.sS.n

    def mat_src(self, kind, idx):
        if kind == "g":
            dst = self.d_Wg[:, :, 128 * idx:128 * (idx + 1)].rearrange(
                "it p f -> p it f")
            return (self.gidxT[128 * idx:128 * (idx + 1), :], INTER, dst)
        if kind == "u":
            dst = self.d_Wu[:, :, 128 * idx:128 * (idx + 1)].rearrange(
                "it p f -> p it f")
            return (self.uidxT[128 * idx:128 * (idx + 1), :], INTER, dst)
        dst = self.d_Wd[:, :, 128 * idx:128 * (idx + 1)].rearrange(
            "d p f -> p d f")
        return (self.didxT[128 * idx:128 * (idx + 1), :], D, dst)

    def mat_gpsimd_service(self, g):
        m = self.mat
        for ji in sorted(m["done"].keys()):
            if ji in m["store_tok"]:
                continue
            kind, idx = self.mat_jobs[ji]
            _, w, dst = self.mat_src(kind, idx)
            sem, val = m["done"][ji]
            g.wait_ge(sem.h, val)
            bp = m["buf_of"][ji] % 2
            m["store_tok"][ji] = self.dma(
                g, dst, self.macc[bp][:, 0:w].rearrange("p (a f) -> p a f",
                                                        f=128), self.dM2[bp])
        while m["loads"] < m["cap"]:
            ji = m["vq"][m["loads"]] if m["loads"] < len(m["vq"]) else None
            if ji is None:
                break
            prev2 = m["vq"][m["loads"] - 2] if m["loads"] >= 2 else None
            prev4 = m["vq"][m["loads"] - 4] if m["loads"] >= 4 else None
            if prev2 is not None and prev2 not in m["done"]:
                break
            if prev4 is not None and prev4 not in m["store_tok"]:
                break
            kind, idx = self.mat_jobs[ji]
            srcap, w, _ = self.mat_src(kind, idx)
            if prev2 is not None:
                sem, val = m["done"][prev2]
                g.wait_ge(sem.h, val)
            if m.get("stage2_guard") and not m.get("stage2_guard_done"):
                for sv in m["stage2_guard"]:
                    self.dmaw(g, sv)
                m["stage2_guard_done"] = True
            li = m["loads"]
            m["load_tok"][ji] = self.dma(g, self.midx[li % 2][:, 0:w], srcap,
                                         self.sG[li % 2])
            m["buf_of"][ji] = li % 2
            m["loads"] += 1

    def mat_pump_drv(self, blk, nj):
        m = self.mat
        left = nj
        while left > 0 and m["vqi"] < len(m["vq"]):
            ji = m["vq"][m["vqi"]]
            if ji not in m["load_tok"]:
                break
            kind, idx = self.mat_jobs[ji]
            _, w, _ = self.mat_src(kind, idx)
            b = m["buf_of"][ji]
            hw = w // 2
            vg = ji in self.vg_jobs
            j0 = m["vj"]
            j1 = min(16, j0 + min(left, 1 if vg else left))
            left -= (j1 - j0)

            @blk.vector
            def _(v, ji=ji, j0=j0, j1=j1, b=b, hw=hw, vg=vg, w=w):
                if j0 == 0:
                    self.dmaw(v, m["load_tok"][ji])
                    if "consts" in self.tok:
                        self.dmaw(v, self.tok["consts"])
                    prev = m["vq"][m["vqi"] - 1] if m["vqi"] >= 1 else None
                    pd = m["done"].get(prev) if prev is not None else None
                    if pd is not None and pd[0] is self.gS:
                        v.wait_ge(self.gS.h, pd[1])
                    if not vg:
                        prev2 = (m["vq"][m["vqi"] - 2]
                                 if m["vqi"] >= 2 else None)
                        stok = (m["store_tok"].get(prev2)
                                if prev2 is not None else None)
                        if stok is not None:
                            self.dmaw(v, stok)
                        self.vchain(v, lambda: v.memset(self.macc[b][:, 0:w],
                                                        0.0))
                for j in range(j0, j1):
                    for hf in range(2):
                        sl = slice(hw * hf, hw * (hf + 1))
                        if vg:
                            gtok = m["vg_tok"].get(("g", ji, 2 * j + hf - 2))
                            if gtok:
                                v.wait_ge(self.gS.h, gtok)
                            ms = self.mtj[:, hw * ((2 * j + hf) % 2):
                                          hw * ((2 * j + hf) % 2) + hw]
                            self.vchain(v, lambda j=j, sl=sl, ms=ms:
                                        v.tensor_scalar(
                                            ms, self.midx[b][:, sl], float(j),
                                            self.lutP[:, j:j + 1],
                                            ALU.is_equal, ALU.mult))
                            m["vg_tok"][("v", ji, 2 * j + hf)] = self.vS.n
                        else:
                            self.vchain(v, lambda j=j, sl=sl: v.tensor_scalar(
                                self.mtj[:, 0:hw], self.midx[b][:, sl],
                                float(j), self.lutP[:, j:j + 1],
                                ALU.is_equal, ALU.mult))
                            self.vchain(v, lambda sl=sl: v.tensor_tensor(
                                self.macc[b][:, sl], self.macc[b][:, sl],
                                self.mtj[:, 0:hw], ALU.add))

            if vg:
                @blk.gpsimd
                def _(g, ji=ji, j0=j0, j1=j1, b=b, hw=hw, w=w):
                    if j0 == 0:
                        self.dmaw(g, m["load_tok"][ji])
                        prev2 = m["vq"][m["vqi"] - 2] if m["vqi"] >= 2 else None
                        stok = (m["store_tok"].get(prev2)
                                if prev2 is not None else None)
                        if stok is not None:
                            self.dmaw(g, stok)
                    for j in range(j0, j1):
                        for hf in range(2):
                            sl = slice(hw * hf, hw * (hf + 1))
                            g.wait_ge(self.vS.h, m["vg_tok"][("v", ji, 2 * j + hf)])
                            g.wait_ge(self.gS.h, self.gS.n)
                            ms = self.mtj[:, hw * ((2 * j + hf) % 2):
                                          hw * ((2 * j + hf) % 2) + hw]
                            if j == 0:
                                g.tensor_copy(self.macc[b][:, sl],
                                              ms).then_inc(self.gS.h, 1)
                            else:
                                g.tensor_tensor(self.macc[b][:, sl],
                                                self.macc[b][:, sl], ms,
                                                ALU.add).then_inc(self.gS.h, 1)
                            self.gS.n += 1
                            m["vg_tok"][("g", ji, 2 * j + hf)] = self.gS.n

            m["vj"] = j1
            if j1 == 16:
                if vg:
                    m["done"][ji] = (self.gS, self.gS.n)
                else:
                    m["done"][ji] = (self.vS, self.vS.n)
                m["vqi"] += 1
                m["vj"] = 0

    def mat_pump_sg(self, blk, nj):
        m = self.mat
        left = nj
        while left > 0:
            if m["squi"] >= len(m["squnits"]):
                return
            ui = m["squi"]
            ji, c0, w = m["squnits"][ui]
            kind, idx = self.mat_jobs[ji]
            srcap, _, dstap = self.mat_src(kind, idx)
            slot = ui % 2
            if ui not in m["sg_ltok"]:
                @blk.gpsimd
                def _(g, ui=ui):
                    for u2 in (ui, ui + 1):
                        if u2 >= len(m["squnits"]) or u2 in m["sg_ltok"]:
                            continue
                        ji2, c2, w2 = m["squnits"][u2]
                        k2, i2 = self.mat_jobs[ji2]
                        src2, _, _ = self.mat_src(k2, i2)
                        st = m["sg_sttok"].get(u2 - 2)
                        if st is not None:
                            self.dmaw(g, st)
                        ra = m["sg_sttok"].get("realloc")
                        if ra is not None and not m.get(f"sg_ra_done_{u2 % 2}"):
                            for sv in ra:
                                self.dmaw(g, sv)
                            m[f"sg_ra_done_{u2 % 2}"] = True
                        m["sg_ltok"][u2] = self.dma(
                            g, self.midx_sg[:, self.SGW * (u2 % 2):
                                            self.SGW * (u2 % 2) + w2],
                            src2[:, c2:c2 + w2], self.sSg[u2 % 2])
            j0 = m["sj"]
            j1 = min(16, j0 + min(left, 2))
            left -= (j1 - j0)

            @blk.scalar
            def _(s, ui=ui, ji=ji, j0=j0, j1=j1, w=w, slot=slot):
                for j in range(j0, j1):
                    if j == 0 and j0 == 0:
                        self.dmaw(s, m["sg_ltok"][ui])
                        s.wait_ge(self.vS.h, self.tok["consts_v"])
                        self.dmaw(s, self.tok["consts"])
                        s.wait_ge(self.gS.h, self.gS.n)
                    gtok = m["sg_stok"].get(("g", ui, j - 2))
                    if gtok:
                        s.wait_ge(self.gS.h, gtok)
                    mi = self.midx_sg[:, self.SGW * slot:self.SGW * slot + w]
                    self.schain(s, lambda j=j, mi=mi: s.activation(
                        self.msq[:, 0:w], mi, AF.Square,
                        bias=self.jneg[:, j:j + 1]))
                    self.schain(s, lambda: s.activation(
                        self.msq2[:, 0:w], self.msq[:, 0:w],
                        AF.Relu, bias=self.onefF[:, :], scale=-1.0))
                    self.schain(s, lambda j=j: s.activation(
                        self.mind[:, self.SGW * (j % 2):self.SGW * (j % 2) + w],
                        self.msq2[:, 0:w], AF.Identity,
                        scale=self.lutP[:, j:j + 1]))
                    m["sg_stok"][("s", ui, j)] = self.sS.n

            @blk.gpsimd
            def _(g, ui=ui, j0=j0, j1=j1, w=w, slot=slot):
                if j0 == 0:
                    self.dmaw(g, m["sg_ltok"][ui])
                    st = m["sg_sttok"].get(ui - 2)
                    if st is not None:
                        self.dmaw(g, st)
                    g.wait_ge(self.gS.h, self.gS.n)
                    g.memset(self.macc_sg[:, self.SGW * slot:
                                          self.SGW * slot + w],
                             0.0).then_inc(self.gS.h, 1)
                    self.gS.n += 1
                for j in range(j0, j1):
                    g.wait_ge(self.sS.h, m["sg_stok"][("s", ui, j)])
                    g.wait_ge(self.gS.h, self.gS.n)
                    g.tensor_tensor(
                        self.macc_sg[:, self.SGW * slot:self.SGW * slot + w],
                        self.macc_sg[:, self.SGW * slot:self.SGW * slot + w],
                        self.mind[:, self.SGW * (j % 2):self.SGW * (j % 2) + w],
                        ALU.add).then_inc(self.gS.h, 1)
                    self.gS.n += 1
                    m["sg_stok"][("g", ui, j)] = self.gS.n
                if j1 == 16:
                    g.wait_ge(self.gS.h, self.gS.n)
                    dst = dstap[:, c0 // 128:(c0 + w) // 128, :] \
                        if False else None
                    kindq, idxq = self.mat_jobs[ji]
                    _, _, dfull = self.mat_src(kindq, idxq)
                    m["sg_sttok"][ui] = self.dma(
                        g, dfull[:, c0 // 128:(c0 + w) // 128, :],
                        self.macc_sg[:, self.SGW * slot:self.SGW * slot + w
                                     ].rearrange("p (a f) -> p a f", f=128),
                        self.dMs[ui % 2])
                    if ui + 1 >= len(m["squnits"]) or \
                            m["squnits"][ui + 1][0] != ji:
                        m["store_tok"][ji] = m["sg_sttok"][ui]

            m["sj"] = j1
            if j1 == 16:
                m["squi"] += 1
                m["sj"] = 0

    def mat_pump_section(self, blk, nj, nsg=0):
        @blk.gpsimd
        def _(g):
            self.mat_gpsimd_service(g)

        if nsg:
            self.mat_pump_sg(blk, nsg)
        self.mat_pump_drv(blk, nj)

    def mat_drain(self, blk, upto=None):
        if upto is None:
            upto = len(self.mat_jobs)
        guard = 0
        while any(ji not in self.mat["store_tok"] for ji in range(upto)):
            self.mat_pump_section(blk, 48, nsg=16)
            guard += 1
            assert guard < 900, "mat drain stuck"

    def mat_stage2(self, blk):
        m = self.mat
        self.mat["cap"] = max(self.mat["loads"], 1)
        guard = 0
        gu_jobs = [ji for ji in range(2 * HT)]
        loaded = set(list(m["load_tok"].keys()))
        sg_started = set(u[0] for i, u in enumerate(m["squnits"])
                         if i <= m["squi"] or i in m["sg_ltok"])
        need = set(gu_jobs) | loaded | sg_started
        while any(ji not in m["store_tok"] for ji in need):
            self.mat_pump_section(blk, 48, nsg=16)
            loaded = set(list(m["load_tok"].keys()))
            sg_started = set(u[0] for i, u in enumerate(m["squnits"])
                             if i <= m["squi"] or i in m["sg_ltok"])
            need = set(gu_jobs) | loaded | sg_started
            guard += 1
            assert guard < 900, "mat stage2 drain stuck"
        m["stage2_guard"] = [(self.dM2[0], self.dM2[0].n),
                             (self.dM2[1], self.dM2[1].n),
                             (self.dMs[0], self.dMs[0].n),
                             (self.dMs[1], self.dMs[1].n)]
        m["stage2_guard_done"] = False
        self.mat_stack.close()
        nc = self.nc
        self.mat_stack = ExitStack()
        self.midx = [self.mat_stack.enter_context(
            nc.sbuf_tensor(f"midxB{i}", [128, D], BF16, side="right")).ap()
            for i in range(2)]
        self.macc = [self.mat_stack.enter_context(
            nc.sbuf_tensor(f"maccB{i}", [128, D], BF16, side="right")).ap()
            for i in range(2)]
        self.mtj = self.mat_stack.enter_context(
            nc.sbuf_tensor("mtjB", [128, D], BF16, side="right")).ap()
        SGW = 512
        self.SGW = SGW
        self.msq = self.mat_stack.enter_context(
            nc.sbuf_tensor("msqB", [128, SGW], BF16, side="right")).ap()
        self.msq2 = self.mat_stack.enter_context(
            nc.sbuf_tensor("msq2B", [128, SGW], BF16, side="right")).ap()
        self.mind = self.mat_stack.enter_context(
            nc.sbuf_tensor("mindB", [128, 2 * SGW], BF16, side="right")).ap()
        self.midx_sg = self.mat_stack.enter_context(
            nc.sbuf_tensor("midx_sgB", [128, 2 * SGW], BF16, side="right")).ap()
        self.macc_sg = self.mat_stack.enter_context(
            nc.sbuf_tensor("macc_sgB", [128, 2 * SGW], BF16, side="right")).ap()
        self.mat["cap"] = len(self.mat_jobs)

    def mat_store_tok(self, last_job):
        best = {}
        for ji in range(last_job + 1):
            t = self.mat["store_tok"].get(ji)
            assert t is not None, "mat job not stored before use"
            sem, val = t
            best[id(sem)] = (sem, max(val, best.get(id(sem), (sem, 0))[1]))
        return list(best.values())

    def ln_vector(self, v, xin, col, extra_waits=()):
        for sem, val in extra_waits:
            if val:
                v.wait_ge(sem.h, val)
        self.vchain(v, lambda: v.bn_stats(self.st6[:, 0:6], xin[:, 0:512]))
        for q in range(1, 4):
            self.vind(v, lambda q=q: v.bn_stats(
                self.st6[:, 6 * q:6 * q + 6], xin[:, 512 * q:512 * (q + 1)]))
        self.vchain(v, lambda: v.bn_aggr(self.mv2[:, :], self.st6[:, :]))
        self.vchain(v, lambda: v.tensor_scalar(
            self.ivar4[:, col:col + 1], self.mv2[:, 1:2], self.epsA[:, :],
            None, ALU.add))
        return self.vchain(v, lambda: v.tensor_scalar(
            self.negm4[:, col:col + 1], self.mv2[:, 0:1], self.negone[:, :],
            None, ALU.mult))

    def ln_scalar(self, s, xin, lnout, col, vtok, extra_waits=()):
        s.wait_ge(self.vS.h, vtok)
        for sem, val in extra_waits:
            if val:
                s.wait_ge(sem.h, val)
        self.schain(s, lambda: s.activation(
            self.veps[:, col % 1:col % 1 + 1], self.ivar4[:, col:col + 1], AF.Ln))
        self.schain(s, lambda: s.activation(
            self.rstd4[:, col:col + 1], self.veps[:, 0:1], AF.Exp, scale=-0.5))
        self.schain(s, lambda: s.activation(
            self.lnb4[:, col:col + 1], self.negm4[:, col:col + 1], AF.Identity,
            scale=self.rstd4[:, col:col + 1]))
        return self.schain(s, lambda: s.activation(
            lnout[:, :], xin[:, :], AF.Identity,
            bias=self.lnb4[:, col:col + 1], scale=self.rstd4[:, col:col + 1]))

    def emit_ln_tile(self, blk, pre, t, xsrc_fn, dstT, dst_stride, first_waits):
        tok = self.tok

        @blk.sync
        def _(sy, t=t):
            wt = tok.get(f"{pre}_lnap_{t - 2}")
            if wt:
                sy.wait_ge(self.sS.h, wt)
            for sem, val in first_waits:
                sy.wait_ge(sem.h, val)
            tok[f"{pre}_xl_{t}"] = self.dma(sy, self.xb[t % 2][:, :],
                                            xsrc_fn(t), self.sX[t % 2])

        @blk.vector
        def _(v, t=t):
            waits = [tok[f"{pre}_xl_{t}"],
                     (self.sS, tok.get(f"{pre}_lnap_{t - 4}", 0))]
            tok[f"{pre}_lnv_{t}"] = self.ln_vector(v, self.xb[t % 2], t % 4,
                                                   extra_waits=waits)

        @blk.scalar
        def _(s, t=t):
            waits = [(self.tM, tok.get(f"{pre}_tp_{t - 4}", 0))]
            tok[f"{pre}_lnap_{t}"] = self.ln_scalar(
                s, self.xb[t % 2], self.lnt[t % 4], t % 4,
                tok[f"{pre}_lnv_{t}"], extra_waits=waits)

        t_off = t % 4 if dst_stride == 512 else t
        tb = [self.psT[0], self.psT[1], self.psA, self.psB]

        @blk.tensor
        def _(tn, t=t):
            tn.wait_ge(self.sS.h, tok[f"{pre}_lnap_{t}"])
            if t == 0:
                self.dmaw(tn, tok["identb"])
            for gb in range(4):
                wt = tok.get(f"{pre}_tpev_prev_{gb}")
                if wt:
                    tn.wait_ge(self.vS.h, wt)
                for k in range(4):
                    h = 4 * gb + k
                    ins = tn.matmul(tb[gb][:, 128 * k:128 * (k + 1)],
                                    self.lnt[t % 4][:, 128 * h:128 * (h + 1)],
                                    self.identb[:, :], start=True, stop=True)
                ins.then_inc(self.tM.h, 1)
                self.tM.n += 1
                tok[f"{pre}_tp_{t}_{gb}"] = self.tM.n

        @blk.vector
        def _(v, t=t, t_off=t_off):
            for gb in range(4):
                v.wait_ge(self.tM.h, tok[f"{pre}_tp_{t}_{gb}"])
                dst = dstT[:, :].rearrange("p (h f) -> p h f", f=dst_stride)[
                    :, 4 * gb:4 * gb + 4, 128 * t_off:128 * (t_off + 1)]
                tok[f"{pre}_tpev_prev_{gb}"] = self.vind(
                    v, lambda dst=dst, gb=gb: v.tensor_copy(
                        dst, tb[gb][:, :].rearrange("p (a b) -> p a b", b=128)))
        tok[f"{pre}_tp_{t}"] = self.tM.n
        tok[f"{pre}_tpev_{t}"] = self.vS.n

    def build(self):
        nc = self.nc
        tok = self.tok
        blk_cm = nc.Block()
        blk = blk_cm.__enter__()

        p1 = ExitStack()
        wkTb = p1.enter_context(nc.sbuf_tensor("wkTb", [128, HT * 512], BF16)).ap()
        wvTb = p1.enter_context(nc.sbuf_tensor("wvTb", [128, HT * 512], BF16)).ap()
        wqTb = p1.enter_context(nc.sbuf_tensor("wqTb", [128, HT * 512], BF16)).ap()
        lnT = [p1.enter_context(nc.sbuf_tensor(f"lnT{i}", [128, HT * 512],
                                               BF16)).ap() for i in range(2)]
        kvst = [p1.enter_context(nc.sbuf_tensor(f"kvst{i}", [128, 512],
                                                BF16)).ap() for i in range(2)]
        self.lnT, self.wkTb, self.wvTb, self.wqTb = lnT, wkTb, wvTb, wqTb
        self.kvst = kvst

        @blk.gpsimd
        def _(g):
            self.dma(g, self.identb[:, :], self.ident, self.sWS)
            self.dma(g, wkTb[:, :].rearrange("p (h f) -> p h f", f=512),
                     self.wkT.rearrange("(h p) f -> p h f", p=128), self.sWS)
            self.dma(g, wvTb[:, :].rearrange("p (h f) -> p h f", f=512),
                     self.wvT.rearrange("(h p) f -> p h f", p=128), self.sWS)
            self.dma(g, wqTb[:, :].rearrange("p (h f) -> p h f", f=512),
                     self.wqT[:, 0:512].rearrange("(h p) f -> p h f", p=128),
                     self.sWS)
            tok["identb"] = (self.sWS, self.sWS.n)
            tok["w_kv"] = (self.sWS, self.sWS.n)
            tok["wq_q0"] = (self.sWS, self.sWS.n)
            self.mat_gpsimd_service(g)

        @blk.sync
        def _(sy):
            tok["consts"] = self.dma(sy, self.lutP[:, :], self.lutp, self.sWC)

        @blk.vector
        def _(v):
            self.dmaw(v, tok["consts"])
            v.memset(self.epsA[:, :], EPS)
            v.memset(self.negone[:, :], -1.0)
            self.vind(v, lambda: v.memset(self.onesb[:, :], 1.0))
            self.vind(v, lambda: v.memset(self.ones1f[:, :], 1.0))
            self.vind(v, lambda: v.memset(self.onefF[:, :], 1.0))
            for j in range(16):
                self.vind(v, lambda j=j: v.memset(self.jneg[:, j:j + 1],
                                                  float(-j)))
            tok["consts_v"] = self.vS.n

        NT_FULL = NSUP * 4

        def x_src(t):
            if t < NT_FULL:
                return self.x_full[128 * t:128 * (t + 1), :]
            tt = t - NT_FULL
            return self.x_own[128 * tt:128 * (tt + 1), :]

        for su in range(NSUP + NSUPQ):
            for i in range(4):
                t = 4 * su + i
                fw = [tok["consts"]] if t == 0 else []
                self.emit_ln_tile(blk, "p1", t, x_src, lnT[su % 2], 512, fw)
            if su >= 1:
                self.emit_proj_supertile(blk, su - 1)
            self.mat_pump_section(blk, MAT_PUMP_P1, nsg=MAT_SG_P1)
        self.emit_proj_supertile(blk, NSUP + NSUPQ - 1)

        self.emit_q_pass(blk, wqTb)
        self.mat_pump_section(blk, MAT_PUMP_P1)

        p1.close()
        p2 = ExitStack()
        Ksl = [p2.enter_context(nc.sbuf_tensor(f"Ksl{i}", [128, SEQ], BF16)).ap()
               for i in range(2)]
        Vsl = [p2.enter_context(nc.sbuf_tensor(f"Vsl{i}", [128, SEQ], BF16)).ap()
               for i in range(2)]
        PT = [p2.enter_context(nc.sbuf_tensor(f"PT{i}", [128, 512], BF16)).ap()
              for i in range(2)]
        cmb = p2.enter_context(nc.sbuf_tensor("cmb", [128, 16 * BLK], BF16)).ap()
        aoT = p2.enter_context(nc.sbuf_tensor("aoT", [128, NH * 256], BF16)).ap()
        woS = [p2.enter_context(nc.sbuf_tensor(f"woS{i}", [128, HT * 128],
                                               BF16)).ap() for i in range(2)]
        qs = [p2.enter_context(nc.sbuf_tensor(f"qs{i}", [128, 256], BF16)).ap()
              for i in range(2)]
        self.avf = p2.enter_context(nc.sbuf_tensor("avf", [128, 256], F32)).ap()
        self.boS = p2.enter_context(nc.sbuf_tensor("boS", [128, DT], F32)).ap()
        self.opbf = p2.enter_context(nc.sbuf_tensor("opbf", [128, 256],
                                                    BF16)).ap()
        self.Ksl, self.Vsl, self.PT, self.cmb, self.aoT, self.woS = \
            Ksl, Vsl, PT, cmb, aoT, woS
        self.qs = qs
        self.hseq = 0
        self.slast = {}

        j_t, j_s, j_v = self.tM.n, self.sS.n, self.vS.n
        self.p2_join = (j_t, j_s, j_v)
        self.p2_jdO = self.store_snapshot()
        prev_ao = None

        @blk.sync
        def _(sy):
            sy.wait_ge(self.sS.h, j_s)
            for sv in self.p2_jdO:
                self.dmaw(sy, sv)
            tok["bo_ld"] = self.dma(sy, self.boS[:, :], self.boP, self.sWC)

        for slot in range(NSLOT):
            nkb = 16 * (slot + 1)

            @blk.gpsimd
            def _(g, slot=slot):
                if slot > 0:
                    g.wait_ge(self.tM.h, tok[f"att_done_{4 * slot - 1}"])
                else:
                    g.wait_ge(self.tM.h, j_t)
                    g.wait_ge(self.sS.h, j_s)
                tok[f"cmask_{slot}"] = self.dma(g, cmb[:, :],
                                                self.cmask[slot, :, :], self.sWS)
                self.mat_gpsimd_service(g)

            for kv in range(KVH):
                gidx = slot * KVH + kv
                par = gidx % 2

                @blk.sync
                def _(sy, kv=kv, gidx=gidx, par=par, nkb=nkb):
                    wt = tok.get(f"att_done_{gidx - 2}")
                    if wt:
                        sy.wait_ge(self.tM.h, wt)
                    else:
                        sy.wait_ge(self.tM.h, j_t)
                        sy.wait_ge(self.sS.h, j_s)
                        for sv in self.p2_jdO:
                            self.dmaw(sy, sv)
                    self.dma(sy, Ksl[par][:, 0:128 * nkb],
                             self.d_K[kv, :, 0:128 * nkb], self.sKV[par])
                    tok[f"kvload_{gidx}"] = self.dma(
                        sy, Vsl[par][:, 0:128 * nkb],
                        self.d_V[kv, :, 0:128 * nkb], self.sKV[par])

                @blk.vector
                def _(v, gidx=gidx, par=par, nkb=nkb):
                    self.dmaw(v, tok[f"kvload_{gidx}"])
                    self.vchain(v, lambda: v.reduce_sum(
                        self.ksum[:, :], Ksl[par][:, 0:128 * nkb], axis=AX.X))
                    self.vchain(v, lambda: v.tensor_scalar(
                        self.negmean[:, :], self.ksum[:, :],
                        -1.0 / (128 * nkb), None, ALU.mult))
                    tok[f"kshift_{gidx}"] = self.vchain(
                        v, lambda: v.tensor_scalar(
                            Ksl[par][:, 0:128 * nkb], Ksl[par][:, 0:128 * nkb],
                            self.negmean[:, :], None, ALU.add))

                for hh in range(4):
                    h = 4 * kv + hh
                    prev_ao = self.emit_attention_head(
                        blk, slot, kv, h, gidx, par, nkb, prev_ao)
                    @blk.gpsimd
                    def _(g):
                        self.mat_gpsimd_service(g)

                    if MAT_PUMP_P2[slot]:
                        self.mat_pump_drv(blk, MAT_PUMP_P2[slot])
                    if MAT_SG_P2:
                        self.mat_pump_sg(blk, MAT_SG_P2)
                tok[f"att_done_{gidx}"] = self.tM.n

            self.emit_attn_outproj(blk, slot)

        p2.close()
        self.mat_stage2(blk)
        self.emit_ffn(blk, self.d_X[0], self.d_X[1], 1)

        self.emit_attn_b(blk)

        self.emit_ffn(blk, self.d_X[2], None, 2)

        if DEBUG_TAPS:
            @blk.gpsimd
            def _(g):
                for sv in self.store_snapshot():
                    self.dmaw(g, sv)
                for (dst, srcap) in [
                        (self.t_x1, self.d_X[0]), (self.t_x2, self.d_X[1]),
                        (self.t_x3, self.d_X[2]),
                        (self.t_wg, self.d_Wg[0, :, :]),
                        (self.t_wd, self.d_Wd[0, :, :]),
                        (self.t_wu, self.d_Wu[0, :, :]),
                        (self.t_k, self.d_K[0, :, 0:1024]),
                        (self.t_v, self.d_V[0, :, 0:1024]),
                        (self.t_q, self.d_Q[0, :, 0:1024])]:
                    self.dma(g, dst, srcap, self.sFX)

        @blk.sync
        def _(sy):
            for sv in self.store_snapshot():
                self.dmaw(sy, sv)

        blk_cm.__exit__(None, None, None)
        self.es.close()
        return nc

    def emit_proj_supertile(self, blk, su):
        tok = self.tok
        own = su >= NSUP
        if own:
            return
        src = self.lnT[su % 2]
        groups = [("k", f) for f in range(4)] + [("v", i) for i in range(4)]

        for gi in range(0, len(groups), 2):
            pair = groups[gi:gi + 2]

            @blk.tensor
            def _(tn, pair=pair, su=su, gi=gi, src=src, own=own):
                tn.wait_ge(self.vS.h, tok[f"p1_tpev_{4 * su + 3}"])
                if gi == 0 and su == 0:
                    self.dmaw(tn, tok["w_kv"])
                for pi, (kind, f) in enumerate(pair):
                    psd = (self.psC, self.psD)[pi]
                    wt = tok.get(f"pj_ev_{su}_{gi - 2 + pi}")
                    if wt is None and su > 0:
                        wt = tok.get(f"pj_ev_{su - 1}_{len(groups) - 2 + pi}")
                    if wt:
                        tn.wait_ge(self.sS.h, wt)
                    for ht in range(HT):
                        if kind == "k":
                            lhs = self.wkTb[:, 512 * ht + 128 * f:
                                            512 * ht + 128 * (f + 1)]
                            rhs = src[:, 512 * ht:512 * (ht + 1)]
                        else:
                            lhs = src[:, 512 * ht + 128 * f:512 * ht + 128 * (f + 1)]
                            rhs = self.wvTb[:, 512 * ht:512 * (ht + 1)]
                        ins = tn.matmul(psd[:, :], lhs, rhs, start=(ht == 0),
                                        stop=(ht == HT - 1))
                    ins.then_inc(self.tM.h, 1)
                    self.tM.n += 1
                    tok[f"pj_mm_{su}_{gi + pi}"] = self.tM.n

            @blk.scalar
            def _(s, pair=pair, su=su, gi=gi):
                for pi, (kind, f) in enumerate(pair):
                    psd = (self.psC, self.psD)[pi]
                    s.wait_ge(self.tM.h, tok[f"pj_mm_{su}_{gi + pi}"])
                    st = self.kvst[(gi + pi) % 2]
                    wt = tok.get(f"pj_st_{su}_{gi - 2 + pi}")
                    if wt is None and su > 0:
                        wt = tok.get(f"pj_st_{su - 1}_{gi + pi + 6}")
                    if wt:
                        self.dmaw(s, wt)
                    tok[f"pj_ev_{su}_{gi + pi}"] = self.sind(
                        s, lambda st=st, psd=psd: s.copy(st[:, :], psd[:, :]))

            if True:
                @blk.sync
                def _(sy, pair=pair, su=su, gi=gi):
                    for pi, (kind, f) in enumerate(pair):
                        sy.wait_ge(self.sS.h, tok[f"pj_ev_{su}_{gi + pi}"])
                        st = self.kvst[(gi + pi) % 2]
                        sem = self.sST[(gi + pi) % 2]
                        if kind == "k":
                            tok[f"pj_st_{su}_{gi + pi}"] = self.dma(
                                sy, self.d_K[f, :, 512 * su:512 * (su + 1)],
                                st[:, :], sem)
                        else:
                            t = 4 * su + f
                            tok[f"pj_st_{su}_{gi + pi}"] = self.dma(
                                sy,
                                self.d_V[:, :, 128 * t:128 * (t + 1)].rearrange(
                                    "kv p f -> p kv f"),
                                st[:, :].rearrange("p (kv f) -> p kv f", f=128),
                                sem)



    def emit_q_pass(self, blk, wqTb):
        tok = self.tok
        for q in range(4):
            if q > 0:
                @blk.gpsimd
                def _(g, q=q):
                    g.wait_ge(self.tM.h, tok[f"q_mm_{q - 1}_last"])
                    tok[f"wq_q{q}"] = self.dma(
                        g, wqTb[:, :].rearrange("p (h f) -> p h f", f=512),
                        self.wqT[:, 512 * q:512 * (q + 1)].rearrange(
                            "(h p) f -> p h f", p=128), self.sWS)
                    self.mat_gpsimd_service(g)
            for su2 in range(2):
                su = NSUP + su2
                for jr in range(2):
                    @blk.tensor
                    def _(tn, q=q, su2=su2, su=su, jr=jr):
                        if su2 == 0 and jr == 0:
                            self.dmaw(tn, tok[f"wq_q{q}"])
                            if q == 0:
                                tn.wait_ge(self.vS.h,
                                           tok[f"p1_tpev_{4 * (NSUP + 1) + 3}"])
                                tn.wait_ge(self.sS.h,
                                           tok[f"pj_ev_{NSUP - 1}_7"])
                        src = self.lnT[su % 2]
                        for pi in range(2):
                            j = 2 * jr + pi
                            psd = (self.psC, self.psD)[pi]
                            wt = tok.get(f"qp_ev_{q}_{su2}_{jr - 1}_{pi}")
                            if wt is None:
                                wt = tok.get(f"qp_ev_{q}_{su2 - 1}_{1}_{pi}")
                            if wt is None:
                                wt = tok.get(f"qp_ev_{q - 1}_1_1_{pi}")
                            if wt:
                                tn.wait_ge(self.sS.h, wt)
                            for ht in range(HT):
                                lhs = wqTb[:, 512 * ht + 128 * j:
                                           512 * ht + 128 * (j + 1)]
                                rhs = src[:, 512 * ht:512 * (ht + 1)]
                                ins = tn.matmul(psd[:, :], lhs, rhs,
                                                start=(ht == 0),
                                                stop=(ht == HT - 1))
                            ins.then_inc(self.tM.h, 1)
                            self.tM.n += 1
                            tok[f"qp_mm_{q}_{su2}_{jr}_{pi}"] = self.tM.n
                        tok[f"q_mm_{q}_last"] = self.tM.n

                    @blk.scalar
                    def _(s, q=q, su2=su2, jr=jr):
                        for pi in range(2):
                            j = 2 * jr + pi
                            psd = (self.psC, self.psD)[pi]
                            s.wait_ge(self.tM.h, tok[f"qp_mm_{q}_{su2}_{jr}_{pi}"])
                            st = self.kvst[pi]
                            wt = tok.get(f"qp_st_{q}_{su2}_{jr - 1}_{pi}")
                            if wt is None:
                                wt = tok.get(f"qp_st_{q}_{su2 - 1}_1_{pi}")
                            if wt is None:
                                wt = tok.get(f"qp_st_{q - 1}_1_1_{pi}")
                            if wt is None:
                                wt = tok.get(f"pj_st_{NSUP - 1}_{6 + pi}")
                            if wt:
                                self.dmaw(s, wt)
                            tok[f"qp_ev_{q}_{su2}_{jr}_{pi}"] = self.sind(
                                s, lambda st=st, psd=psd: s.copy(st[:, :], psd[:, :]))

                    @blk.sync
                    def _(sy, q=q, su2=su2, jr=jr):
                        for pi in range(2):
                            j = 2 * jr + pi
                            h = 4 * q + j
                            sy.wait_ge(self.sS.h, tok[f"qp_ev_{q}_{su2}_{jr}_{pi}"])
                            tok[f"qp_st_{q}_{su2}_{jr}_{pi}"] = self.dma(
                                sy, self.d_Q[h, :, 512 * su2:512 * (su2 + 1)],
                                self.kvst[pi][:, :], self.sST[pi])

    def emit_attention_head(self, blk, slot, kv, h, gidx, par, nkb, prev_ao):
        tok = self.tok
        npair = nkb // 2
        tail0 = npair - 8
        Ksl, Vsl, PT, cmb, aoT = (self.Ksl, self.Vsl, self.PT, self.cmb,
                                  self.aoT)
        hseq = self.hseq
        self.hseq += 1
        qb = self.qs[hseq % 2]

        @blk.sync
        def _(sy):
            wt = self.slast.get(hseq - 2)
            if wt:
                sy.wait_ge(self.tM.h, wt)
            elif hseq < 2:
                for sv in self.p2_jdO:
                    self.dmaw(sy, sv)
            tok[f"qld_{hseq}"] = self.dma(
                sy, qb[:, :], self.d_Q[h, :, 256 * slot:256 * (slot + 1)],
                self.sQ2[hseq % 2])

        base = self.tM.n

        def s_tok(p):
            return base + (1 if p == 0 else 2 * p)

        def a_tok(p):
            return base + (2 * npair if p == npair - 1 else 3 + 2 * p)

        bc_tok = base + 2 * npair + 1

        exp_toks = []

        @blk.scalar
        def _(s):
            for p in range(npair):
                s.wait_ge(self.tM.h, s_tok(p))
                ps = (self.psA, self.psB)[p % 2]
                ins = s.activation(PT[p % 2][:, :], ps[:, :], AF.Exp)
                ins.then_inc(self.sS.h, 1)
                self.sS.n += 1
                exp_toks.append(self.sS.n)

        msk_toks = {}
        r_tok = [None]
        av_tok = [None]
        ao_tok = [None]

        @blk.gpsimd
        def _(g):
            for p in range(tail0, npair):
                g.wait_ge(self.sS.h, exp_toks[p])
                m = p - tail0
                for half in range(2):
                    g.wait_ge(self.gS.h, self.gS.n)
                    g.tensor_tensor(
                        PT[p % 2][:, 256 * half:256 * (half + 1)],
                        PT[p % 2][:, 256 * half:256 * (half + 1)],
                        cmb[:, 256 * (2 * m + half):256 * (2 * m + half + 1)],
                        ALU.mult).then_inc(self.gS.h, 1)
                    self.gS.n += 1
                msk_toks[p] = self.gS.n

        @blk.vector
        def _(v):
            v.wait_ge(self.tM.h, a_tok(npair - 1))
            r_tok[0] = self.vchain(v, lambda: v.reciprocal(self.rL[:, :],
                                                           self.psL))
            av_tok[0] = self.vchain(v, lambda: v.tensor_copy(
                self.avf[:, :], self.psC[:, 0:256]))
            v.wait_ge(self.tM.h, bc_tok)
            ao_tok[0] = self.vchain(v, lambda: v.tensor_tensor(
                aoT[:, 256 * h:256 * (h + 1)], self.avf[:, :],
                self.psD[:, 0:256], ALU.mult))
            tok[f"ao_{slot}_{h}"] = ao_tok[0]

        @blk.tensor
        def _(tn):
            if prev_ao is None:
                tn.wait_ge(self.sS.h, self.p2_join[1])
                tn.wait_ge(self.vS.h, self.p2_join[2])

            def s_pair(p):
                ps = (self.psA, self.psB)[p % 2]
                if p == 0:
                    tn.wait_ge(self.vS.h, tok[f"kshift_{gidx}"])
                    self.dmaw(tn, tok[f"kvload_{gidx}"])
                    self.dmaw(tn, tok[f"qld_{hseq}"])
                    if h == 4 * kv:
                        self.dmaw(tn, tok[f"cmask_{slot}"])
                if p >= 2:
                    tn.wait_ge(self.sS.h, exp_toks[p - 2])
                for half in range(2):
                    kb = 2 * p + half
                    ins = tn.matmul(ps[:, 256 * half:256 * (half + 1)],
                                    Ksl[par][:, 128 * kb:128 * (kb + 1)],
                                    qb[:, :], start=True, stop=True)
                ins.then_inc(self.tM.h, 1)
                self.tM.n += 1
                assert self.tM.n == s_tok(p), (self.tM.n, s_tok(p))

            def lav_pair(p):
                if p == 0 and prev_ao is not None:
                    tn.wait_ge(self.vS.h, prev_ao)
                if p == 0 and h == 0 and slot > 0:
                    tn.wait_ge(self.sS.h, tok[f"op_ev_{slot - 1}_{DT - 1}"])
                if p >= tail0:
                    tn.wait_ge(self.gS.h, msk_toks[p])
                else:
                    tn.wait_ge(self.sS.h, exp_toks[p])
                for half in range(2):
                    kb = 2 * p + half
                    tn.matmul(self.psL, self.onesb[:, :],
                              PT[p % 2][:, 256 * half:256 * (half + 1)],
                              start=(kb == 0), stop=(kb == nkb - 1))
                    ins = tn.matmul(self.psC[:, 0:256],
                                    Vsl[par][:, 128 * kb:128 * (kb + 1)],
                                    PT[p % 2][:, 256 * half:256 * (half + 1)],
                                    start=(kb == 0), stop=(kb == nkb - 1))
                ins.then_inc(self.tM.h, 1)
                self.tM.n += 1
                assert self.tM.n == a_tok(p), (self.tM.n, a_tok(p))

            s_pair(0)
            if npair > 1:
                s_pair(1)
            for p in range(2, npair):
                lav_pair(p - 2)
                s_pair(p)
            if npair > 1:
                lav_pair(npair - 2)
            lav_pair(npair - 1)
            tn.wait_ge(self.vS.h, r_tok[0])
            ins = tn.matmul(self.psD[:, 0:256], self.ones1f[:, :],
                            self.rL[:, :], start=True, stop=True)
            ins.then_inc(self.tM.h, 1)
            self.tM.n += 1
            assert self.tM.n == bc_tok

        self.slast[hseq] = base + (1 if npair == 1 else 2 * (npair - 1))
        return ao_tok[0]

    def emit_attn_outproj(self, blk, slot):
        tok = self.tok

        @blk.sync
        def _(sy, slot=slot):
            for c in range(2):
                wt = tok.get(f"x1st_{slot - 1}_{c}")
                if wt:
                    self.dmaw(sy, wt)
                if slot == 0:
                    sy.wait_ge(self.sS.h, self.p2_join[1])
                tok[f"xown_{slot}_{c}"] = self.dma(
                    sy, self.xb[c][:, :],
                    self.x_own[256 * slot + 128 * c:256 * slot + 128 * (c + 1),
                               :], self.sX[c])

        for d in range(DT):
            @blk.gpsimd
            def _(g, d=d, slot=slot):
                wt = tok.get(f"op_mm_{slot}_{d - 1}")
                if wt is None and slot > 0:
                    wt = tok.get(f"op_mm_{slot - 1}_{DT - 1}")
                if wt:
                    g.wait_ge(self.tM.h, wt)
                elif slot == 0 and d < 2:
                    for sv in self.p2_jdO:
                        self.dmaw(g, sv)
                tok[f"op_w_{slot}_{d}"] = self.dma(
                    g, self.woS[d % 2][:, :], self.woTr[d, :, :],
                    self.sWO[d % 2])

            @blk.tensor
            def _(tn, d=d, slot=slot):
                self.dmaw(tn, tok[f"op_w_{slot}_{d}"])
                if d == 0:
                    tn.wait_ge(self.vS.h, tok[f"ao_{slot}_{15}"])
                wt = tok.get(f"op_ev_{slot}_{d - 1}")
                if wt:
                    tn.wait_ge(self.sS.h, wt)
                for h in range(NH):
                    ins = tn.matmul(self.psF[:, 0:256],
                                    self.woS[d % 2][:, 128 * h:128 * (h + 1)],
                                    self.aoT[:, 256 * h:256 * (h + 1)],
                                    start=(h == 0), stop=(h == NH - 1))
                ins.then_inc(self.tM.h, 1)
                self.tM.n += 1
                tok[f"op_mm_{slot}_{d}"] = self.tM.n

            @blk.scalar
            def _(s, d=d, slot=slot):
                s.wait_ge(self.tM.h, tok[f"op_mm_{slot}_{d}"])
                if slot == 0 and d == 0:
                    self.dmaw(s, tok["bo_ld"])
                wt = tok.get(f"op_tp_{slot}_{d - 1}")
                if wt:
                    s.wait_ge(self.tM.h, wt)
                tok[f"op_ev_{slot}_{d}"] = self.sind(
                    s, lambda d=d: s.activation(
                        self.opbf[:, :], self.psF[:, 0:256], AF.Identity,
                        bias=self.boS[:, d:d + 1]))

            @blk.tensor
            def _(tn, d=d, slot=slot):
                tn.wait_ge(self.sS.h, tok[f"op_ev_{slot}_{d}"])
                wt = tok.get(f"op_add_{slot}_{d - 2}")
                if wt:
                    tn.wait_ge(self.vS.h, wt)
                for c in range(2):
                    ins = tn.matmul(self.psT[d % 2][:, 128 * c:128 * (c + 1)],
                                    self.opbf[:, 128 * c:128 * (c + 1)],
                                    self.identb[:, :], start=True, stop=True)
                ins.then_inc(self.tM.h, 1)
                self.tM.n += 1
                tok[f"op_tp_{slot}_{d}"] = self.tM.n

            @blk.vector
            def _(v, d=d, slot=slot):
                v.wait_ge(self.tM.h, tok[f"op_tp_{slot}_{d}"])
                if d == 0:
                    self.dmaw(v, tok[f"xown_{slot}_0"])
                    self.dmaw(v, tok[f"xown_{slot}_1"])
                for c in range(2):
                    self.vind(v, lambda d=d, c=c: v.tensor_tensor(
                        self.xb[c][:, 128 * d:128 * (d + 1)],
                        self.xb[c][:, 128 * d:128 * (d + 1)],
                        self.psT[d % 2][:, 128 * c:128 * (c + 1)], ALU.add))
                tok[f"op_add_{slot}_{d}"] = self.vS.n

        @blk.sync
        def _(sy, slot=slot):
            sy.wait_ge(self.vS.h, tok[f"op_add_{slot}_{DT - 1}"])
            for c in range(2):
                tok[f"x1st_{slot}_{c}"] = self.dma(
                    sy, self.d_X[0][256 * slot + 128 * c:
                                    256 * slot + 128 * (c + 1), :],
                    self.xb[c][:, :], self.sXR[c])
        self.mat_pump_section(blk, 6, nsg=4)

    def emit_ffn(self, blk, xin_d, xout_d, phase):
        tok = self.tok
        nc = self.nc
        pre = f"f{phase}"
        pfh = ExitStack()
        hT = pfh.enter_context(nc.sbuf_tensor(f"hT_{phase}",
                                              [128, NIT * 1024], BF16)).ap()
        pf1 = ExitStack()
        ln2T = pf1.enter_context(nc.sbuf_tensor(f"ln2T_{phase}",
                                                [128, HT * 1024], BF16)).ap()
        wgS = [pf1.enter_context(nc.sbuf_tensor(f"wgS{i}_{phase}",
                                                [128, HT * 128], BF16)).ap()
               for i in range(2)]
        wuS = [pf1.enter_context(nc.sbuf_tensor(f"wuS{i}_{phase}",
                                                [128, HT * 128], BF16)).ap()
               for i in range(2)]
        gsb = [pf1.enter_context(nc.sbuf_tensor(f"gsb{i}_{phase}",
                                                [128, 512], BF16)).ap()
               for i in range(2)]
        jt, js, jv = self.tM.n, self.sS.n, self.vS.n

        def xsrc(t):
            return xin_d[128 * t:128 * (t + 1), :]

        fw = [(self.tM, jt), (self.sS, js), (self.vS, jv)]
        if phase == 1:
            fw.append(tok["x1st_3_0"])
            fw.append(tok["x1st_3_1"])
            fw.extend(self.mat["stage2_guard"])
        else:
            fw.append((self.sFX, self.sFX.n))
        for t in range(8):
            self.emit_ln_tile(blk, pre, t, xsrc, ln2T, 1024,
                              fw if t < 2 else [])
            if phase == 1:
                self.mat_pump_section(blk, MAT_PUMP_FFN, nsg=6)
        ln_done = self.sS.n

        for it in range(NIT):
            pair = it % 2

            @blk.sync
            def _(sy, it=it, pair=pair):
                wt = tok.get(f"{pre}_gu_{it - 2}_1")
                if wt:
                    sy.wait_ge(self.tM.h, wt)
                if it == 0 and phase == 1:
                    for sv in self.mat_store_tok(2 * HT - 1):
                        self.dmaw(sy, sv)
                self.dma(sy, wgS[pair][:, :], self.d_Wg[it, :, :],
                         self.sWU[pair])
                tok[f"{pre}_wgu_{it}"] = self.dma(
                    sy, wuS[pair][:, :], self.d_Wu[it, :, :], self.sWU[pair])

            @blk.tensor
            def _(tn, it=it, pair=pair):
                self.dmaw(tn, tok[f"{pre}_wgu_{it}"])
                if it == 0:
                    tn.wait_ge(self.sS.h, ln_done)
                for q2 in range(2):
                    psg = (self.psC, self.psE)[q2]
                    psq = (self.psD, self.psF)[q2]
                    wt = tok.get(f"{pre}_sil_{it - 1}_{q2}")
                    if wt:
                        tn.wait_ge(self.sS.h, wt)
                    wt2 = tok.get(f"{pre}_hml_{it - 1}_{q2}")
                    if wt2:
                        tn.wait_ge(self.vS.h, wt2)
                    for ht in range(HT):
                        tn.matmul(psg[:, :],
                                  wgS[pair][:, 128 * ht:128 * (ht + 1)],
                                  ln2T[:, 1024 * ht + 512 * q2:
                                       1024 * ht + 512 * (q2 + 1)],
                                  start=(ht == 0), stop=(ht == HT - 1))
                    for ht in range(HT):
                        ins = tn.matmul(psq[:, :],
                                        wuS[pair][:, 128 * ht:128 * (ht + 1)],
                                        ln2T[:, 1024 * ht + 512 * q2:
                                             1024 * ht + 512 * (q2 + 1)],
                                        start=(ht == 0), stop=(ht == HT - 1))
                    ins.then_inc(self.tM.h, 1)
                    self.tM.n += 1
                    tok[f"{pre}_gu_{it}_{q2}"] = self.tM.n

            @blk.scalar
            def _(s, it=it):
                for q2 in range(2):
                    psg = (self.psC, self.psE)[q2]
                    s.wait_ge(self.tM.h, tok[f"{pre}_gu_{it}_{q2}"])
                    wt = tok.get(f"{pre}_hml_{it - 1}_{q2}")
                    if wt:
                        s.wait_ge(self.vS.h, wt)
                    tok[f"{pre}_sil_{it}_{q2}"] = self.sind(
                        s, lambda q2=q2, psg=psg: s.activation(
                            gsb[q2][:, :], psg[:, :], AF.Silu))

            @blk.vector
            def _(v, it=it):
                for q2 in range(2):
                    psq = (self.psD, self.psF)[q2]
                    v.wait_ge(self.sS.h, tok[f"{pre}_sil_{it}_{q2}"])
                    self.vind(v, lambda it=it, q2=q2, psq=psq: v.tensor_tensor(
                        hT[:, 1024 * it + 512 * q2:1024 * it + 512 * (q2 + 1)],
                        gsb[q2][:, :], psq[:, :], ALU.mult))
                    tok[f"{pre}_hml_{it}_{q2}"] = self.vS.n

            if phase == 1:
                self.mat_pump_section(blk, MAT_PUMP_FFN, nsg=4)

        if phase == 1:
            self.mat_drain(blk)
            self.mat_stack.close()
        gu_done_t = self.tM.n
        hml_done_v = self.vS.n
        pf1.close()

        pf2 = ExitStack()
        xst = [pf2.enter_context(nc.sbuf_tensor(f"xst{i}_{phase}",
                                                [128, D], F32, side="right")).ap()
               for i in range(8)]
        wdS = [pf2.enter_context(nc.sbuf_tensor(f"wdS{i}_{phase}",
                                                [128, NIT * 128], BF16)).ap()
               for i in range(2)]

        @blk.sync
        def _(sy):
            sy.wait_ge(self.tM.h, gu_done_t)
            sy.wait_ge(self.vS.h, hml_done_v)
            if phase == 1:
                for sv in self.mat_store_tok(len(self.mat_jobs) - 1):
                    self.dmaw(sy, sv)
            for t in range(8):
                self.dma(sy, xst[t][:, :], xin_d[128 * t:128 * (t + 1), :],
                         self.sXT)
            tok[f"{pre}_xst_ld"] = (self.sXT, self.sXT.n)

        for d in range(DT):
            pair = d % 2

            @blk.sync
            def _(sy, d=d, pair=pair):
                wt = tok.get(f"{pre}_dn_{d - 2}_1")
                if wt:
                    sy.wait_ge(self.tM.h, wt)
                tok[f"{pre}_wd_{d}"] = self.dma(
                    sy, wdS[pair][:, :], self.d_Wd[d, :, :], self.sWD[pair])

            @blk.tensor
            def _(tn, d=d, pair=pair):
                self.dmaw(tn, tok[f"{pre}_wd_{d}"])
                if d == 0:
                    tn.wait_ge(self.vS.h, tok[f"{pre}_tpev_7"])
                for q2 in range(2):
                    psd = (self.psA, self.psB)[q2]
                    wt = tok.get(f"{pre}_dnev_{d - 1}_{q2}")
                    if wt:
                        tn.wait_ge(self.sS.h, wt)
                    for it in range(NIT):
                        ins = tn.matmul(
                            psd[:, :], wdS[pair][:, 128 * it:128 * (it + 1)],
                            hT[:, 1024 * it + 512 * q2:1024 * it + 512 * (q2 + 1)],
                            start=(it == 0), stop=(it == NIT - 1))
                    ins.then_inc(self.tM.h, 1)
                    self.tM.n += 1
                    tok[f"{pre}_dn_{d}_{q2}"] = self.tM.n

            @blk.scalar
            def _(s, d=d):
                for q2 in range(2):
                    psd = (self.psA, self.psB)[q2]
                    s.wait_ge(self.tM.h, tok[f"{pre}_dn_{d}_{q2}"])
                    wt = tok.get(f"{pre}_dtp_{d - 1}_{q2}")
                    if wt:
                        s.wait_ge(self.tM.h, wt)
                    tok[f"{pre}_dnev_{d}_{q2}"] = self.sind(
                        s, lambda q2=q2, psd=psd: s.copy(
                            self.dnbf[q2][:, :], psd[:, :]))

            @blk.tensor
            def _(tn, d=d):
                for q2 in range(2):
                    tn.wait_ge(self.sS.h, tok[f"{pre}_dnev_{d}_{q2}"])
                    wt = tok.get(f"{pre}_dadd_{d - 1}_{q2}")
                    if wt:
                        tn.wait_ge(self.vS.h, wt)
                    for c in range(4):
                        ins = tn.matmul(
                            self.psT[q2][:, 128 * c:128 * (c + 1)],
                            self.dnbf[q2][:, 128 * c:128 * (c + 1)],
                            self.identb[:, :], start=True, stop=True)
                    ins.then_inc(self.tM.h, 1)
                    self.tM.n += 1
                    tok[f"{pre}_dtp_{d}_{q2}"] = self.tM.n

            @blk.vector
            def _(v, d=d):
                if d == 0:
                    self.dmaw(v, tok[f"{pre}_xst_ld"])
                for q2 in range(2):
                    v.wait_ge(self.tM.h, tok[f"{pre}_dtp_{d}_{q2}"])
                    for c in range(4):
                        t = 4 * q2 + c
                        self.vind(v, lambda t=t, d=d, q2=q2, c=c: v.tensor_tensor(
                            xst[t][:, 128 * d:128 * (d + 1)],
                            xst[t][:, 128 * d:128 * (d + 1)],
                            self.psT[q2][:, 128 * c:128 * (c + 1)], ALU.add))
                    tok[f"{pre}_dadd_{d}_{q2}"] = self.vS.n

        @blk.sync
        def _(sy):
            dst = xout_d if xout_d is not None else self.y_own
            for t in range(8):
                sy.wait_ge(self.vS.h, tok[f"{pre}_dadd_{DT - 1}_{t // 4}"])
                tok[f"{pre}_xst_{t}"] = self.dma(
                    sy, dst[128 * t:128 * (t + 1), :], xst[t][:, :], self.sFX)

        pf2.close()
        pfh.close()

    def emit_attn_b(self, blk):
        tok = self.tok
        nc = self.nc
        pre = "b"
        pb = ExitStack()
        ln3T = pb.enter_context(nc.sbuf_tensor("ln3T", [128, HT * 1024],
                                               BF16)).ap()
        wvbS = pb.enter_context(nc.sbuf_tensor("wvbS", [128, HT * 512],
                                               BF16)).ap()
        wpA = pb.enter_context(nc.sbuf_tensor("wpA", [128, KVH * D], BF16)).ap()
        wpB = pb.enter_context(nc.sbuf_tensor("wpB", [128, KVH * D], BF16)).ap()
        vbT = pb.enter_context(nc.sbuf_tensor("vbT", [128, KVH * 1024],
                                              BF16)).ap()
        xst = [pb.enter_context(nc.sbuf_tensor(f"bxst{i}", [128, D], F32,
                                               side="right")).ap()
               for i in range(8)]

        jt, js, jv = self.tM.n, self.sS.n, self.vS.n

        @blk.gpsimd
        def _(g):
            g.wait_ge(self.tM.h, jt)
            g.wait_ge(self.sS.h, js)
            g.wait_ge(self.vS.h, jv)
            self.dma(g, wvbS[:, :].rearrange("p (h f) -> p h f", f=512),
                     self.wvbT.rearrange("(h p) f -> p h f", p=128), self.sWS)
            tok["wvb_ld"] = (self.sWS, self.sWS.n)
            wp4 = self.wpT.rearrange("(kv j p) d -> j p kv d", kv=4, p=128)
            tok["wp_0"] = self.dma(
                g, wpA[:, :].rearrange("p (kv d) -> p kv d", d=D),
                wp4[0, :, :, :], self.sWS)

        for j in range(1, 4):
            @blk.gpsimd
            def _(g, j=j):
                if j > 1:
                    g.wait_ge(self.vS.h, tok[f"wp_add_{j - 1}"])
                tok[f"wp_l{j}"] = self.dma(
                    g, wpB[:, :].rearrange("p (kv d) -> p kv d", d=D),
                    self.wpT.rearrange("(kv j p) d -> j p kv d",
                                       kv=4, p=128)[j, :, :, :], self.sWS)

            @blk.vector
            def _(v, j=j):
                self.dmaw(v, tok[f"wp_l{j}"])
                tok[f"wp_add_{j}"] = self.vchain(v, lambda: v.tensor_tensor(
                    wpA[:, :], wpA[:, :], wpB[:, :], ALU.add))

        def xsrc(t):
            return self.d_X[1][128 * t:128 * (t + 1), :]

        fw = [(self.tM, jt), (self.sS, js), (self.vS, jv),
              (self.sFX, self.sFX.n)]
        for t in range(8):
            self.emit_ln_tile(blk, pre, t, xsrc, ln3T, 1024,
                              fw if t < 2 else [])
        ln_done = self.sS.n

        for f in range(4):
            @blk.tensor
            def _(tn, f=f):
                if f == 0:
                    tn.wait_ge(self.sS.h, ln_done)
                    self.dmaw(tn, (self.sWS, self.sWS.n))
                for q2 in range(2):
                    psd = (self.psC, self.psD)[q2]
                    wt = tok.get(f"vb_ev_{f - 1}_{q2}")
                    if wt:
                        tn.wait_ge(self.sS.h, wt)
                    for ht in range(HT):
                        ins = tn.matmul(psd[:, :],
                                        wvbS[:, 512 * ht + 128 * f:
                                             512 * ht + 128 * (f + 1)],
                                        ln3T[:, 1024 * ht + 512 * q2:
                                             1024 * ht + 512 * (q2 + 1)],
                                        start=(ht == 0), stop=(ht == HT - 1))
                    ins.then_inc(self.tM.h, 1)
                    self.tM.n += 1
                    tok[f"vb_mm_{f}_{q2}"] = self.tM.n

            @blk.scalar
            def _(s, f=f):
                for q2 in range(2):
                    psd = (self.psC, self.psD)[q2]
                    s.wait_ge(self.tM.h, tok[f"vb_mm_{f}_{q2}"])
                    dst = vbT[:, 1024 * f + 512 * q2:1024 * f + 512 * (q2 + 1)]
                    tok[f"vb_ev_{f}_{q2}"] = self.sind(
                        s, lambda dst=dst, psd=psd: s.copy(dst, psd[:, :]))

        @blk.sync
        def _(sy):
            sy.wait_ge(self.sS.h, tok["vb_ev_3_1"])
            for t in range(8):
                self.dma(sy, xst[t][:, :],
                         self.d_X[1][128 * t:128 * (t + 1), :], self.sXT)
            tok["b_xst_ld"] = (self.sXT, self.sXT.n)

        for d in range(DT):
            @blk.tensor
            def _(tn, d=d):
                if d == 0:
                    tn.wait_ge(self.vS.h, tok["wp_add_3"])
                    tn.wait_ge(self.sS.h, tok["vb_ev_3_1"])
                for q2 in range(2):
                    psd = (self.psA, self.psB)[q2]
                    wt = tok.get(f"b_dnev_{d - 1}_{q2}")
                    if wt:
                        tn.wait_ge(self.sS.h, wt)
                    for f in range(4):
                        ins = tn.matmul(psd[:, :],
                                        wpA[:, D * f + 128 * d:
                                            D * f + 128 * (d + 1)],
                                        vbT[:, 1024 * f + 512 * q2:
                                            1024 * f + 512 * (q2 + 1)],
                                        start=(f == 0), stop=(f == 3))
                    ins.then_inc(self.tM.h, 1)
                    self.tM.n += 1
                    tok[f"b_dn_{d}_{q2}"] = self.tM.n

            @blk.scalar
            def _(s, d=d):
                for q2 in range(2):
                    psd = (self.psA, self.psB)[q2]
                    s.wait_ge(self.tM.h, tok[f"b_dn_{d}_{q2}"])
                    wt = tok.get(f"b_dtp_{d - 1}_{q2}")
                    if wt:
                        s.wait_ge(self.tM.h, wt)
                    tok[f"b_dnev_{d}_{q2}"] = self.sind(
                        s, lambda q2=q2, psd=psd: s.copy(
                            self.dnbf[q2][:, :], psd[:, :]))

            @blk.tensor
            def _(tn, d=d):
                for q2 in range(2):
                    tn.wait_ge(self.sS.h, tok[f"b_dnev_{d}_{q2}"])
                    wt = tok.get(f"b_dadd_{d - 1}_{q2}")
                    if wt:
                        tn.wait_ge(self.vS.h, wt)
                    for c in range(4):
                        ins = tn.matmul(
                            self.psT[q2][:, 128 * c:128 * (c + 1)],
                            self.dnbf[q2][:, 128 * c:128 * (c + 1)],
                            self.identb[:, :], start=True, stop=True)
                    ins.then_inc(self.tM.h, 1)
                    self.tM.n += 1
                    tok[f"b_dtp_{d}_{q2}"] = self.tM.n

            @blk.vector
            def _(v, d=d):
                if d == 0:
                    self.dmaw(v, tok["b_xst_ld"])
                for q2 in range(2):
                    v.wait_ge(self.tM.h, tok[f"b_dtp_{d}_{q2}"])
                    for c in range(4):
                        t = 4 * q2 + c
                        self.vind(v, lambda t=t, d=d, q2=q2, c=c: v.tensor_tensor(
                            xst[t][:, 128 * d:128 * (d + 1)],
                            xst[t][:, 128 * d:128 * (d + 1)],
                            self.psT[q2][:, 128 * c:128 * (c + 1)], ALU.add))
                    tok[f"b_dadd_{d}_{q2}"] = self.vS.n

        @blk.sync
        def _(sy):
            for t in range(8):
                sy.wait_ge(self.vS.h, tok[f"b_dadd_{DT - 1}_{t // 4}"])
                tok[f"b_xst_{t}"] = self.dma(
                    sy, self.d_X[2][128 * t:128 * (t + 1), :], xst[t][:, :],
                    self.sFX)

        pb.close()


_prog_cache = {}


def _get_prog():
    if "nc" not in _prog_cache:
        b = Builder()
        _prog_cache["nc"] = b.build()
    return _prog_cache["nc"]


def kernel(L0_attn_a_hidden_states, L0_attn_a_Wq, L0_attn_a_Wk, L0_attn_a_Wv,
           L0_attn_a_Wo, L0_attn_a_bo, L0_attn_b_weight_qkv,
           L0_attn_b_weight_proj, L0_ffn_lut, L0_ffn_gate_idx, L0_ffn_up_idx,
           L0_ffn_down_idx):
    f32 = np.float32
    x = np.ascontiguousarray(np.asarray(L0_attn_a_hidden_states, f32)
                             .reshape(SEQ, D))
    wqT = np.ascontiguousarray(np.asarray(L0_attn_a_Wq, f32).T)
    wkT = np.ascontiguousarray(np.asarray(L0_attn_a_Wk, f32).T)
    wvT = np.ascontiguousarray(np.asarray(L0_attn_a_Wv, f32).T)
    wo = np.asarray(L0_attn_a_Wo, f32)
    woTr = np.ascontiguousarray(
        wo.reshape(DT, 128, NH, 128).transpose(0, 3, 2, 1)).reshape(
            DT, 128, NH * 128)
    boP = np.ascontiguousarray(np.asarray(L0_attn_a_bo, f32)
                               .reshape(DT, 128).T)
    qkv = np.asarray(L0_attn_b_weight_qkv, f32)
    wvb = np.concatenate([qkv[768 * g + 640:768 * (g + 1)] for g in range(4)],
                         axis=0)
    wvbT = np.ascontiguousarray(wvb.T)
    wpT = np.ascontiguousarray(np.asarray(L0_attn_b_weight_proj, f32).T)
    lutp = np.ascontiguousarray(
        np.broadcast_to(np.asarray(L0_ffn_lut, f32), (128, 16)))
    gidxT = np.ascontiguousarray(np.asarray(L0_ffn_gate_idx, np.int32).T)
    uidxT = np.ascontiguousarray(np.asarray(L0_ffn_up_idx, np.int32).T)
    didxT = np.ascontiguousarray(np.asarray(L0_ffn_down_idx, np.int32).T)
    ident = np.eye(128, dtype=f32)

    pp = np.arange(128)[:, None]
    cq = np.arange(BLK)[None, :]
    cmasks = []
    for c in range(NCORES):
        cm = np.zeros((NSLOT, 128, 16 * BLK), f32)
        for k in range(NSLOT):
            for m in range(16):
                valid = (m * 128 + pp) <= (2 * c * 128 + cq)
                cm[k, :, BLK * m:BLK * (m + 1)] = valid.astype(f32)
        cmasks.append(cm)

    nc = _get_prog()
    in_maps = []
    for c in range(NCORES):
        rows = np.concatenate([np.arange(256 * (8 * k + c), 256 * (8 * k + c) + 256)
                               for k in range(NSLOT)])
        in_maps.append(dict(
            x_full=x, x_own=np.ascontiguousarray(x[rows]), wqT=wqT, wkT=wkT,
            wvT=wvT, woTr=woTr, boP=boP, wvbT=wvbT, wpT=wpT, lutp=lutp,
            gidxT=gidxT, uidxT=uidxT, didxT=didxT, cmask=cmasks[c],
            ident=ident))
    res = run_bass_kernel_spmd(nc, in_maps, core_ids=list(range(NCORES)))
    _prog_cache["taps"] = res.results
    y = np.empty((SEQ, D), f32)
    for c in range(NCORES):
        yo = np.asarray(res.results[c]["y_own"])
        for k in range(NSLOT):
            b = 8 * k + c
            y[256 * b:256 * (b + 1)] = yo[256 * k:256 * (k + 1)]
    return y.reshape(1, SEQ, D)


if __name__ == "__main__":
    b = Builder()
    nc = b.build()
    print("program built OK")


# version 23
# speedup vs baseline: 1.0856x; 1.0285x over previous
import numpy as np
from contextlib import ExitStack

import concourse.bass as bass
import concourse.mybir as mybir
from concourse.bass_utils import run_bass_kernel_spmd

F32 = mybir.dt.float32
BF16 = mybir.dt.bfloat16
I32 = mybir.dt.int32
AF = mybir.ActivationFunctionType
ALU = mybir.AluOpType
AX = mybir.AxisListType

SEQ = 8192
D = 2048
DEBUG_TAPS = False
NH = 16
KVH = 4
INTER = 5504
NIT = 43
HT = 16
DT = 16
BLK = 256
NSLOT = 4
NCORES = 8
OWN = NSLOT * BLK
EPS = 1e-5
NSUP = SEQ // 512
NSUPQ = OWN // 512

MAT_PUMP_P1 = 7
MAT_SG_P1 = 5
MAT_PUMP_P2 = (4, 7, 10, 13)
MAT_SG_P2 = 4
MAT_PUMP_FFN = 6


class Sem:
    def __init__(self, nc, es, name):
        self.h = es.enter_context(nc.semaphore(name))
        self.n = 0


class Builder:
    def __init__(self):
        self.nc = bass.Bass("TRN2", target_bir_lowering=False)
        nc = self.nc
        self.es = ExitStack()
        self.tok = {}

        def din(name, shape, dt=F32):
            return nc.dram_tensor(name, shape, dt, kind="ExternalInput").ap()

        self.x_full = din("x_full", [SEQ, D])
        self.x_own = din("x_own", [OWN, D])
        self.wqT = din("wqT", [D, D])
        self.wkT = din("wkT", [D, 512])
        self.wvT = din("wvT", [D, 512])
        self.woTr = din("woTr", [DT, 128, NH * 128])
        self.boP = din("boP", [128, DT])
        self.wvbT = din("wvbT", [D, 512])
        self.wpT = din("wpT", [D, D])
        self.lutp = din("lutp", [128, 16])
        self.gidxT = din("gidxT", [D, INTER], I32)
        self.uidxT = din("uidxT", [D, INTER], I32)
        self.didxT = din("didxT", [INTER, D], I32)
        self.cmask = din("cmask", [NSLOT, 128, 16 * BLK])
        self.ident = din("ident", [128, 128])
        self.y_own = nc.dram_tensor("y_own", [OWN, D], F32,
                                    kind="ExternalOutput").ap()
        if DEBUG_TAPS:
            self.t_x1 = nc.dram_tensor("t_x1", [OWN, D], F32,
                                       kind="ExternalOutput").ap()
            self.t_x2 = nc.dram_tensor("t_x2", [OWN, D], F32,
                                       kind="ExternalOutput").ap()
            self.t_x3 = nc.dram_tensor("t_x3", [OWN, D], F32,
                                       kind="ExternalOutput").ap()
            self.t_wg = nc.dram_tensor("t_wg", [128, HT * 128], F32,
                                       kind="ExternalOutput").ap()
            self.t_wd = nc.dram_tensor("t_wd", [128, NIT * 128], F32,
                                       kind="ExternalOutput").ap()
            self.t_wu = nc.dram_tensor("t_wu", [128, HT * 128], F32,
                                       kind="ExternalOutput").ap()
            self.t_k = nc.dram_tensor("t_k", [128, 1024], F32,
                                      kind="ExternalOutput").ap()
            self.t_v = nc.dram_tensor("t_v", [128, 1024], F32,
                                      kind="ExternalOutput").ap()
            self.t_q = nc.dram_tensor("t_q", [128, 1024], F32,
                                      kind="ExternalOutput").ap()

        self.d_K = nc.dram_tensor("d_K", [KVH, 128, SEQ], BF16, kind="Internal").ap()
        self.d_V = nc.dram_tensor("d_V", [KVH, 128, SEQ], BF16,
                                  kind="Internal").ap()
        self.d_Q = nc.dram_tensor("d_Q", [NH, 128, 1024], BF16,
                                  kind="Internal").ap()
        self.d_Wg = nc.dram_tensor("d_Wg", [NIT, 128, HT * 128], BF16,
                                   kind="Internal").ap()
        self.d_Wu = nc.dram_tensor("d_Wu", [NIT, 128, HT * 128], BF16,
                                   kind="Internal").ap()
        self.d_Wd = nc.dram_tensor("d_Wd", [DT, 128, NIT * 128], BF16,
                                   kind="Internal").ap()
        self.d_X = [nc.dram_tensor(f"d_X{i}", [OWN, D], F32, kind="Internal").ap()
                    for i in (1, 2, 3)]

        def mk(name):
            return Sem(nc, self.es, name)

        def mk2(name):
            return [Sem(nc, self.es, name + "0"), Sem(nc, self.es, name + "1")]

        self.sWS = mk("sWS")
        self.sWC = mk("sWC")
        self.sX = mk2("sX")
        self.sKV = mk2("sKV")
        self.sQ2 = mk2("sQ2")
        self.sWU = mk2("sWU")
        self.sWD = mk2("sWD")
        self.sWO = mk2("sWO")
        self.sXT = mk("sXT")
        self.sG = mk2("sG")
        self.sSg = mk2("sSg")
        self.sST = mk2("sST")
        self.sXR = mk2("sXR")
        self.sFX = mk("sFX")
        self.dM2 = mk2("dM")
        self.dMs = mk2("dMs")
        self.vS = Sem(nc, self.es, "vS")
        self.sS = Sem(nc, self.es, "sS")
        self.gS = Sem(nc, self.es, "gS")
        self.tM = Sem(nc, self.es, "tM")

        def sb(name, shape, dt):
            return nc.alloc_sbuf_tensor(name, shape, dt).ap()

        self.lutP = sb("lutP", [128, 16], F32)
        self.identb = sb("identb", [128, 128], BF16)
        self.onesb = sb("onesb", [128, 1], BF16)
        self.ones1f = sb("ones1f", [1, 128], F32)
        self.epsA = sb("epsA", [128, 1], F32)
        self.negone = sb("negone", [128, 1], F32)
        self.st6 = sb("st6", [128, 24], F32)
        self.mv2 = sb("mv2", [128, 2], F32)
        self.veps = sb("veps", [128, 1], F32)
        self.ivar4 = sb("ivar4", [128, 4], F32)
        self.negm4 = sb("negm4", [128, 4], F32)
        self.rstd4 = sb("rstd4", [128, 4], F32)
        self.lnb4 = sb("lnb4", [128, 4], F32)
        self.xb = [sb(f"xb{i}", [128, D], F32) for i in range(2)]
        self.lnt = [sb(f"lnt{i}", [128, D], BF16) for i in range(4)]
        self.ksum = sb("ksum", [128, 1], F32)
        self.negmean = sb("negmean", [128, 1], F32)
        self.rL = sb("rL", [1, 256], F32)
        self.jneg = sb("jneg", [128, 16], F32)
        self.onefF = sb("onefF", [128, 1], F32)
        self.dnbf = [sb(f"dnbf{i}", [128, 512], BF16) for i in range(2)]

        self.mat_stack = ExitStack()
        self.midx = [self.mat_stack.enter_context(
            nc.sbuf_tensor(f"midx{i}", [128, INTER], BF16, side="right")).ap()
            for i in range(2)]
        self.macc = [self.mat_stack.enter_context(
            nc.sbuf_tensor(f"macc{i}", [128, INTER], BF16, side="right")).ap()
            for i in range(2)]
        self.mtj = self.mat_stack.enter_context(
            nc.sbuf_tensor("mtj", [128, INTER], BF16, side="right")).ap()
        SGW = 1408
        self.SGW = SGW
        self.msq = self.mat_stack.enter_context(
            nc.sbuf_tensor("msq", [128, SGW], BF16, side="right")).ap()
        self.msq2 = self.mat_stack.enter_context(
            nc.sbuf_tensor("msq2", [128, SGW], BF16, side="right")).ap()
        self.mind = self.mat_stack.enter_context(
            nc.sbuf_tensor("mind", [128, 2 * SGW], BF16, side="right")).ap()
        self.midx_sg = self.mat_stack.enter_context(
            nc.sbuf_tensor("midx_sg", [128, 2 * SGW], BF16, side="right")).ap()
        self.macc_sg = self.mat_stack.enter_context(
            nc.sbuf_tensor("macc_sg", [128, 2 * SGW], BF16, side="right")).ap()

        def psu(name, shape, dt=F32):
            return nc.alloc_psum_tensor(name, shape, dt).ap()

        self.psA = psu("psA", [128, 512])
        self.psB = psu("psB", [128, 512])
        self.psC = psu("psC", [128, 512])
        self.psD = psu("psD", [128, 512])
        self.psE = psu("psE", [128, 512])
        self.psF = psu("psF", [128, 512])
        self.psL = self.psF[0:1, 0:256]
        self.psT = [psu(f"psT{i}", [128, 512]) for i in range(2)]

        self.mat_jobs = ([("g", i) for i in range(HT)]
                         + [("u", i) for i in range(HT)]
                         + [("d", i) for i in range(NIT)])
        SG_JOBS = {30, 31, 71, 72, 73, 74}
        self.vg_jobs = set()
        self.mat = dict(loads=0, load_tok={}, store_tok={}, done={},
                        cap=len(self.mat_jobs), sg_jobs=SG_JOBS,
                        vq=[i for i in range(len(self.mat_jobs))
                            if i not in SG_JOBS],
                        vqi=0, vj=0, buf_of={}, vg_tok={},
                        squnits=[], squi=0, sj=0, sg_ltok={}, sg_sttok={},
                        sg_stok={})
        for ji in sorted(SG_JOBS):
            kind, idx = self.mat_jobs[ji]
            W = INTER if kind in ("g", "u") else D
            qs_ = []
            c = 0
            while c < W:
                w = min(1408, W - c)
                qs_.append((ji, c, w))
                c += w
            self.mat["squnits"].extend(qs_)

    def dma(self, q, out, in_, sem):
        q.dma_start(out=out, in_=in_).then_inc(sem.h, 16)
        sem.n += 16
        return (sem, sem.n)

    def dmaw(self, eng, tokv):
        eng.wait_ge(tokv[0].h, tokv[1])

    def store_sems(self):
        return [self.sST[0], self.sST[1], self.sXR[0], self.sXR[1], self.sFX,
                self.dM2[0], self.dM2[1], self.dMs[0], self.dMs[1]]

    def store_snapshot(self):
        return [(s, s.n) for s in self.store_sems() if s.n > 0]

    def vchain(self, v, emit):
        v.wait_ge(self.vS.h, self.vS.n)
        emit().then_inc(self.vS.h, 1)
        self.vS.n += 1
        return self.vS.n

    def vind(self, v, emit):
        emit().then_inc(self.vS.h, 1)
        self.vS.n += 1
        return self.vS.n

    def schain(self, s, emit):
        s.wait_ge(self.sS.h, self.sS.n)
        emit().then_inc(self.sS.h, 1)
        self.sS.n += 1
        return self.sS.n

    def sind(self, s, emit):
        emit().then_inc(self.sS.h, 1)
        self.sS.n += 1
        return self.sS.n

    def mat_src(self, kind, idx):
        if kind == "g":
            dst = self.d_Wg[:, :, 128 * idx:128 * (idx + 1)].rearrange(
                "it p f -> p it f")
            return (self.gidxT[128 * idx:128 * (idx + 1), :], INTER, dst)
        if kind == "u":
            dst = self.d_Wu[:, :, 128 * idx:128 * (idx + 1)].rearrange(
                "it p f -> p it f")
            return (self.uidxT[128 * idx:128 * (idx + 1), :], INTER, dst)
        dst = self.d_Wd[:, :, 128 * idx:128 * (idx + 1)].rearrange(
            "d p f -> p d f")
        return (self.didxT[128 * idx:128 * (idx + 1), :], D, dst)

    def mat_gpsimd_service(self, g):
        m = self.mat
        for ji in sorted(m["done"].keys()):
            if ji in m["store_tok"]:
                continue
            kind, idx = self.mat_jobs[ji]
            _, w, dst = self.mat_src(kind, idx)
            sem, val = m["done"][ji]
            g.wait_ge(sem.h, val)
            bp = m["buf_of"][ji] % 2
            m["store_tok"][ji] = self.dma(
                g, dst, self.macc[bp][:, 0:w].rearrange("p (a f) -> p a f",
                                                        f=128), self.dM2[bp])
        while m["loads"] < m["cap"]:
            ji = m["vq"][m["loads"]] if m["loads"] < len(m["vq"]) else None
            if ji is None:
                break
            prev2 = m["vq"][m["loads"] - 2] if m["loads"] >= 2 else None
            prev4 = m["vq"][m["loads"] - 4] if m["loads"] >= 4 else None
            if prev2 is not None and prev2 not in m["done"]:
                break
            if prev4 is not None and prev4 not in m["store_tok"]:
                break
            kind, idx = self.mat_jobs[ji]
            srcap, w, _ = self.mat_src(kind, idx)
            if prev2 is not None:
                sem, val = m["done"][prev2]
                g.wait_ge(sem.h, val)
            if m.get("stage2_guard") and not m.get("stage2_guard_done"):
                for sv in m["stage2_guard"]:
                    self.dmaw(g, sv)
                m["stage2_guard_done"] = True
            li = m["loads"]
            m["load_tok"][ji] = self.dma(g, self.midx[li % 2][:, 0:w], srcap,
                                         self.sG[li % 2])
            m["buf_of"][ji] = li % 2
            m["loads"] += 1

    def mat_pump_drv(self, blk, nj):
        m = self.mat
        left = nj
        while left > 0 and m["vqi"] < len(m["vq"]):
            ji = m["vq"][m["vqi"]]
            if ji not in m["load_tok"]:
                break
            kind, idx = self.mat_jobs[ji]
            _, w, _ = self.mat_src(kind, idx)
            b = m["buf_of"][ji]
            hw = w // 2
            vg = ji in self.vg_jobs
            j0 = m["vj"]
            j1 = min(16, j0 + min(left, 1 if vg else left))
            left -= (j1 - j0)

            @blk.vector
            def _(v, ji=ji, j0=j0, j1=j1, b=b, hw=hw, vg=vg, w=w):
                if j0 == 0:
                    self.dmaw(v, m["load_tok"][ji])
                    if "consts" in self.tok:
                        self.dmaw(v, self.tok["consts"])
                    prev = m["vq"][m["vqi"] - 1] if m["vqi"] >= 1 else None
                    pd = m["done"].get(prev) if prev is not None else None
                    if pd is not None and pd[0] is self.gS:
                        v.wait_ge(self.gS.h, pd[1])
                    if not vg:
                        prev2 = (m["vq"][m["vqi"] - 2]
                                 if m["vqi"] >= 2 else None)
                        stok = (m["store_tok"].get(prev2)
                                if prev2 is not None else None)
                        if stok is not None:
                            self.dmaw(v, stok)
                        self.vchain(v, lambda: v.memset(self.macc[b][:, 0:w],
                                                        0.0))
                for j in range(j0, j1):
                    for hf in range(2):
                        sl = slice(hw * hf, hw * (hf + 1))
                        if vg:
                            gtok = m["vg_tok"].get(("g", ji, 2 * j + hf - 2))
                            if gtok:
                                v.wait_ge(self.gS.h, gtok)
                            ms = self.mtj[:, hw * ((2 * j + hf) % 2):
                                          hw * ((2 * j + hf) % 2) + hw]
                            self.vchain(v, lambda j=j, sl=sl, ms=ms:
                                        v.tensor_scalar(
                                            ms, self.midx[b][:, sl], float(j),
                                            self.lutP[:, j:j + 1],
                                            ALU.is_equal, ALU.mult))
                            m["vg_tok"][("v", ji, 2 * j + hf)] = self.vS.n
                        else:
                            self.vchain(v, lambda j=j, sl=sl: v.tensor_scalar(
                                self.mtj[:, 0:hw], self.midx[b][:, sl],
                                float(j), self.lutP[:, j:j + 1],
                                ALU.is_equal, ALU.mult))
                            self.vchain(v, lambda sl=sl: v.tensor_tensor(
                                self.macc[b][:, sl], self.macc[b][:, sl],
                                self.mtj[:, 0:hw], ALU.add))

            if vg:
                @blk.gpsimd
                def _(g, ji=ji, j0=j0, j1=j1, b=b, hw=hw, w=w):
                    if j0 == 0:
                        self.dmaw(g, m["load_tok"][ji])
                        prev2 = m["vq"][m["vqi"] - 2] if m["vqi"] >= 2 else None
                        stok = (m["store_tok"].get(prev2)
                                if prev2 is not None else None)
                        if stok is not None:
                            self.dmaw(g, stok)
                    for j in range(j0, j1):
                        for hf in range(2):
                            sl = slice(hw * hf, hw * (hf + 1))
                            g.wait_ge(self.vS.h, m["vg_tok"][("v", ji, 2 * j + hf)])
                            g.wait_ge(self.gS.h, self.gS.n)
                            ms = self.mtj[:, hw * ((2 * j + hf) % 2):
                                          hw * ((2 * j + hf) % 2) + hw]
                            if j == 0:
                                g.tensor_copy(self.macc[b][:, sl],
                                              ms).then_inc(self.gS.h, 1)
                            else:
                                g.tensor_tensor(self.macc[b][:, sl],
                                                self.macc[b][:, sl], ms,
                                                ALU.add).then_inc(self.gS.h, 1)
                            self.gS.n += 1
                            m["vg_tok"][("g", ji, 2 * j + hf)] = self.gS.n

            m["vj"] = j1
            if j1 == 16:
                if vg:
                    m["done"][ji] = (self.gS, self.gS.n)
                else:
                    m["done"][ji] = (self.vS, self.vS.n)
                m["vqi"] += 1
                m["vj"] = 0

    def mat_pump_sg(self, blk, nj):
        m = self.mat
        left = nj
        while left > 0:
            if m["squi"] >= len(m["squnits"]):
                return
            ui = m["squi"]
            ji, c0, w = m["squnits"][ui]
            kind, idx = self.mat_jobs[ji]
            srcap, _, dstap = self.mat_src(kind, idx)
            slot = ui % 2
            if ui not in m["sg_ltok"]:
                @blk.gpsimd
                def _(g, ui=ui):
                    for u2 in (ui, ui + 1):
                        if u2 >= len(m["squnits"]) or u2 in m["sg_ltok"]:
                            continue
                        ji2, c2, w2 = m["squnits"][u2]
                        k2, i2 = self.mat_jobs[ji2]
                        src2, _, _ = self.mat_src(k2, i2)
                        st = m["sg_sttok"].get(u2 - 2)
                        if st is not None:
                            self.dmaw(g, st)
                        ra = m["sg_sttok"].get("realloc")
                        if ra is not None and not m.get(f"sg_ra_done_{u2 % 2}"):
                            for sv in ra:
                                self.dmaw(g, sv)
                            m[f"sg_ra_done_{u2 % 2}"] = True
                        m["sg_ltok"][u2] = self.dma(
                            g, self.midx_sg[:, self.SGW * (u2 % 2):
                                            self.SGW * (u2 % 2) + w2],
                            src2[:, c2:c2 + w2], self.sSg[u2 % 2])
            j0 = m["sj"]
            j1 = min(16, j0 + min(left, 2))
            left -= (j1 - j0)

            @blk.scalar
            def _(s, ui=ui, ji=ji, j0=j0, j1=j1, w=w, slot=slot):
                for j in range(j0, j1):
                    if j == 0 and j0 == 0:
                        self.dmaw(s, m["sg_ltok"][ui])
                        s.wait_ge(self.vS.h, self.tok["consts_v"])
                        self.dmaw(s, self.tok["consts"])
                        s.wait_ge(self.gS.h, self.gS.n)
                    gtok = m["sg_stok"].get(("g", ui, j - 2))
                    if gtok:
                        s.wait_ge(self.gS.h, gtok)
                    mi = self.midx_sg[:, self.SGW * slot:self.SGW * slot + w]
                    self.schain(s, lambda j=j, mi=mi: s.activation(
                        self.msq[:, 0:w], mi, AF.Square,
                        bias=self.jneg[:, j:j + 1]))
                    self.schain(s, lambda: s.activation(
                        self.msq2[:, 0:w], self.msq[:, 0:w],
                        AF.Relu, bias=self.onefF[:, :], scale=-1.0))
                    self.schain(s, lambda j=j: s.activation(
                        self.mind[:, self.SGW * (j % 2):self.SGW * (j % 2) + w],
                        self.msq2[:, 0:w], AF.Identity,
                        scale=self.lutP[:, j:j + 1]))
                    m["sg_stok"][("s", ui, j)] = self.sS.n

            @blk.gpsimd
            def _(g, ui=ui, j0=j0, j1=j1, w=w, slot=slot):
                if j0 == 0:
                    self.dmaw(g, m["sg_ltok"][ui])
                    st = m["sg_sttok"].get(ui - 2)
                    if st is not None:
                        self.dmaw(g, st)
                    g.wait_ge(self.gS.h, self.gS.n)
                    g.memset(self.macc_sg[:, self.SGW * slot:
                                          self.SGW * slot + w],
                             0.0).then_inc(self.gS.h, 1)
                    self.gS.n += 1
                for j in range(j0, j1):
                    g.wait_ge(self.sS.h, m["sg_stok"][("s", ui, j)])
                    g.wait_ge(self.gS.h, self.gS.n)
                    g.tensor_tensor(
                        self.macc_sg[:, self.SGW * slot:self.SGW * slot + w],
                        self.macc_sg[:, self.SGW * slot:self.SGW * slot + w],
                        self.mind[:, self.SGW * (j % 2):self.SGW * (j % 2) + w],
                        ALU.add).then_inc(self.gS.h, 1)
                    self.gS.n += 1
                    m["sg_stok"][("g", ui, j)] = self.gS.n
                if j1 == 16:
                    g.wait_ge(self.gS.h, self.gS.n)
                    dst = dstap[:, c0 // 128:(c0 + w) // 128, :] \
                        if False else None
                    kindq, idxq = self.mat_jobs[ji]
                    _, _, dfull = self.mat_src(kindq, idxq)
                    m["sg_sttok"][ui] = self.dma(
                        g, dfull[:, c0 // 128:(c0 + w) // 128, :],
                        self.macc_sg[:, self.SGW * slot:self.SGW * slot + w
                                     ].rearrange("p (a f) -> p a f", f=128),
                        self.dMs[ui % 2])
                    if ui + 1 >= len(m["squnits"]) or \
                            m["squnits"][ui + 1][0] != ji:
                        m["store_tok"][ji] = m["sg_sttok"][ui]

            m["sj"] = j1
            if j1 == 16:
                m["squi"] += 1
                m["sj"] = 0

    def mat_pump_section(self, blk, nj, nsg=0):
        @blk.gpsimd
        def _(g):
            self.mat_gpsimd_service(g)

        if nsg:
            self.mat_pump_sg(blk, nsg)
        self.mat_pump_drv(blk, nj)

    def mat_drain(self, blk, upto=None):
        if upto is None:
            upto = len(self.mat_jobs)
        guard = 0
        while any(ji not in self.mat["store_tok"] for ji in range(upto)):
            self.mat_pump_section(blk, 48, nsg=16)
            guard += 1
            assert guard < 900, "mat drain stuck"

    def mat_stage2(self, blk):
        m = self.mat
        self.mat["cap"] = max(self.mat["loads"], 1)
        guard = 0
        gu_jobs = [ji for ji in range(2 * HT)]
        loaded = set(list(m["load_tok"].keys()))
        sg_started = set(u[0] for i, u in enumerate(m["squnits"])
                         if i <= m["squi"] or i in m["sg_ltok"])
        need = set(gu_jobs) | loaded | sg_started
        while any(ji not in m["store_tok"] for ji in need):
            self.mat_pump_section(blk, 48, nsg=16)
            loaded = set(list(m["load_tok"].keys()))
            sg_started = set(u[0] for i, u in enumerate(m["squnits"])
                             if i <= m["squi"] or i in m["sg_ltok"])
            need = set(gu_jobs) | loaded | sg_started
            guard += 1
            assert guard < 900, "mat stage2 drain stuck"
        m["stage2_guard"] = [(self.dM2[0], self.dM2[0].n),
                             (self.dM2[1], self.dM2[1].n),
                             (self.dMs[0], self.dMs[0].n),
                             (self.dMs[1], self.dMs[1].n)]
        m["stage2_guard_done"] = False
        self.mat_stack.close()
        nc = self.nc
        self.mat_stack = ExitStack()
        self.midx = [self.mat_stack.enter_context(
            nc.sbuf_tensor(f"midxB{i}", [128, D], BF16, side="right")).ap()
            for i in range(2)]
        self.macc = [self.mat_stack.enter_context(
            nc.sbuf_tensor(f"maccB{i}", [128, D], BF16, side="right")).ap()
            for i in range(2)]
        self.mtj = self.mat_stack.enter_context(
            nc.sbuf_tensor("mtjB", [128, D], BF16, side="right")).ap()
        SGW = 512
        self.SGW = SGW
        self.msq = self.mat_stack.enter_context(
            nc.sbuf_tensor("msqB", [128, SGW], BF16, side="right")).ap()
        self.msq2 = self.mat_stack.enter_context(
            nc.sbuf_tensor("msq2B", [128, SGW], BF16, side="right")).ap()
        self.mind = self.mat_stack.enter_context(
            nc.sbuf_tensor("mindB", [128, 2 * SGW], BF16, side="right")).ap()
        self.midx_sg = self.mat_stack.enter_context(
            nc.sbuf_tensor("midx_sgB", [128, 2 * SGW], BF16, side="right")).ap()
        self.macc_sg = self.mat_stack.enter_context(
            nc.sbuf_tensor("macc_sgB", [128, 2 * SGW], BF16, side="right")).ap()
        self.mat["cap"] = len(self.mat_jobs)

    def mat_store_tok(self, last_job):
        best = {}
        for ji in range(last_job + 1):
            t = self.mat["store_tok"].get(ji)
            assert t is not None, "mat job not stored before use"
            sem, val = t
            best[id(sem)] = (sem, max(val, best.get(id(sem), (sem, 0))[1]))
        return list(best.values())

    def ln_vector(self, v, xin, col, extra_waits=()):
        for sem, val in extra_waits:
            if val:
                v.wait_ge(sem.h, val)
        self.vchain(v, lambda: v.bn_stats(self.st6[:, 0:6], xin[:, 0:512]))
        for q in range(1, 4):
            self.vind(v, lambda q=q: v.bn_stats(
                self.st6[:, 6 * q:6 * q + 6], xin[:, 512 * q:512 * (q + 1)]))
        self.vchain(v, lambda: v.bn_aggr(self.mv2[:, :], self.st6[:, :]))
        self.vchain(v, lambda: v.tensor_scalar(
            self.ivar4[:, col:col + 1], self.mv2[:, 1:2], self.epsA[:, :],
            None, ALU.add))
        return self.vchain(v, lambda: v.tensor_scalar(
            self.negm4[:, col:col + 1], self.mv2[:, 0:1], self.negone[:, :],
            None, ALU.mult))

    def ln_scalar(self, s, xin, lnout, col, vtok, extra_waits=()):
        s.wait_ge(self.vS.h, vtok)
        for sem, val in extra_waits:
            if val:
                s.wait_ge(sem.h, val)
        self.schain(s, lambda: s.activation(
            self.veps[:, col % 1:col % 1 + 1], self.ivar4[:, col:col + 1], AF.Ln))
        self.schain(s, lambda: s.activation(
            self.rstd4[:, col:col + 1], self.veps[:, 0:1], AF.Exp, scale=-0.5))
        self.schain(s, lambda: s.activation(
            self.lnb4[:, col:col + 1], self.negm4[:, col:col + 1], AF.Identity,
            scale=self.rstd4[:, col:col + 1]))
        return self.schain(s, lambda: s.activation(
            lnout[:, :], xin[:, :], AF.Identity,
            bias=self.lnb4[:, col:col + 1], scale=self.rstd4[:, col:col + 1]))

    def emit_ln_tile(self, blk, pre, t, xsrc_fn, dstT, dst_stride, first_waits):
        tok = self.tok

        @blk.sync
        def _(sy, t=t):
            wt = tok.get(f"{pre}_lnap_{t - 2}")
            if wt:
                sy.wait_ge(self.sS.h, wt)
            for sem, val in first_waits:
                sy.wait_ge(sem.h, val)
            tok[f"{pre}_xl_{t}"] = self.dma(sy, self.xb[t % 2][:, :],
                                            xsrc_fn(t), self.sX[t % 2])

        @blk.vector
        def _(v, t=t):
            waits = [tok[f"{pre}_xl_{t}"],
                     (self.sS, tok.get(f"{pre}_lnap_{t - 4}", 0))]
            tok[f"{pre}_lnv_{t}"] = self.ln_vector(v, self.xb[t % 2], t % 4,
                                                   extra_waits=waits)

        @blk.scalar
        def _(s, t=t):
            waits = [(self.tM, tok.get(f"{pre}_tp_{t - 4}", 0))]
            tok[f"{pre}_lnap_{t}"] = self.ln_scalar(
                s, self.xb[t % 2], self.lnt[t % 4], t % 4,
                tok[f"{pre}_lnv_{t}"], extra_waits=waits)

        t_off = t % 4 if dst_stride == 512 else t
        tb = [self.psT[0], self.psT[1], self.psA, self.psB]

        @blk.tensor
        def _(tn, t=t):
            tn.wait_ge(self.sS.h, tok[f"{pre}_lnap_{t}"])
            if t == 0:
                self.dmaw(tn, tok["identb"])
            for gb in range(4):
                wt = tok.get(f"{pre}_tpev_prev_{gb}")
                if wt:
                    tn.wait_ge(self.vS.h, wt)
                for k in range(4):
                    h = 4 * gb + k
                    ins = tn.matmul(tb[gb][:, 128 * k:128 * (k + 1)],
                                    self.lnt[t % 4][:, 128 * h:128 * (h + 1)],
                                    self.identb[:, :], start=True, stop=True)
                ins.then_inc(self.tM.h, 1)
                self.tM.n += 1
                tok[f"{pre}_tp_{t}_{gb}"] = self.tM.n

        @blk.vector
        def _(v, t=t, t_off=t_off):
            for gb in range(4):
                v.wait_ge(self.tM.h, tok[f"{pre}_tp_{t}_{gb}"])
                dst = dstT[:, :].rearrange("p (h f) -> p h f", f=dst_stride)[
                    :, 4 * gb:4 * gb + 4, 128 * t_off:128 * (t_off + 1)]
                tok[f"{pre}_tpev_prev_{gb}"] = self.vind(
                    v, lambda dst=dst, gb=gb: v.tensor_copy(
                        dst, tb[gb][:, :].rearrange("p (a b) -> p a b", b=128)))
        tok[f"{pre}_tp_{t}"] = self.tM.n
        tok[f"{pre}_tpev_{t}"] = self.vS.n

    def build(self):
        nc = self.nc
        tok = self.tok
        blk_cm = nc.Block()
        blk = blk_cm.__enter__()

        p1 = ExitStack()
        wkTb = p1.enter_context(nc.sbuf_tensor("wkTb", [128, HT * 512], BF16)).ap()
        wvTb = p1.enter_context(nc.sbuf_tensor("wvTb", [128, HT * 512], BF16)).ap()
        wqTb = p1.enter_context(nc.sbuf_tensor("wqTb", [128, HT * 512], BF16)).ap()
        lnT = [p1.enter_context(nc.sbuf_tensor(f"lnT{i}", [128, HT * 512],
                                               BF16)).ap() for i in range(2)]
        kvst = [p1.enter_context(nc.sbuf_tensor(f"kvst{i}", [128, 512],
                                                BF16)).ap() for i in range(2)]
        self.lnT, self.wkTb, self.wvTb, self.wqTb = lnT, wkTb, wvTb, wqTb
        self.kvst = kvst

        @blk.gpsimd
        def _(g):
            self.dma(g, self.identb[:, :], self.ident, self.sWS)
            self.dma(g, wkTb[:, :].rearrange("p (h f) -> p h f", f=512),
                     self.wkT.rearrange("(h p) f -> p h f", p=128), self.sWS)
            self.dma(g, wvTb[:, :].rearrange("p (h f) -> p h f", f=512),
                     self.wvT.rearrange("(h p) f -> p h f", p=128), self.sWS)
            self.dma(g, wqTb[:, :].rearrange("p (h f) -> p h f", f=512),
                     self.wqT[:, 0:512].rearrange("(h p) f -> p h f", p=128),
                     self.sWS)
            tok["identb"] = (self.sWS, self.sWS.n)
            tok["w_kv"] = (self.sWS, self.sWS.n)
            tok["wq_q0"] = (self.sWS, self.sWS.n)
            self.mat_gpsimd_service(g)

        @blk.sync
        def _(sy):
            tok["consts"] = self.dma(sy, self.lutP[:, :], self.lutp, self.sWC)

        @blk.vector
        def _(v):
            self.dmaw(v, tok["consts"])
            v.memset(self.epsA[:, :], EPS)
            v.memset(self.negone[:, :], -1.0)
            self.vind(v, lambda: v.memset(self.onesb[:, :], 1.0))
            self.vind(v, lambda: v.memset(self.ones1f[:, :], 1.0))
            self.vind(v, lambda: v.memset(self.onefF[:, :], 1.0))
            for j in range(16):
                self.vind(v, lambda j=j: v.memset(self.jneg[:, j:j + 1],
                                                  float(-j)))
            tok["consts_v"] = self.vS.n

        NT_FULL = NSUP * 4

        def x_src(t):
            if t < NT_FULL:
                return self.x_full[128 * t:128 * (t + 1), :]
            tt = t - NT_FULL
            return self.x_own[128 * tt:128 * (tt + 1), :]

        for su in range(NSUP + NSUPQ):
            for i in range(4):
                t = 4 * su + i
                fw = [tok["consts"]] if t == 0 else []
                self.emit_ln_tile(blk, "p1", t, x_src, lnT[su % 2], 512, fw)
            if su >= 1:
                self.emit_proj_supertile(blk, su - 1)
            self.mat_pump_section(blk, MAT_PUMP_P1, nsg=MAT_SG_P1)
        self.emit_proj_supertile(blk, NSUP + NSUPQ - 1)

        self.emit_q_pass(blk, wqTb)
        self.mat_pump_section(blk, MAT_PUMP_P1)

        p1.close()
        p2 = ExitStack()
        Ksl = [p2.enter_context(nc.sbuf_tensor(f"Ksl{i}", [128, SEQ], BF16)).ap()
               for i in range(2)]
        Vsl = [p2.enter_context(nc.sbuf_tensor(f"Vsl{i}", [128, SEQ], BF16)).ap()
               for i in range(2)]
        PT = [p2.enter_context(nc.sbuf_tensor(f"PT{i}", [128, 512], BF16)).ap()
              for i in range(2)]
        cmb = p2.enter_context(nc.sbuf_tensor("cmb", [128, 16 * BLK], BF16)).ap()
        aoT = p2.enter_context(nc.sbuf_tensor("aoT", [128, NH * 256], BF16)).ap()
        woS = [p2.enter_context(nc.sbuf_tensor(f"woS{i}", [128, HT * 128],
                                               BF16)).ap() for i in range(2)]
        qs = [p2.enter_context(nc.sbuf_tensor(f"qs{i}", [128, 256], BF16)).ap()
              for i in range(2)]
        self.avf = p2.enter_context(nc.sbuf_tensor("avf", [128, 256], F32)).ap()
        self.boS = p2.enter_context(nc.sbuf_tensor("boS", [128, DT], F32)).ap()
        self.opbf = p2.enter_context(nc.sbuf_tensor("opbf", [128, 256],
                                                    BF16)).ap()
        self.Ksl, self.Vsl, self.PT, self.cmb, self.aoT, self.woS = \
            Ksl, Vsl, PT, cmb, aoT, woS
        self.qs = qs
        self.hseq = 0
        self.slast = {}

        j_t, j_s, j_v = self.tM.n, self.sS.n, self.vS.n
        self.p2_join = (j_t, j_s, j_v)
        self.p2_jdO = self.store_snapshot()
        prev_ao = None

        @blk.sync
        def _(sy):
            sy.wait_ge(self.sS.h, j_s)
            for sv in self.p2_jdO:
                self.dmaw(sy, sv)
            tok["bo_ld"] = self.dma(sy, self.boS[:, :], self.boP, self.sWC)

        for slot in range(NSLOT):
            nkb = 16 * (slot + 1)

            @blk.gpsimd
            def _(g, slot=slot):
                if slot > 0:
                    g.wait_ge(self.tM.h, tok[f"att_done_{4 * slot - 1}"])
                else:
                    g.wait_ge(self.tM.h, j_t)
                    g.wait_ge(self.sS.h, j_s)
                tok[f"cmask_{slot}"] = self.dma(g, cmb[:, :],
                                                self.cmask[slot, :, :], self.sWS)
                self.mat_gpsimd_service(g)

            for kv in range(KVH):
                gidx = slot * KVH + kv
                par = gidx % 2

                @blk.sync
                def _(sy, kv=kv, gidx=gidx, par=par, nkb=nkb):
                    wt = tok.get(f"att_done_{gidx - 2}")
                    if wt:
                        sy.wait_ge(self.tM.h, wt)
                    else:
                        sy.wait_ge(self.tM.h, j_t)
                        sy.wait_ge(self.sS.h, j_s)
                        for sv in self.p2_jdO:
                            self.dmaw(sy, sv)
                    self.dma(sy, Ksl[par][:, 0:128 * nkb],
                             self.d_K[kv, :, 0:128 * nkb], self.sKV[par])
                    tok[f"kvload_{gidx}"] = self.dma(
                        sy, Vsl[par][:, 0:128 * nkb],
                        self.d_V[kv, :, 0:128 * nkb], self.sKV[par])

                @blk.vector
                def _(v, gidx=gidx, par=par, nkb=nkb):
                    self.dmaw(v, tok[f"kvload_{gidx}"])
                    self.vchain(v, lambda: v.reduce_sum(
                        self.ksum[:, :], Ksl[par][:, 0:128 * nkb], axis=AX.X))
                    self.vchain(v, lambda: v.tensor_scalar(
                        self.negmean[:, :], self.ksum[:, :],
                        -1.0 / (128 * nkb), None, ALU.mult))
                    tok[f"kshift_{gidx}"] = self.vchain(
                        v, lambda: v.tensor_scalar(
                            Ksl[par][:, 0:128 * nkb], Ksl[par][:, 0:128 * nkb],
                            self.negmean[:, :], None, ALU.add))

                for hh in range(4):
                    h = 4 * kv + hh
                    prev_ao = self.emit_attention_head(
                        blk, slot, kv, h, gidx, par, nkb, prev_ao)
                    @blk.gpsimd
                    def _(g):
                        self.mat_gpsimd_service(g)

                    if MAT_PUMP_P2[slot]:
                        self.mat_pump_drv(blk, MAT_PUMP_P2[slot])
                    if MAT_SG_P2:
                        self.mat_pump_sg(blk, MAT_SG_P2)
                tok[f"att_done_{gidx}"] = self.tM.n

            self.emit_attn_outproj(blk, slot)

        p2.close()
        self.mat_stage2(blk)
        self.emit_ffn(blk, self.d_X[0], self.d_X[1], 1)

        self.emit_attn_b(blk)

        self.emit_ffn(blk, self.d_X[2], None, 2)

        if DEBUG_TAPS:
            @blk.gpsimd
            def _(g):
                for sv in self.store_snapshot():
                    self.dmaw(g, sv)
                for (dst, srcap) in [
                        (self.t_x1, self.d_X[0]), (self.t_x2, self.d_X[1]),
                        (self.t_x3, self.d_X[2]),
                        (self.t_wg, self.d_Wg[0, :, :]),
                        (self.t_wd, self.d_Wd[0, :, :]),
                        (self.t_wu, self.d_Wu[0, :, :]),
                        (self.t_k, self.d_K[0, :, 0:1024]),
                        (self.t_v, self.d_V[0, :, 0:1024]),
                        (self.t_q, self.d_Q[0, :, 0:1024])]:
                    self.dma(g, dst, srcap, self.sFX)

        @blk.sync
        def _(sy):
            for sv in self.store_snapshot():
                self.dmaw(sy, sv)

        blk_cm.__exit__(None, None, None)
        self.es.close()
        return nc

    def emit_proj_supertile(self, blk, su):
        tok = self.tok
        own = su >= NSUP
        if own:
            return
        src = self.lnT[su % 2]
        groups = [("k", f) for f in range(4)] + [("v", i) for i in range(4)]

        for gi in range(0, len(groups), 2):
            pair = groups[gi:gi + 2]

            @blk.tensor
            def _(tn, pair=pair, su=su, gi=gi, src=src, own=own):
                tn.wait_ge(self.vS.h, tok[f"p1_tpev_{4 * su + 3}"])
                if gi == 0 and su == 0:
                    self.dmaw(tn, tok["w_kv"])
                for pi, (kind, f) in enumerate(pair):
                    psd = (self.psC, self.psD)[pi]
                    wt = tok.get(f"pj_ev_{su}_{gi - 2 + pi}")
                    if wt is None and su > 0:
                        wt = tok.get(f"pj_ev_{su - 1}_{len(groups) - 2 + pi}")
                    if wt:
                        tn.wait_ge(self.sS.h, wt)
                    for ht in range(HT):
                        if kind == "k":
                            lhs = self.wkTb[:, 512 * ht + 128 * f:
                                            512 * ht + 128 * (f + 1)]
                            rhs = src[:, 512 * ht:512 * (ht + 1)]
                        else:
                            lhs = src[:, 512 * ht + 128 * f:512 * ht + 128 * (f + 1)]
                            rhs = self.wvTb[:, 512 * ht:512 * (ht + 1)]
                        ins = tn.matmul(psd[:, :], lhs, rhs, start=(ht == 0),
                                        stop=(ht == HT - 1))
                    ins.then_inc(self.tM.h, 1)
                    self.tM.n += 1
                    tok[f"pj_mm_{su}_{gi + pi}"] = self.tM.n

            @blk.scalar
            def _(s, pair=pair, su=su, gi=gi):
                for pi, (kind, f) in enumerate(pair):
                    psd = (self.psC, self.psD)[pi]
                    s.wait_ge(self.tM.h, tok[f"pj_mm_{su}_{gi + pi}"])
                    st = self.kvst[(gi + pi) % 2]
                    wt = tok.get(f"pj_st_{su}_{gi - 2 + pi}")
                    if wt is None and su > 0:
                        wt = tok.get(f"pj_st_{su - 1}_{gi + pi + 6}")
                    if wt:
                        self.dmaw(s, wt)
                    tok[f"pj_ev_{su}_{gi + pi}"] = self.sind(
                        s, lambda st=st, psd=psd: s.copy(st[:, :], psd[:, :]))

            if True:
                @blk.sync
                def _(sy, pair=pair, su=su, gi=gi):
                    for pi, (kind, f) in enumerate(pair):
                        sy.wait_ge(self.sS.h, tok[f"pj_ev_{su}_{gi + pi}"])
                        st = self.kvst[(gi + pi) % 2]
                        sem = self.sST[(gi + pi) % 2]
                        if kind == "k":
                            tok[f"pj_st_{su}_{gi + pi}"] = self.dma(
                                sy, self.d_K[f, :, 512 * su:512 * (su + 1)],
                                st[:, :], sem)
                        else:
                            t = 4 * su + f
                            tok[f"pj_st_{su}_{gi + pi}"] = self.dma(
                                sy,
                                self.d_V[:, :, 128 * t:128 * (t + 1)].rearrange(
                                    "kv p f -> p kv f"),
                                st[:, :].rearrange("p (kv f) -> p kv f", f=128),
                                sem)



    def emit_q_pass(self, blk, wqTb):
        tok = self.tok
        for q in range(4):
            if q > 0:
                @blk.gpsimd
                def _(g, q=q):
                    g.wait_ge(self.tM.h, tok[f"q_mm_{q - 1}_last"])
                    tok[f"wq_q{q}"] = self.dma(
                        g, wqTb[:, :].rearrange("p (h f) -> p h f", f=512),
                        self.wqT[:, 512 * q:512 * (q + 1)].rearrange(
                            "(h p) f -> p h f", p=128), self.sWS)
                    self.mat_gpsimd_service(g)
            for su2 in range(2):
                su = NSUP + su2
                for jr in range(2):
                    @blk.tensor
                    def _(tn, q=q, su2=su2, su=su, jr=jr):
                        if su2 == 0 and jr == 0:
                            self.dmaw(tn, tok[f"wq_q{q}"])
                            if q == 0:
                                tn.wait_ge(self.vS.h,
                                           tok[f"p1_tpev_{4 * (NSUP + 1) + 3}"])
                                tn.wait_ge(self.sS.h,
                                           tok[f"pj_ev_{NSUP - 1}_7"])
                        src = self.lnT[su % 2]
                        for pi in range(2):
                            j = 2 * jr + pi
                            psd = (self.psC, self.psD)[pi]
                            wt = tok.get(f"qp_ev_{q}_{su2}_{jr - 1}_{pi}")
                            if wt is None:
                                wt = tok.get(f"qp_ev_{q}_{su2 - 1}_{1}_{pi}")
                            if wt is None:
                                wt = tok.get(f"qp_ev_{q - 1}_1_1_{pi}")
                            if wt:
                                tn.wait_ge(self.sS.h, wt)
                            for ht in range(HT):
                                lhs = wqTb[:, 512 * ht + 128 * j:
                                           512 * ht + 128 * (j + 1)]
                                rhs = src[:, 512 * ht:512 * (ht + 1)]
                                ins = tn.matmul(psd[:, :], lhs, rhs,
                                                start=(ht == 0),
                                                stop=(ht == HT - 1))
                            ins.then_inc(self.tM.h, 1)
                            self.tM.n += 1
                            tok[f"qp_mm_{q}_{su2}_{jr}_{pi}"] = self.tM.n
                        tok[f"q_mm_{q}_last"] = self.tM.n

                    @blk.scalar
                    def _(s, q=q, su2=su2, jr=jr):
                        for pi in range(2):
                            j = 2 * jr + pi
                            psd = (self.psC, self.psD)[pi]
                            s.wait_ge(self.tM.h, tok[f"qp_mm_{q}_{su2}_{jr}_{pi}"])
                            st = self.kvst[pi]
                            wt = tok.get(f"qp_st_{q}_{su2}_{jr - 1}_{pi}")
                            if wt is None:
                                wt = tok.get(f"qp_st_{q}_{su2 - 1}_1_{pi}")
                            if wt is None:
                                wt = tok.get(f"qp_st_{q - 1}_1_1_{pi}")
                            if wt is None:
                                wt = tok.get(f"pj_st_{NSUP - 1}_{6 + pi}")
                            if wt:
                                self.dmaw(s, wt)
                            tok[f"qp_ev_{q}_{su2}_{jr}_{pi}"] = self.sind(
                                s, lambda st=st, psd=psd: s.copy(st[:, :], psd[:, :]))

                    @blk.sync
                    def _(sy, q=q, su2=su2, jr=jr):
                        for pi in range(2):
                            j = 2 * jr + pi
                            h = 4 * q + j
                            sy.wait_ge(self.sS.h, tok[f"qp_ev_{q}_{su2}_{jr}_{pi}"])
                            tok[f"qp_st_{q}_{su2}_{jr}_{pi}"] = self.dma(
                                sy, self.d_Q[h, :, 512 * su2:512 * (su2 + 1)],
                                self.kvst[pi][:, :], self.sST[pi])

    def emit_attention_head(self, blk, slot, kv, h, gidx, par, nkb, prev_ao):
        tok = self.tok
        npair = nkb // 2
        tail0 = npair - 8
        Ksl, Vsl, PT, cmb, aoT = (self.Ksl, self.Vsl, self.PT, self.cmb,
                                  self.aoT)
        hseq = self.hseq
        self.hseq += 1
        qb = self.qs[hseq % 2]

        @blk.sync
        def _(sy):
            wt = self.slast.get(hseq - 2)
            if wt:
                sy.wait_ge(self.tM.h, wt)
            elif hseq < 2:
                for sv in self.p2_jdO:
                    self.dmaw(sy, sv)
            tok[f"qld_{hseq}"] = self.dma(
                sy, qb[:, :], self.d_Q[h, :, 256 * slot:256 * (slot + 1)],
                self.sQ2[hseq % 2])

        base = self.tM.n

        def s_tok(p):
            return base + (1 if p == 0 else 2 * p)

        def a_tok(p):
            return base + (2 * npair if p == npair - 1 else 3 + 2 * p)

        bc_tok = base + 2 * npair + 1

        exp_toks = []

        @blk.scalar
        def _(s):
            for p in range(npair):
                s.wait_ge(self.tM.h, s_tok(p))
                ps = (self.psA, self.psB)[p % 2]
                ins = s.activation(PT[p % 2][:, :], ps[:, :], AF.Exp)
                ins.then_inc(self.sS.h, 1)
                self.sS.n += 1
                exp_toks.append(self.sS.n)

        msk_toks = {}
        r_tok = [None]
        av_tok = [None]
        ao_tok = [None]

        @blk.gpsimd
        def _(g):
            for p in range(tail0, npair):
                g.wait_ge(self.sS.h, exp_toks[p])
                m = p - tail0
                for half in range(2):
                    g.wait_ge(self.gS.h, self.gS.n)
                    g.tensor_tensor(
                        PT[p % 2][:, 256 * half:256 * (half + 1)],
                        PT[p % 2][:, 256 * half:256 * (half + 1)],
                        cmb[:, 256 * (2 * m + half):256 * (2 * m + half + 1)],
                        ALU.mult).then_inc(self.gS.h, 1)
                    self.gS.n += 1
                msk_toks[p] = self.gS.n

        @blk.vector
        def _(v):
            v.wait_ge(self.tM.h, a_tok(npair - 1))
            r_tok[0] = self.vchain(v, lambda: v.reciprocal(self.rL[:, :],
                                                           self.psL))
            av_tok[0] = self.vchain(v, lambda: v.tensor_copy(
                self.avf[:, :], self.psC[:, 0:256]))
            v.wait_ge(self.tM.h, bc_tok)
            ao_tok[0] = self.vchain(v, lambda: v.tensor_tensor(
                aoT[:, 256 * h:256 * (h + 1)], self.avf[:, :],
                self.psD[:, 0:256], ALU.mult))
            tok[f"ao_{slot}_{h}"] = ao_tok[0]

        @blk.tensor
        def _(tn):
            if prev_ao is None:
                tn.wait_ge(self.sS.h, self.p2_join[1])
                tn.wait_ge(self.vS.h, self.p2_join[2])

            def s_pair(p):
                ps = (self.psA, self.psB)[p % 2]
                if p == 0:
                    tn.wait_ge(self.vS.h, tok[f"kshift_{gidx}"])
                    self.dmaw(tn, tok[f"kvload_{gidx}"])
                    self.dmaw(tn, tok[f"qld_{hseq}"])
                    if h == 4 * kv:
                        self.dmaw(tn, tok[f"cmask_{slot}"])
                if p >= 2:
                    tn.wait_ge(self.sS.h, exp_toks[p - 2])
                for half in range(2):
                    kb = 2 * p + half
                    ins = tn.matmul(ps[:, 256 * half:256 * (half + 1)],
                                    Ksl[par][:, 128 * kb:128 * (kb + 1)],
                                    qb[:, :], start=True, stop=True)
                ins.then_inc(self.tM.h, 1)
                self.tM.n += 1
                assert self.tM.n == s_tok(p), (self.tM.n, s_tok(p))

            def lav_pair(p):
                if p == 0 and prev_ao is not None:
                    tn.wait_ge(self.vS.h, prev_ao)
                if p == 0 and h == 0 and slot > 0:
                    tn.wait_ge(self.sS.h, tok[f"op_ev_{slot - 1}_{DT - 1}"])
                if p >= tail0:
                    tn.wait_ge(self.gS.h, msk_toks[p])
                else:
                    tn.wait_ge(self.sS.h, exp_toks[p])
                for half in range(2):
                    kb = 2 * p + half
                    tn.matmul(self.psL, self.onesb[:, :],
                              PT[p % 2][:, 256 * half:256 * (half + 1)],
                              start=(kb == 0), stop=(kb == nkb - 1))
                    ins = tn.matmul(self.psC[:, 0:256],
                                    Vsl[par][:, 128 * kb:128 * (kb + 1)],
                                    PT[p % 2][:, 256 * half:256 * (half + 1)],
                                    start=(kb == 0), stop=(kb == nkb - 1))
                ins.then_inc(self.tM.h, 1)
                self.tM.n += 1
                assert self.tM.n == a_tok(p), (self.tM.n, a_tok(p))

            s_pair(0)
            if npair > 1:
                s_pair(1)
            for p in range(2, npair):
                lav_pair(p - 2)
                s_pair(p)
            if npair > 1:
                lav_pair(npair - 2)
            lav_pair(npair - 1)
            tn.wait_ge(self.vS.h, r_tok[0])
            ins = tn.matmul(self.psD[:, 0:256], self.ones1f[:, :],
                            self.rL[:, :], start=True, stop=True)
            ins.then_inc(self.tM.h, 1)
            self.tM.n += 1
            assert self.tM.n == bc_tok

        self.slast[hseq] = base + (1 if npair == 1 else 2 * (npair - 1))
        return ao_tok[0]

    def emit_attn_outproj(self, blk, slot):
        tok = self.tok

        @blk.sync
        def _(sy, slot=slot):
            for c in range(2):
                wt = tok.get(f"x1st_{slot - 1}_{c}")
                if wt:
                    self.dmaw(sy, wt)
                if slot == 0:
                    sy.wait_ge(self.sS.h, self.p2_join[1])
                tok[f"xown_{slot}_{c}"] = self.dma(
                    sy, self.xb[c][:, :],
                    self.x_own[256 * slot + 128 * c:256 * slot + 128 * (c + 1),
                               :], self.sX[c])

        for d in range(DT):
            @blk.gpsimd
            def _(g, d=d, slot=slot):
                wt = tok.get(f"op_mm_{slot}_{d - 1}")
                if wt is None and slot > 0:
                    wt = tok.get(f"op_mm_{slot - 1}_{DT - 1}")
                if wt:
                    g.wait_ge(self.tM.h, wt)
                elif slot == 0 and d < 2:
                    for sv in self.p2_jdO:
                        self.dmaw(g, sv)
                tok[f"op_w_{slot}_{d}"] = self.dma(
                    g, self.woS[d % 2][:, :], self.woTr[d, :, :],
                    self.sWO[d % 2])

            @blk.tensor
            def _(tn, d=d, slot=slot):
                self.dmaw(tn, tok[f"op_w_{slot}_{d}"])
                if d == 0:
                    tn.wait_ge(self.vS.h, tok[f"ao_{slot}_{15}"])
                wt = tok.get(f"op_ev_{slot}_{d - 1}")
                if wt:
                    tn.wait_ge(self.sS.h, wt)
                for h in range(NH):
                    ins = tn.matmul(self.psF[:, 0:256],
                                    self.woS[d % 2][:, 128 * h:128 * (h + 1)],
                                    self.aoT[:, 256 * h:256 * (h + 1)],
                                    start=(h == 0), stop=(h == NH - 1))
                ins.then_inc(self.tM.h, 1)
                self.tM.n += 1
                tok[f"op_mm_{slot}_{d}"] = self.tM.n

            @blk.scalar
            def _(s, d=d, slot=slot):
                s.wait_ge(self.tM.h, tok[f"op_mm_{slot}_{d}"])
                if slot == 0 and d == 0:
                    self.dmaw(s, tok["bo_ld"])
                wt = tok.get(f"op_tp_{slot}_{d - 1}")
                if wt:
                    s.wait_ge(self.tM.h, wt)
                tok[f"op_ev_{slot}_{d}"] = self.sind(
                    s, lambda d=d: s.activation(
                        self.opbf[:, :], self.psF[:, 0:256], AF.Identity,
                        bias=self.boS[:, d:d + 1]))

            @blk.tensor
            def _(tn, d=d, slot=slot):
                tn.wait_ge(self.sS.h, tok[f"op_ev_{slot}_{d}"])
                wt = tok.get(f"op_add_{slot}_{d - 2}")
                if wt:
                    tn.wait_ge(self.vS.h, wt)
                for c in range(2):
                    ins = tn.matmul(self.psT[d % 2][:, 128 * c:128 * (c + 1)],
                                    self.opbf[:, 128 * c:128 * (c + 1)],
                                    self.identb[:, :], start=True, stop=True)
                ins.then_inc(self.tM.h, 1)
                self.tM.n += 1
                tok[f"op_tp_{slot}_{d}"] = self.tM.n

            @blk.vector
            def _(v, d=d, slot=slot):
                v.wait_ge(self.tM.h, tok[f"op_tp_{slot}_{d}"])
                if d == 0:
                    self.dmaw(v, tok[f"xown_{slot}_0"])
                    self.dmaw(v, tok[f"xown_{slot}_1"])
                for c in range(2):
                    self.vind(v, lambda d=d, c=c: v.tensor_tensor(
                        self.xb[c][:, 128 * d:128 * (d + 1)],
                        self.xb[c][:, 128 * d:128 * (d + 1)],
                        self.psT[d % 2][:, 128 * c:128 * (c + 1)], ALU.add))
                tok[f"op_add_{slot}_{d}"] = self.vS.n

        @blk.sync
        def _(sy, slot=slot):
            sy.wait_ge(self.vS.h, tok[f"op_add_{slot}_{DT - 1}"])
            for c in range(2):
                tok[f"x1st_{slot}_{c}"] = self.dma(
                    sy, self.d_X[0][256 * slot + 128 * c:
                                    256 * slot + 128 * (c + 1), :],
                    self.xb[c][:, :], self.sXR[c])
        self.mat_pump_section(blk, 6, nsg=4)

    def emit_ffn(self, blk, xin_d, xout_d, phase):
        tok = self.tok
        nc = self.nc
        pre = f"f{phase}"
        pfh = ExitStack()
        hT = pfh.enter_context(nc.sbuf_tensor(f"hT_{phase}",
                                              [128, NIT * 1024], BF16)).ap()
        pf1 = ExitStack()
        ln2T = pf1.enter_context(nc.sbuf_tensor(f"ln2T_{phase}",
                                                [128, HT * 1024], BF16)).ap()
        wgS = [pf1.enter_context(nc.sbuf_tensor(f"wgS{i}_{phase}",
                                                [128, HT * 128], BF16)).ap()
               for i in range(2)]
        wuS = [pf1.enter_context(nc.sbuf_tensor(f"wuS{i}_{phase}",
                                                [128, HT * 128], BF16)).ap()
               for i in range(2)]
        gsb = [pf1.enter_context(nc.sbuf_tensor(f"gsb{i}_{phase}",
                                                [128, 512], BF16)).ap()
               for i in range(2)]
        jt, js, jv = self.tM.n, self.sS.n, self.vS.n

        def xsrc(t):
            return xin_d[128 * t:128 * (t + 1), :]

        fw = [(self.tM, jt), (self.sS, js), (self.vS, jv)]
        if phase == 1:
            fw.append(tok["x1st_3_0"])
            fw.append(tok["x1st_3_1"])
            fw.extend(self.mat["stage2_guard"])
        else:
            fw.append((self.sFX, self.sFX.n))
        for t in range(8):
            self.emit_ln_tile(blk, pre, t, xsrc, ln2T, 1024,
                              fw if t < 2 else [])
            if phase == 1:
                self.mat_pump_section(blk, MAT_PUMP_FFN, nsg=6)
        ln_done = self.sS.n

        for it in range(NIT):
            pair = it % 2

            @blk.sync
            def _(sy, it=it, pair=pair):
                wt = tok.get(f"{pre}_gu_{it - 2}_1")
                if wt:
                    sy.wait_ge(self.tM.h, wt)
                if it == 0 and phase == 1:
                    for sv in self.mat_store_tok(2 * HT - 1):
                        self.dmaw(sy, sv)
                self.dma(sy, wgS[pair][:, :], self.d_Wg[it, :, :],
                         self.sWU[pair])
                tok[f"{pre}_wgu_{it}"] = self.dma(
                    sy, wuS[pair][:, :], self.d_Wu[it, :, :], self.sWU[pair])

            @blk.tensor
            def _(tn, it=it, pair=pair):
                self.dmaw(tn, tok[f"{pre}_wgu_{it}"])
                if it == 0:
                    tn.wait_ge(self.sS.h, ln_done)
                for q2 in range(2):
                    psg = (self.psC, self.psE)[q2]
                    psq = (self.psD, self.psF)[q2]
                    wt = tok.get(f"{pre}_sil_{it - 1}_{q2}")
                    if wt:
                        tn.wait_ge(self.sS.h, wt)
                    wt2 = tok.get(f"{pre}_hml_{it - 1}_{q2}")
                    if wt2:
                        tn.wait_ge(self.vS.h, wt2)
                    for ht in range(HT):
                        tn.matmul(psg[:, :],
                                  wgS[pair][:, 128 * ht:128 * (ht + 1)],
                                  ln2T[:, 1024 * ht + 512 * q2:
                                       1024 * ht + 512 * (q2 + 1)],
                                  start=(ht == 0), stop=(ht == HT - 1))
                    for ht in range(HT):
                        ins = tn.matmul(psq[:, :],
                                        wuS[pair][:, 128 * ht:128 * (ht + 1)],
                                        ln2T[:, 1024 * ht + 512 * q2:
                                             1024 * ht + 512 * (q2 + 1)],
                                        start=(ht == 0), stop=(ht == HT - 1))
                    ins.then_inc(self.tM.h, 1)
                    self.tM.n += 1
                    tok[f"{pre}_gu_{it}_{q2}"] = self.tM.n

            @blk.scalar
            def _(s, it=it):
                for q2 in range(2):
                    psg = (self.psC, self.psE)[q2]
                    s.wait_ge(self.tM.h, tok[f"{pre}_gu_{it}_{q2}"])
                    wt = tok.get(f"{pre}_hml_{it - 1}_{q2}")
                    if wt:
                        s.wait_ge(self.vS.h, wt)
                    tok[f"{pre}_sil_{it}_{q2}"] = self.sind(
                        s, lambda q2=q2, psg=psg: s.activation(
                            gsb[q2][:, :], psg[:, :], AF.Silu))

            @blk.vector
            def _(v, it=it):
                for q2 in range(2):
                    psq = (self.psD, self.psF)[q2]
                    v.wait_ge(self.sS.h, tok[f"{pre}_sil_{it}_{q2}"])
                    self.vind(v, lambda it=it, q2=q2, psq=psq: v.tensor_tensor(
                        hT[:, 1024 * it + 512 * q2:1024 * it + 512 * (q2 + 1)],
                        gsb[q2][:, :], psq[:, :], ALU.mult))
                    tok[f"{pre}_hml_{it}_{q2}"] = self.vS.n

            if phase == 1:
                self.mat_pump_section(blk, MAT_PUMP_FFN, nsg=4)

        if phase == 1:
            self.mat_drain(blk)
            self.mat_stack.close()
        gu_done_t = self.tM.n
        hml_done_v = self.vS.n
        pf1.close()

        pf2 = ExitStack()
        xst = [pf2.enter_context(nc.sbuf_tensor(f"xst{i}_{phase}",
                                                [128, D], F32, side="right")).ap()
               for i in range(8)]
        wdS = [pf2.enter_context(nc.sbuf_tensor(f"wdS{i}_{phase}",
                                                [128, NIT * 128], BF16)).ap()
               for i in range(2)]

        @blk.sync
        def _(sy):
            sy.wait_ge(self.tM.h, gu_done_t)
            sy.wait_ge(self.vS.h, hml_done_v)
            if phase == 1:
                for sv in self.mat_store_tok(len(self.mat_jobs) - 1):
                    self.dmaw(sy, sv)
            for t in range(8):
                self.dma(sy, xst[t][:, :], xin_d[128 * t:128 * (t + 1), :],
                         self.sXT)
            tok[f"{pre}_xst_ld"] = (self.sXT, self.sXT.n)

        for d in range(DT):
            pair = d % 2

            @blk.sync
            def _(sy, d=d, pair=pair):
                wt = tok.get(f"{pre}_dn_{d - 2}_1")
                if wt:
                    sy.wait_ge(self.tM.h, wt)
                tok[f"{pre}_wd_{d}"] = self.dma(
                    sy, wdS[pair][:, :], self.d_Wd[d, :, :], self.sWD[pair])

            @blk.tensor
            def _(tn, d=d, pair=pair):
                self.dmaw(tn, tok[f"{pre}_wd_{d}"])
                if d == 0:
                    tn.wait_ge(self.vS.h, tok[f"{pre}_tpev_7"])
                for q2 in range(2):
                    psd = (self.psA, self.psB)[q2]
                    wt = tok.get(f"{pre}_dnev_{d - 1}_{q2}")
                    if wt:
                        tn.wait_ge(self.sS.h, wt)
                    for it in range(NIT):
                        ins = tn.matmul(
                            psd[:, :], wdS[pair][:, 128 * it:128 * (it + 1)],
                            hT[:, 1024 * it + 512 * q2:1024 * it + 512 * (q2 + 1)],
                            start=(it == 0), stop=(it == NIT - 1))
                    ins.then_inc(self.tM.h, 1)
                    self.tM.n += 1
                    tok[f"{pre}_dn_{d}_{q2}"] = self.tM.n

            @blk.scalar
            def _(s, d=d):
                for q2 in range(2):
                    psd = (self.psA, self.psB)[q2]
                    s.wait_ge(self.tM.h, tok[f"{pre}_dn_{d}_{q2}"])
                    wt = tok.get(f"{pre}_dtp_{d - 1}_{q2}")
                    if wt:
                        s.wait_ge(self.tM.h, wt)
                    tok[f"{pre}_dnev_{d}_{q2}"] = self.sind(
                        s, lambda q2=q2, psd=psd: s.copy(
                            self.dnbf[q2][:, :], psd[:, :]))

            @blk.tensor
            def _(tn, d=d):
                for q2 in range(2):
                    tn.wait_ge(self.sS.h, tok[f"{pre}_dnev_{d}_{q2}"])
                    wt = tok.get(f"{pre}_dadd_{d - 1}_{q2}")
                    if wt:
                        tn.wait_ge(self.vS.h, wt)
                    for c in range(4):
                        ins = tn.matmul(
                            self.psT[q2][:, 128 * c:128 * (c + 1)],
                            self.dnbf[q2][:, 128 * c:128 * (c + 1)],
                            self.identb[:, :], start=True, stop=True)
                    ins.then_inc(self.tM.h, 1)
                    self.tM.n += 1
                    tok[f"{pre}_dtp_{d}_{q2}"] = self.tM.n

            @blk.vector
            def _(v, d=d):
                if d == 0:
                    self.dmaw(v, tok[f"{pre}_xst_ld"])
                for q2 in range(2):
                    v.wait_ge(self.tM.h, tok[f"{pre}_dtp_{d}_{q2}"])
                    for c in range(4):
                        t = 4 * q2 + c
                        self.vind(v, lambda t=t, d=d, q2=q2, c=c: v.tensor_tensor(
                            xst[t][:, 128 * d:128 * (d + 1)],
                            xst[t][:, 128 * d:128 * (d + 1)],
                            self.psT[q2][:, 128 * c:128 * (c + 1)], ALU.add))
                    tok[f"{pre}_dadd_{d}_{q2}"] = self.vS.n

        @blk.sync
        def _(sy):
            dst = xout_d if xout_d is not None else self.y_own
            for t in range(8):
                sy.wait_ge(self.vS.h, tok[f"{pre}_dadd_{DT - 1}_{t // 4}"])
                tok[f"{pre}_xst_{t}"] = self.dma(
                    sy, dst[128 * t:128 * (t + 1), :], xst[t][:, :], self.sFX)

        pf2.close()
        pfh.close()

    def emit_attn_b(self, blk):
        tok = self.tok
        nc = self.nc
        pre = "b"
        pb = ExitStack()
        ln3T = pb.enter_context(nc.sbuf_tensor("ln3T", [128, HT * 1024],
                                               BF16)).ap()
        wvbS = pb.enter_context(nc.sbuf_tensor("wvbS", [128, HT * 512],
                                               BF16)).ap()
        wpA = pb.enter_context(nc.sbuf_tensor("wpA", [128, KVH * D], BF16)).ap()
        wpB = pb.enter_context(nc.sbuf_tensor("wpB", [128, KVH * D], BF16)).ap()
        vbT = pb.enter_context(nc.sbuf_tensor("vbT", [128, KVH * 1024],
                                              BF16)).ap()
        xst = [pb.enter_context(nc.sbuf_tensor(f"bxst{i}", [128, D], F32,
                                               side="right")).ap()
               for i in range(8)]

        jt, js, jv = self.tM.n, self.sS.n, self.vS.n

        @blk.gpsimd
        def _(g):
            g.wait_ge(self.tM.h, jt)
            g.wait_ge(self.sS.h, js)
            g.wait_ge(self.vS.h, jv)
            self.dma(g, wvbS[:, :].rearrange("p (h f) -> p h f", f=512),
                     self.wvbT.rearrange("(h p) f -> p h f", p=128), self.sWS)
            tok["wvb_ld"] = (self.sWS, self.sWS.n)
            wp4 = self.wpT.rearrange("(kv j p) d -> j p kv d", kv=4, p=128)
            tok["wp_0"] = self.dma(
                g, wpA[:, :].rearrange("p (kv d) -> p kv d", d=D),
                wp4[0, :, :, :], self.sWS)

        for j in range(1, 4):
            @blk.gpsimd
            def _(g, j=j):
                if j > 1:
                    g.wait_ge(self.vS.h, tok[f"wp_add_{j - 1}"])
                tok[f"wp_l{j}"] = self.dma(
                    g, wpB[:, :].rearrange("p (kv d) -> p kv d", d=D),
                    self.wpT.rearrange("(kv j p) d -> j p kv d",
                                       kv=4, p=128)[j, :, :, :], self.sWS)

            @blk.vector
            def _(v, j=j):
                self.dmaw(v, tok[f"wp_l{j}"])
                tok[f"wp_add_{j}"] = self.vchain(v, lambda: v.tensor_tensor(
                    wpA[:, :], wpA[:, :], wpB[:, :], ALU.add))

        def xsrc(t):
            return self.d_X[1][128 * t:128 * (t + 1), :]

        fw = [(self.tM, jt), (self.sS, js), (self.vS, jv),
              (self.sFX, self.sFX.n)]
        for t in range(8):
            self.emit_ln_tile(blk, pre, t, xsrc, ln3T, 1024,
                              fw if t < 2 else [])
        ln_done = self.sS.n

        for f in range(4):
            @blk.tensor
            def _(tn, f=f):
                if f == 0:
                    tn.wait_ge(self.sS.h, ln_done)
                    self.dmaw(tn, (self.sWS, self.sWS.n))
                for q2 in range(2):
                    psd = (self.psC, self.psD)[q2]
                    wt = tok.get(f"vb_ev_{f - 1}_{q2}")
                    if wt:
                        tn.wait_ge(self.sS.h, wt)
                    for ht in range(HT):
                        ins = tn.matmul(psd[:, :],
                                        wvbS[:, 512 * ht + 128 * f:
                                             512 * ht + 128 * (f + 1)],
                                        ln3T[:, 1024 * ht + 512 * q2:
                                             1024 * ht + 512 * (q2 + 1)],
                                        start=(ht == 0), stop=(ht == HT - 1))
                    ins.then_inc(self.tM.h, 1)
                    self.tM.n += 1
                    tok[f"vb_mm_{f}_{q2}"] = self.tM.n

            @blk.scalar
            def _(s, f=f):
                for q2 in range(2):
                    psd = (self.psC, self.psD)[q2]
                    s.wait_ge(self.tM.h, tok[f"vb_mm_{f}_{q2}"])
                    dst = vbT[:, 1024 * f + 512 * q2:1024 * f + 512 * (q2 + 1)]
                    tok[f"vb_ev_{f}_{q2}"] = self.sind(
                        s, lambda dst=dst, psd=psd: s.copy(dst, psd[:, :]))

        @blk.sync
        def _(sy):
            sy.wait_ge(self.sS.h, tok["vb_ev_3_1"])
            for t in range(8):
                self.dma(sy, xst[t][:, :],
                         self.d_X[1][128 * t:128 * (t + 1), :], self.sXT)
            tok["b_xst_ld"] = (self.sXT, self.sXT.n)

        for d in range(DT):
            @blk.tensor
            def _(tn, d=d):
                if d == 0:
                    tn.wait_ge(self.vS.h, tok["wp_add_3"])
                    tn.wait_ge(self.sS.h, tok["vb_ev_3_1"])
                for q2 in range(2):
                    psd = (self.psA, self.psB)[q2]
                    wt = tok.get(f"b_dnev_{d - 1}_{q2}")
                    if wt:
                        tn.wait_ge(self.sS.h, wt)
                    for f in range(4):
                        ins = tn.matmul(psd[:, :],
                                        wpA[:, D * f + 128 * d:
                                            D * f + 128 * (d + 1)],
                                        vbT[:, 1024 * f + 512 * q2:
                                            1024 * f + 512 * (q2 + 1)],
                                        start=(f == 0), stop=(f == 3))
                    ins.then_inc(self.tM.h, 1)
                    self.tM.n += 1
                    tok[f"b_dn_{d}_{q2}"] = self.tM.n

            @blk.scalar
            def _(s, d=d):
                for q2 in range(2):
                    psd = (self.psA, self.psB)[q2]
                    s.wait_ge(self.tM.h, tok[f"b_dn_{d}_{q2}"])
                    wt = tok.get(f"b_dtp_{d - 1}_{q2}")
                    if wt:
                        s.wait_ge(self.tM.h, wt)
                    tok[f"b_dnev_{d}_{q2}"] = self.sind(
                        s, lambda q2=q2, psd=psd: s.copy(
                            self.dnbf[q2][:, :], psd[:, :]))

            @blk.tensor
            def _(tn, d=d):
                for q2 in range(2):
                    tn.wait_ge(self.sS.h, tok[f"b_dnev_{d}_{q2}"])
                    wt = tok.get(f"b_dadd_{d - 1}_{q2}")
                    if wt:
                        tn.wait_ge(self.vS.h, wt)
                    for c in range(4):
                        ins = tn.matmul(
                            self.psT[q2][:, 128 * c:128 * (c + 1)],
                            self.dnbf[q2][:, 128 * c:128 * (c + 1)],
                            self.identb[:, :], start=True, stop=True)
                    ins.then_inc(self.tM.h, 1)
                    self.tM.n += 1
                    tok[f"b_dtp_{d}_{q2}"] = self.tM.n

            @blk.vector
            def _(v, d=d):
                if d == 0:
                    self.dmaw(v, tok["b_xst_ld"])
                for q2 in range(2):
                    v.wait_ge(self.tM.h, tok[f"b_dtp_{d}_{q2}"])
                    for c in range(4):
                        t = 4 * q2 + c
                        self.vind(v, lambda t=t, d=d, q2=q2, c=c: v.tensor_tensor(
                            xst[t][:, 128 * d:128 * (d + 1)],
                            xst[t][:, 128 * d:128 * (d + 1)],
                            self.psT[q2][:, 128 * c:128 * (c + 1)], ALU.add))
                    tok[f"b_dadd_{d}_{q2}"] = self.vS.n

        @blk.sync
        def _(sy):
            for t in range(8):
                sy.wait_ge(self.vS.h, tok[f"b_dadd_{DT - 1}_{t // 4}"])
                tok[f"b_xst_{t}"] = self.dma(
                    sy, self.d_X[2][128 * t:128 * (t + 1), :], xst[t][:, :],
                    self.sFX)

        pb.close()


_prog_cache = {}


def _get_prog():
    if "nc" not in _prog_cache:
        b = Builder()
        _prog_cache["nc"] = b.build()
    return _prog_cache["nc"]


def kernel(L0_attn_a_hidden_states, L0_attn_a_Wq, L0_attn_a_Wk, L0_attn_a_Wv,
           L0_attn_a_Wo, L0_attn_a_bo, L0_attn_b_weight_qkv,
           L0_attn_b_weight_proj, L0_ffn_lut, L0_ffn_gate_idx, L0_ffn_up_idx,
           L0_ffn_down_idx):
    f32 = np.float32
    x = np.ascontiguousarray(np.asarray(L0_attn_a_hidden_states, f32)
                             .reshape(SEQ, D))
    wqT = np.ascontiguousarray(np.asarray(L0_attn_a_Wq, f32).T)
    wkT = np.ascontiguousarray(np.asarray(L0_attn_a_Wk, f32).T)
    wvT = np.ascontiguousarray(np.asarray(L0_attn_a_Wv, f32).T)
    wo = np.asarray(L0_attn_a_Wo, f32)
    woTr = np.ascontiguousarray(
        wo.reshape(DT, 128, NH, 128).transpose(0, 3, 2, 1)).reshape(
            DT, 128, NH * 128)
    boP = np.ascontiguousarray(np.asarray(L0_attn_a_bo, f32)
                               .reshape(DT, 128).T)
    qkv = np.asarray(L0_attn_b_weight_qkv, f32)
    wvb = np.concatenate([qkv[768 * g + 640:768 * (g + 1)] for g in range(4)],
                         axis=0)
    wvbT = np.ascontiguousarray(wvb.T)
    wpT = np.ascontiguousarray(np.asarray(L0_attn_b_weight_proj, f32).T)
    lutp = np.ascontiguousarray(
        np.broadcast_to(np.asarray(L0_ffn_lut, f32), (128, 16)))
    gidxT = np.ascontiguousarray(np.asarray(L0_ffn_gate_idx, np.int32).T)
    uidxT = np.ascontiguousarray(np.asarray(L0_ffn_up_idx, np.int32).T)
    didxT = np.ascontiguousarray(np.asarray(L0_ffn_down_idx, np.int32).T)
    ident = np.eye(128, dtype=f32)

    pp = np.arange(128)[:, None]
    cq = np.arange(BLK)[None, :]
    cmasks = []
    for c in range(NCORES):
        cm = np.zeros((NSLOT, 128, 16 * BLK), f32)
        for k in range(NSLOT):
            for m in range(16):
                valid = (m * 128 + pp) <= (2 * c * 128 + cq)
                cm[k, :, BLK * m:BLK * (m + 1)] = valid.astype(f32)
        cmasks.append(cm)

    nc = _get_prog()
    in_maps = []
    for c in range(NCORES):
        rows = np.concatenate([np.arange(256 * (8 * k + c), 256 * (8 * k + c) + 256)
                               for k in range(NSLOT)])
        in_maps.append(dict(
            x_full=x, x_own=np.ascontiguousarray(x[rows]), wqT=wqT, wkT=wkT,
            wvT=wvT, woTr=woTr, boP=boP, wvbT=wvbT, wpT=wpT, lutp=lutp,
            gidxT=gidxT, uidxT=uidxT, didxT=didxT, cmask=cmasks[c],
            ident=ident))
    res = run_bass_kernel_spmd(nc, in_maps, core_ids=list(range(NCORES)))
    _prog_cache["taps"] = res.results
    y = np.empty((SEQ, D), f32)
    for c in range(NCORES):
        yo = np.asarray(res.results[c]["y_own"])
        for k in range(NSLOT):
            b = 8 * k + c
            y[256 * b:256 * (b + 1)] = yo[256 * k:256 * (k + 1)]
    return y.reshape(1, SEQ, D)


if __name__ == "__main__":
    b = Builder()
    nc = b.build()
    print("program built OK")


# version 24 (final)
# speedup vs baseline: 1.0870x; 1.0013x over previous
import numpy as np
from contextlib import ExitStack

import concourse.bass as bass
import concourse.mybir as mybir
from concourse.bass_utils import run_bass_kernel_spmd

F32 = mybir.dt.float32
BF16 = mybir.dt.bfloat16
I32 = mybir.dt.int32
AF = mybir.ActivationFunctionType
ALU = mybir.AluOpType
AX = mybir.AxisListType

SEQ = 8192
D = 2048
DEBUG_TAPS = False
NH = 16
KVH = 4
INTER = 5504
NIT = 43
HT = 16
DT = 16
BLK = 256
NSLOT = 4
NCORES = 8
OWN = NSLOT * BLK
EPS = 1e-5
NSUP = SEQ // 512
NSUPQ = OWN // 512

MAT_PUMP_P1 = 7
MAT_SG_P1 = 5
MAT_PUMP_P2 = (4, 7, 10, 13)
MAT_SG_P2 = 4
MAT_PUMP_FFN = 6


class Sem:
    def __init__(self, nc, es, name):
        self.h = es.enter_context(nc.semaphore(name))
        self.n = 0


class Builder:
    def __init__(self):
        self.nc = bass.Bass("TRN2", target_bir_lowering=False)
        nc = self.nc
        self.es = ExitStack()
        self.tok = {}

        def din(name, shape, dt=F32):
            return nc.dram_tensor(name, shape, dt, kind="ExternalInput").ap()

        self.x_full = din("x_full", [SEQ, D])
        self.x_own = din("x_own", [OWN, D])
        self.wqT = din("wqT", [D, D])
        self.wkT = din("wkT", [D, 512])
        self.wvT = din("wvT", [D, 512])
        self.woTr = din("woTr", [DT, 128, NH * 128])
        self.boP = din("boP", [128, DT])
        self.wvbT = din("wvbT", [D, 512])
        self.wpT = din("wpT", [D, D])
        self.lutp = din("lutp", [128, 16])
        self.gidxT = din("gidxT", [D, INTER], I32)
        self.uidxT = din("uidxT", [D, INTER], I32)
        self.didxT = din("didxT", [INTER, D], I32)
        self.cmask = din("cmask", [NSLOT, 128, 16 * BLK])
        self.ident = din("ident", [128, 128])
        self.y_own = nc.dram_tensor("y_own", [OWN, D], F32,
                                    kind="ExternalOutput").ap()
        if DEBUG_TAPS:
            self.t_x1 = nc.dram_tensor("t_x1", [OWN, D], F32,
                                       kind="ExternalOutput").ap()
            self.t_x2 = nc.dram_tensor("t_x2", [OWN, D], F32,
                                       kind="ExternalOutput").ap()
            self.t_x3 = nc.dram_tensor("t_x3", [OWN, D], F32,
                                       kind="ExternalOutput").ap()
            self.t_wg = nc.dram_tensor("t_wg", [128, HT * 128], F32,
                                       kind="ExternalOutput").ap()
            self.t_wd = nc.dram_tensor("t_wd", [128, NIT * 128], F32,
                                       kind="ExternalOutput").ap()
            self.t_wu = nc.dram_tensor("t_wu", [128, HT * 128], F32,
                                       kind="ExternalOutput").ap()
            self.t_k = nc.dram_tensor("t_k", [128, 1024], F32,
                                      kind="ExternalOutput").ap()
            self.t_v = nc.dram_tensor("t_v", [128, 1024], F32,
                                      kind="ExternalOutput").ap()
            self.t_q = nc.dram_tensor("t_q", [128, 1024], F32,
                                      kind="ExternalOutput").ap()

        self.d_K = nc.dram_tensor("d_K", [KVH, 128, SEQ], BF16, kind="Internal").ap()
        self.d_V = nc.dram_tensor("d_V", [KVH, 128, SEQ], BF16,
                                  kind="Internal").ap()
        self.d_Q = nc.dram_tensor("d_Q", [NH, 128, 1024], BF16,
                                  kind="Internal").ap()
        self.d_Wg = nc.dram_tensor("d_Wg", [NIT, 128, HT * 128], BF16,
                                   kind="Internal").ap()
        self.d_Wu = nc.dram_tensor("d_Wu", [NIT, 128, HT * 128], BF16,
                                   kind="Internal").ap()
        self.d_Wd = nc.dram_tensor("d_Wd", [DT, 128, NIT * 128], BF16,
                                   kind="Internal").ap()
        self.d_X = [nc.dram_tensor(f"d_X{i}", [OWN, D], F32, kind="Internal").ap()
                    for i in (1, 2, 3)]

        def mk(name):
            return Sem(nc, self.es, name)

        def mk2(name):
            return [Sem(nc, self.es, name + "0"), Sem(nc, self.es, name + "1")]

        self.sWS = mk("sWS")
        self.sWC = mk("sWC")
        self.sX = mk2("sX")
        self.sKV = mk2("sKV")
        self.sQ2 = mk2("sQ2")
        self.sWU = mk2("sWU")
        self.sWD = mk2("sWD")
        self.sWO = mk2("sWO")
        self.sXT = mk("sXT")
        self.sG = mk2("sG")
        self.sSg = mk2("sSg")
        self.sST = mk2("sST")
        self.sXR = mk2("sXR")
        self.sFX = mk("sFX")
        self.dM2 = mk2("dM")
        self.dMs = mk2("dMs")
        self.vS = Sem(nc, self.es, "vS")
        self.sS = Sem(nc, self.es, "sS")
        self.gS = Sem(nc, self.es, "gS")
        self.tM = Sem(nc, self.es, "tM")

        def sb(name, shape, dt):
            return nc.alloc_sbuf_tensor(name, shape, dt).ap()

        self.lutP = sb("lutP", [128, 16], F32)
        self.identb = sb("identb", [128, 128], BF16)
        self.onesb = sb("onesb", [128, 1], BF16)
        self.ones1f = sb("ones1f", [1, 128], F32)
        self.epsA = sb("epsA", [128, 1], F32)
        self.negone = sb("negone", [128, 1], F32)
        self.st6 = sb("st6", [128, 24], F32)
        self.mv2 = sb("mv2", [128, 2], F32)
        self.veps = sb("veps", [128, 1], F32)
        self.ivar4 = sb("ivar4", [128, 4], F32)
        self.negm4 = sb("negm4", [128, 4], F32)
        self.rstd4 = sb("rstd4", [128, 4], F32)
        self.lnb4 = sb("lnb4", [128, 4], F32)
        self.xb = [sb(f"xb{i}", [128, D], F32) for i in range(2)]
        self.lnt = [sb(f"lnt{i}", [128, D], BF16) for i in range(4)]
        self.ksum = sb("ksum", [128, 1], F32)
        self.negmean = sb("negmean", [128, 1], F32)
        self.rL = sb("rL", [1, 256], F32)
        self.jneg = sb("jneg", [128, 16], F32)
        self.onefF = sb("onefF", [128, 1], F32)
        self.dnbf = [sb(f"dnbf{i}", [128, 512], BF16) for i in range(2)]

        self.mat_stack = ExitStack()
        self.midx = [self.mat_stack.enter_context(
            nc.sbuf_tensor(f"midx{i}", [128, INTER], BF16, side="right")).ap()
            for i in range(2)]
        self.macc = [self.mat_stack.enter_context(
            nc.sbuf_tensor(f"macc{i}", [128, INTER], BF16, side="right")).ap()
            for i in range(2)]
        self.mtj = self.mat_stack.enter_context(
            nc.sbuf_tensor("mtj", [128, INTER], BF16, side="right")).ap()
        SGW = 1408
        self.SGW = SGW
        self.msq = self.mat_stack.enter_context(
            nc.sbuf_tensor("msq", [128, SGW], BF16, side="right")).ap()
        self.msq2 = self.mat_stack.enter_context(
            nc.sbuf_tensor("msq2", [128, SGW], BF16, side="right")).ap()
        self.mind = self.mat_stack.enter_context(
            nc.sbuf_tensor("mind", [128, 2 * SGW], BF16, side="right")).ap()
        self.midx_sg = self.mat_stack.enter_context(
            nc.sbuf_tensor("midx_sg", [128, 2 * SGW], BF16, side="right")).ap()
        self.macc_sg = self.mat_stack.enter_context(
            nc.sbuf_tensor("macc_sg", [128, 2 * SGW], BF16, side="right")).ap()

        def psu(name, shape, dt=F32):
            return nc.alloc_psum_tensor(name, shape, dt).ap()

        self.psA = psu("psA", [128, 512])
        self.psB = psu("psB", [128, 512])
        self.psC = psu("psC", [128, 512])
        self.psD = psu("psD", [128, 512])
        self.psE = psu("psE", [128, 512])
        self.psF = psu("psF", [128, 512])
        self.psL = self.psF[0:1, 0:256]
        self.psT = [psu(f"psT{i}", [128, 512]) for i in range(2)]

        self.mat_jobs = ([("g", i) for i in range(HT)]
                         + [("u", i) for i in range(HT)]
                         + [("d", i) for i in range(NIT)])
        SG_JOBS = {30, 31, 71, 72, 73, 74}
        self.vg_jobs = set()
        self.mat = dict(loads=0, load_tok={}, store_tok={}, done={},
                        cap=len(self.mat_jobs), sg_jobs=SG_JOBS,
                        vq=[i for i in range(len(self.mat_jobs))
                            if i not in SG_JOBS],
                        vqi=0, vj=0, buf_of={}, vg_tok={},
                        squnits=[], squi=0, sj=0, sg_ltok={}, sg_sttok={},
                        sg_stok={})
        for ji in sorted(SG_JOBS):
            kind, idx = self.mat_jobs[ji]
            W = INTER if kind in ("g", "u") else D
            qs_ = []
            c = 0
            while c < W:
                w = min(1408, W - c)
                qs_.append((ji, c, w))
                c += w
            self.mat["squnits"].extend(qs_)

    def dma(self, q, out, in_, sem):
        q.dma_start(out=out, in_=in_).then_inc(sem.h, 16)
        sem.n += 16
        return (sem, sem.n)

    def dmaw(self, eng, tokv):
        eng.wait_ge(tokv[0].h, tokv[1])

    def store_sems(self):
        return [self.sST[0], self.sST[1], self.sXR[0], self.sXR[1], self.sFX,
                self.dM2[0], self.dM2[1], self.dMs[0], self.dMs[1]]

    def store_snapshot(self):
        return [(s, s.n) for s in self.store_sems() if s.n > 0]

    def vchain(self, v, emit):
        v.wait_ge(self.vS.h, self.vS.n)
        emit().then_inc(self.vS.h, 1)
        self.vS.n += 1
        return self.vS.n

    def vind(self, v, emit):
        emit().then_inc(self.vS.h, 1)
        self.vS.n += 1
        return self.vS.n

    def schain(self, s, emit):
        s.wait_ge(self.sS.h, self.sS.n)
        emit().then_inc(self.sS.h, 1)
        self.sS.n += 1
        return self.sS.n

    def sind(self, s, emit):
        emit().then_inc(self.sS.h, 1)
        self.sS.n += 1
        return self.sS.n

    def mat_src(self, kind, idx):
        if kind == "g":
            dst = self.d_Wg[:, :, 128 * idx:128 * (idx + 1)].rearrange(
                "it p f -> p it f")
            return (self.gidxT[128 * idx:128 * (idx + 1), :], INTER, dst)
        if kind == "u":
            dst = self.d_Wu[:, :, 128 * idx:128 * (idx + 1)].rearrange(
                "it p f -> p it f")
            return (self.uidxT[128 * idx:128 * (idx + 1), :], INTER, dst)
        dst = self.d_Wd[:, :, 128 * idx:128 * (idx + 1)].rearrange(
            "d p f -> p d f")
        return (self.didxT[128 * idx:128 * (idx + 1), :], D, dst)

    def mat_gpsimd_service(self, g):
        m = self.mat
        for ji in sorted(m["done"].keys()):
            if ji in m["store_tok"]:
                continue
            kind, idx = self.mat_jobs[ji]
            _, w, dst = self.mat_src(kind, idx)
            sem, val = m["done"][ji]
            g.wait_ge(sem.h, val)
            bp = m["buf_of"][ji] % 2
            m["store_tok"][ji] = self.dma(
                g, dst, self.macc[bp][:, 0:w].rearrange("p (a f) -> p a f",
                                                        f=128), self.dM2[bp])
        while m["loads"] < m["cap"]:
            ji = m["vq"][m["loads"]] if m["loads"] < len(m["vq"]) else None
            if ji is None:
                break
            prev2 = m["vq"][m["loads"] - 2] if m["loads"] >= 2 else None
            prev4 = m["vq"][m["loads"] - 4] if m["loads"] >= 4 else None
            if prev2 is not None and prev2 not in m["done"]:
                break
            if prev4 is not None and prev4 not in m["store_tok"]:
                break
            kind, idx = self.mat_jobs[ji]
            srcap, w, _ = self.mat_src(kind, idx)
            if prev2 is not None:
                sem, val = m["done"][prev2]
                g.wait_ge(sem.h, val)
            if m.get("stage2_guard") and not m.get("stage2_guard_done"):
                for sv in m["stage2_guard"]:
                    self.dmaw(g, sv)
                m["stage2_guard_done"] = True
            li = m["loads"]
            m["load_tok"][ji] = self.dma(g, self.midx[li % 2][:, 0:w], srcap,
                                         self.sG[li % 2])
            m["buf_of"][ji] = li % 2
            m["loads"] += 1

    def mat_pump_drv(self, blk, nj):
        m = self.mat
        left = nj
        while left > 0 and m["vqi"] < len(m["vq"]):
            ji = m["vq"][m["vqi"]]
            if ji not in m["load_tok"]:
                break
            kind, idx = self.mat_jobs[ji]
            _, w, _ = self.mat_src(kind, idx)
            b = m["buf_of"][ji]
            hw = w // 2
            vg = ji in self.vg_jobs
            j0 = m["vj"]
            j1 = min(16, j0 + min(left, 1 if vg else left))
            left -= (j1 - j0)

            @blk.vector
            def _(v, ji=ji, j0=j0, j1=j1, b=b, hw=hw, vg=vg, w=w):
                if j0 == 0:
                    self.dmaw(v, m["load_tok"][ji])
                    if "consts" in self.tok:
                        self.dmaw(v, self.tok["consts"])
                    prev = m["vq"][m["vqi"] - 1] if m["vqi"] >= 1 else None
                    pd = m["done"].get(prev) if prev is not None else None
                    if pd is not None and pd[0] is self.gS:
                        v.wait_ge(self.gS.h, pd[1])
                    if not vg:
                        prev2 = (m["vq"][m["vqi"] - 2]
                                 if m["vqi"] >= 2 else None)
                        stok = (m["store_tok"].get(prev2)
                                if prev2 is not None else None)
                        if stok is not None:
                            self.dmaw(v, stok)
                        self.vchain(v, lambda: v.memset(self.macc[b][:, 0:w],
                                                        0.0))
                for j in range(j0, j1):
                    for hf in range(2):
                        sl = slice(hw * hf, hw * (hf + 1))
                        if vg:
                            gtok = m["vg_tok"].get(("g", ji, 2 * j + hf - 2))
                            if gtok:
                                v.wait_ge(self.gS.h, gtok)
                            ms = self.mtj[:, hw * ((2 * j + hf) % 2):
                                          hw * ((2 * j + hf) % 2) + hw]
                            self.vchain(v, lambda j=j, sl=sl, ms=ms:
                                        v.tensor_scalar(
                                            ms, self.midx[b][:, sl], float(j),
                                            self.lutP[:, j:j + 1],
                                            ALU.is_equal, ALU.mult))
                            m["vg_tok"][("v", ji, 2 * j + hf)] = self.vS.n
                        else:
                            self.vchain(v, lambda j=j, sl=sl: v.tensor_scalar(
                                self.mtj[:, 0:hw], self.midx[b][:, sl],
                                float(j), self.lutP[:, j:j + 1],
                                ALU.is_equal, ALU.mult))
                            self.vchain(v, lambda sl=sl: v.tensor_tensor(
                                self.macc[b][:, sl], self.macc[b][:, sl],
                                self.mtj[:, 0:hw], ALU.add))

            if vg:
                @blk.gpsimd
                def _(g, ji=ji, j0=j0, j1=j1, b=b, hw=hw, w=w):
                    if j0 == 0:
                        self.dmaw(g, m["load_tok"][ji])
                        prev2 = m["vq"][m["vqi"] - 2] if m["vqi"] >= 2 else None
                        stok = (m["store_tok"].get(prev2)
                                if prev2 is not None else None)
                        if stok is not None:
                            self.dmaw(g, stok)
                    for j in range(j0, j1):
                        for hf in range(2):
                            sl = slice(hw * hf, hw * (hf + 1))
                            g.wait_ge(self.vS.h, m["vg_tok"][("v", ji, 2 * j + hf)])
                            g.wait_ge(self.gS.h, self.gS.n)
                            ms = self.mtj[:, hw * ((2 * j + hf) % 2):
                                          hw * ((2 * j + hf) % 2) + hw]
                            if j == 0:
                                g.tensor_copy(self.macc[b][:, sl],
                                              ms).then_inc(self.gS.h, 1)
                            else:
                                g.tensor_tensor(self.macc[b][:, sl],
                                                self.macc[b][:, sl], ms,
                                                ALU.add).then_inc(self.gS.h, 1)
                            self.gS.n += 1
                            m["vg_tok"][("g", ji, 2 * j + hf)] = self.gS.n

            m["vj"] = j1
            if j1 == 16:
                if vg:
                    m["done"][ji] = (self.gS, self.gS.n)
                else:
                    m["done"][ji] = (self.vS, self.vS.n)
                m["vqi"] += 1
                m["vj"] = 0

    def mat_pump_sg(self, blk, nj):
        m = self.mat
        left = nj
        while left > 0:
            if m["squi"] >= len(m["squnits"]):
                return
            ui = m["squi"]
            ji, c0, w = m["squnits"][ui]
            kind, idx = self.mat_jobs[ji]
            srcap, _, dstap = self.mat_src(kind, idx)
            slot = ui % 2
            if ui not in m["sg_ltok"]:
                @blk.gpsimd
                def _(g, ui=ui):
                    for u2 in (ui, ui + 1):
                        if u2 >= len(m["squnits"]) or u2 in m["sg_ltok"]:
                            continue
                        ji2, c2, w2 = m["squnits"][u2]
                        k2, i2 = self.mat_jobs[ji2]
                        src2, _, _ = self.mat_src(k2, i2)
                        st = m["sg_sttok"].get(u2 - 2)
                        if st is not None:
                            self.dmaw(g, st)
                        ra = m["sg_sttok"].get("realloc")
                        if ra is not None and not m.get(f"sg_ra_done_{u2 % 2}"):
                            for sv in ra:
                                self.dmaw(g, sv)
                            m[f"sg_ra_done_{u2 % 2}"] = True
                        m["sg_ltok"][u2] = self.dma(
                            g, self.midx_sg[:, self.SGW * (u2 % 2):
                                            self.SGW * (u2 % 2) + w2],
                            src2[:, c2:c2 + w2], self.sSg[u2 % 2])
            j0 = m["sj"]
            j1 = min(16, j0 + min(left, 2))
            left -= (j1 - j0)

            @blk.scalar
            def _(s, ui=ui, ji=ji, j0=j0, j1=j1, w=w, slot=slot):
                for j in range(j0, j1):
                    if j == 0 and j0 == 0:
                        self.dmaw(s, m["sg_ltok"][ui])
                        s.wait_ge(self.vS.h, self.tok["consts_v"])
                        self.dmaw(s, self.tok["consts"])
                        s.wait_ge(self.gS.h, self.gS.n)
                    gtok = m["sg_stok"].get(("g", ui, j - 2))
                    if gtok:
                        s.wait_ge(self.gS.h, gtok)
                    mi = self.midx_sg[:, self.SGW * slot:self.SGW * slot + w]
                    self.schain(s, lambda j=j, mi=mi: s.activation(
                        self.msq[:, 0:w], mi, AF.Square,
                        bias=self.jneg[:, j:j + 1]))
                    self.schain(s, lambda: s.activation(
                        self.msq2[:, 0:w], self.msq[:, 0:w],
                        AF.Relu, bias=self.onefF[:, :], scale=-1.0))
                    self.schain(s, lambda j=j: s.activation(
                        self.mind[:, self.SGW * (j % 2):self.SGW * (j % 2) + w],
                        self.msq2[:, 0:w], AF.Identity,
                        scale=self.lutP[:, j:j + 1]))
                    m["sg_stok"][("s", ui, j)] = self.sS.n

            @blk.gpsimd
            def _(g, ui=ui, j0=j0, j1=j1, w=w, slot=slot):
                if j0 == 0:
                    self.dmaw(g, m["sg_ltok"][ui])
                    st = m["sg_sttok"].get(ui - 2)
                    if st is not None:
                        self.dmaw(g, st)
                    g.wait_ge(self.gS.h, self.gS.n)
                    g.memset(self.macc_sg[:, self.SGW * slot:
                                          self.SGW * slot + w],
                             0.0).then_inc(self.gS.h, 1)
                    self.gS.n += 1
                for j in range(j0, j1):
                    g.wait_ge(self.sS.h, m["sg_stok"][("s", ui, j)])
                    g.wait_ge(self.gS.h, self.gS.n)
                    g.tensor_tensor(
                        self.macc_sg[:, self.SGW * slot:self.SGW * slot + w],
                        self.macc_sg[:, self.SGW * slot:self.SGW * slot + w],
                        self.mind[:, self.SGW * (j % 2):self.SGW * (j % 2) + w],
                        ALU.add).then_inc(self.gS.h, 1)
                    self.gS.n += 1
                    m["sg_stok"][("g", ui, j)] = self.gS.n
                if j1 == 16:
                    g.wait_ge(self.gS.h, self.gS.n)
                    dst = dstap[:, c0 // 128:(c0 + w) // 128, :] \
                        if False else None
                    kindq, idxq = self.mat_jobs[ji]
                    _, _, dfull = self.mat_src(kindq, idxq)
                    m["sg_sttok"][ui] = self.dma(
                        g, dfull[:, c0 // 128:(c0 + w) // 128, :],
                        self.macc_sg[:, self.SGW * slot:self.SGW * slot + w
                                     ].rearrange("p (a f) -> p a f", f=128),
                        self.dMs[ui % 2])
                    if ui + 1 >= len(m["squnits"]) or \
                            m["squnits"][ui + 1][0] != ji:
                        m["store_tok"][ji] = m["sg_sttok"][ui]

            m["sj"] = j1
            if j1 == 16:
                m["squi"] += 1
                m["sj"] = 0

    def mat_pump_section(self, blk, nj, nsg=0):
        @blk.gpsimd
        def _(g):
            self.mat_gpsimd_service(g)

        if nsg:
            self.mat_pump_sg(blk, nsg)
        self.mat_pump_drv(blk, nj)

    def mat_drain(self, blk, upto=None):
        if upto is None:
            upto = len(self.mat_jobs)
        guard = 0
        while any(ji not in self.mat["store_tok"] for ji in range(upto)):
            self.mat_pump_section(blk, 48, nsg=16)
            guard += 1
            assert guard < 900, "mat drain stuck"

    def mat_stage2(self, blk):
        m = self.mat
        self.mat["cap"] = max(self.mat["loads"], 1)
        guard = 0
        gu_jobs = [ji for ji in range(2 * HT)]
        loaded = set(list(m["load_tok"].keys()))
        sg_started = set(u[0] for i, u in enumerate(m["squnits"])
                         if i <= m["squi"] or i in m["sg_ltok"])
        need = set(gu_jobs) | loaded | sg_started
        while any(ji not in m["store_tok"] for ji in need):
            self.mat_pump_section(blk, 48, nsg=16)
            loaded = set(list(m["load_tok"].keys()))
            sg_started = set(u[0] for i, u in enumerate(m["squnits"])
                             if i <= m["squi"] or i in m["sg_ltok"])
            need = set(gu_jobs) | loaded | sg_started
            guard += 1
            assert guard < 900, "mat stage2 drain stuck"
        m["stage2_guard"] = [(self.dM2[0], self.dM2[0].n),
                             (self.dM2[1], self.dM2[1].n),
                             (self.dMs[0], self.dMs[0].n),
                             (self.dMs[1], self.dMs[1].n)]
        m["stage2_guard_done"] = False
        self.mat_stack.close()
        nc = self.nc
        self.mat_stack = ExitStack()
        self.midx = [self.mat_stack.enter_context(
            nc.sbuf_tensor(f"midxB{i}", [128, D], BF16, side="right")).ap()
            for i in range(2)]
        self.macc = [self.mat_stack.enter_context(
            nc.sbuf_tensor(f"maccB{i}", [128, D], BF16, side="right")).ap()
            for i in range(2)]
        self.mtj = self.mat_stack.enter_context(
            nc.sbuf_tensor("mtjB", [128, D], BF16, side="right")).ap()
        SGW = 512
        self.SGW = SGW
        self.msq = self.mat_stack.enter_context(
            nc.sbuf_tensor("msqB", [128, SGW], BF16, side="right")).ap()
        self.msq2 = self.mat_stack.enter_context(
            nc.sbuf_tensor("msq2B", [128, SGW], BF16, side="right")).ap()
        self.mind = self.mat_stack.enter_context(
            nc.sbuf_tensor("mindB", [128, 2 * SGW], BF16, side="right")).ap()
        self.midx_sg = self.mat_stack.enter_context(
            nc.sbuf_tensor("midx_sgB", [128, 2 * SGW], BF16, side="right")).ap()
        self.macc_sg = self.mat_stack.enter_context(
            nc.sbuf_tensor("macc_sgB", [128, 2 * SGW], BF16, side="right")).ap()
        self.mat["cap"] = len(self.mat_jobs)

    def mat_store_tok(self, last_job):
        best = {}
        for ji in range(last_job + 1):
            t = self.mat["store_tok"].get(ji)
            assert t is not None, "mat job not stored before use"
            sem, val = t
            best[id(sem)] = (sem, max(val, best.get(id(sem), (sem, 0))[1]))
        return list(best.values())

    def ln_vector(self, v, xin, col, extra_waits=()):
        for sem, val in extra_waits:
            if val:
                v.wait_ge(sem.h, val)
        self.vchain(v, lambda: v.bn_stats(self.st6[:, 0:6], xin[:, 0:512]))
        for q in range(1, 4):
            self.vind(v, lambda q=q: v.bn_stats(
                self.st6[:, 6 * q:6 * q + 6], xin[:, 512 * q:512 * (q + 1)]))
        self.vchain(v, lambda: v.bn_aggr(self.mv2[:, :], self.st6[:, :]))
        self.vchain(v, lambda: v.tensor_scalar(
            self.ivar4[:, col:col + 1], self.mv2[:, 1:2], self.epsA[:, :],
            None, ALU.add))
        return self.vchain(v, lambda: v.tensor_scalar(
            self.negm4[:, col:col + 1], self.mv2[:, 0:1], self.negone[:, :],
            None, ALU.mult))

    def ln_scalar(self, s, xin, lnout, col, vtok, extra_waits=()):
        s.wait_ge(self.vS.h, vtok)
        for sem, val in extra_waits:
            if val:
                s.wait_ge(sem.h, val)
        self.schain(s, lambda: s.activation(
            self.veps[:, col % 1:col % 1 + 1], self.ivar4[:, col:col + 1], AF.Ln))
        self.schain(s, lambda: s.activation(
            self.rstd4[:, col:col + 1], self.veps[:, 0:1], AF.Exp, scale=-0.5))
        self.schain(s, lambda: s.activation(
            self.lnb4[:, col:col + 1], self.negm4[:, col:col + 1], AF.Identity,
            scale=self.rstd4[:, col:col + 1]))
        return self.schain(s, lambda: s.activation(
            lnout[:, :], xin[:, :], AF.Identity,
            bias=self.lnb4[:, col:col + 1], scale=self.rstd4[:, col:col + 1]))

    def emit_ln_tile(self, blk, pre, t, xsrc_fn, dstT, dst_stride, first_waits):
        tok = self.tok

        @blk.sync
        def _(sy, t=t):
            wt = tok.get(f"{pre}_lnap_{t - 2}")
            if wt:
                sy.wait_ge(self.sS.h, wt)
            for sem, val in first_waits:
                sy.wait_ge(sem.h, val)
            tok[f"{pre}_xl_{t}"] = self.dma(sy, self.xb[t % 2][:, :],
                                            xsrc_fn(t), self.sX[t % 2])

        @blk.vector
        def _(v, t=t):
            waits = [tok[f"{pre}_xl_{t}"],
                     (self.sS, tok.get(f"{pre}_lnap_{t - 4}", 0))]
            tok[f"{pre}_lnv_{t}"] = self.ln_vector(v, self.xb[t % 2], t % 4,
                                                   extra_waits=waits)

        @blk.scalar
        def _(s, t=t):
            waits = [(self.tM, tok.get(f"{pre}_tp_{t - 4}", 0))]
            tok[f"{pre}_lnap_{t}"] = self.ln_scalar(
                s, self.xb[t % 2], self.lnt[t % 4], t % 4,
                tok[f"{pre}_lnv_{t}"], extra_waits=waits)

        t_off = t % 4 if dst_stride == 512 else t
        tb = [self.psT[0], self.psT[1], self.psA, self.psB]

        @blk.tensor
        def _(tn, t=t):
            tn.wait_ge(self.sS.h, tok[f"{pre}_lnap_{t}"])
            if t == 0:
                self.dmaw(tn, tok["identb"])
            for gb in range(4):
                wt = tok.get(f"{pre}_tpev_prev_{gb}")
                if wt:
                    tn.wait_ge(self.vS.h, wt)
                for k in range(4):
                    h = 4 * gb + k
                    ins = tn.matmul(tb[gb][:, 128 * k:128 * (k + 1)],
                                    self.lnt[t % 4][:, 128 * h:128 * (h + 1)],
                                    self.identb[:, :], start=True, stop=True)
                ins.then_inc(self.tM.h, 1)
                self.tM.n += 1
                tok[f"{pre}_tp_{t}_{gb}"] = self.tM.n

        @blk.vector
        def _(v, t=t, t_off=t_off):
            for gb in range(4):
                v.wait_ge(self.tM.h, tok[f"{pre}_tp_{t}_{gb}"])
                dst = dstT[:, :].rearrange("p (h f) -> p h f", f=dst_stride)[
                    :, 4 * gb:4 * gb + 4, 128 * t_off:128 * (t_off + 1)]
                tok[f"{pre}_tpev_prev_{gb}"] = self.vind(
                    v, lambda dst=dst, gb=gb: v.tensor_copy(
                        dst, tb[gb][:, :].rearrange("p (a b) -> p a b", b=128)))
        tok[f"{pre}_tp_{t}"] = self.tM.n
        tok[f"{pre}_tpev_{t}"] = self.vS.n

    def build(self):
        nc = self.nc
        tok = self.tok
        blk_cm = nc.Block()
        blk = blk_cm.__enter__()

        p1 = ExitStack()
        wkTb = p1.enter_context(nc.sbuf_tensor("wkTb", [128, HT * 512], BF16)).ap()
        wvTb = p1.enter_context(nc.sbuf_tensor("wvTb", [128, HT * 512], BF16)).ap()
        wqTb = p1.enter_context(nc.sbuf_tensor("wqTb", [128, HT * 512], BF16)).ap()
        lnT = [p1.enter_context(nc.sbuf_tensor(f"lnT{i}", [128, HT * 512],
                                               BF16)).ap() for i in range(2)]
        kvst = [p1.enter_context(nc.sbuf_tensor(f"kvst{i}", [128, 512],
                                                BF16)).ap() for i in range(2)]
        self.lnT, self.wkTb, self.wvTb, self.wqTb = lnT, wkTb, wvTb, wqTb
        self.kvst = kvst

        @blk.gpsimd
        def _(g):
            self.dma(g, self.identb[:, :], self.ident, self.sWS)
            self.dma(g, wkTb[:, :].rearrange("p (h f) -> p h f", f=512),
                     self.wkT.rearrange("(h p) f -> p h f", p=128), self.sWS)
            self.dma(g, wvTb[:, :].rearrange("p (h f) -> p h f", f=512),
                     self.wvT.rearrange("(h p) f -> p h f", p=128), self.sWS)
            self.dma(g, wqTb[:, :].rearrange("p (h f) -> p h f", f=512),
                     self.wqT[:, 0:512].rearrange("(h p) f -> p h f", p=128),
                     self.sWS)
            tok["identb"] = (self.sWS, self.sWS.n)
            tok["w_kv"] = (self.sWS, self.sWS.n)
            tok["wq_q0"] = (self.sWS, self.sWS.n)
            self.mat_gpsimd_service(g)

        @blk.sync
        def _(sy):
            tok["consts"] = self.dma(sy, self.lutP[:, :], self.lutp, self.sWC)

        @blk.vector
        def _(v):
            self.dmaw(v, tok["consts"])
            v.memset(self.epsA[:, :], EPS)
            v.memset(self.negone[:, :], -1.0)
            self.vind(v, lambda: v.memset(self.onesb[:, :], 1.0))
            self.vind(v, lambda: v.memset(self.ones1f[:, :], 1.0))
            self.vind(v, lambda: v.memset(self.onefF[:, :], 1.0))
            for j in range(16):
                self.vind(v, lambda j=j: v.memset(self.jneg[:, j:j + 1],
                                                  float(-j)))
            tok["consts_v"] = self.vS.n

        NT_FULL = NSUP * 4

        def x_src(t):
            if t < NT_FULL:
                return self.x_full[128 * t:128 * (t + 1), :]
            tt = t - NT_FULL
            return self.x_own[128 * tt:128 * (tt + 1), :]

        for su in range(NSUP + NSUPQ):
            for i in range(4):
                t = 4 * su + i
                fw = [tok["consts"]] if t == 0 else []
                self.emit_ln_tile(blk, "p1", t, x_src, lnT[su % 2], 512, fw)
            if su >= 1:
                self.emit_proj_supertile(blk, su - 1)
            self.mat_pump_section(blk, MAT_PUMP_P1, nsg=MAT_SG_P1)
        self.emit_proj_supertile(blk, NSUP + NSUPQ - 1)

        self.emit_q_pass(blk, wqTb)
        self.mat_pump_section(blk, MAT_PUMP_P1)

        p1.close()
        p2 = ExitStack()
        Ksl = [p2.enter_context(nc.sbuf_tensor(f"Ksl{i}", [128, SEQ], BF16)).ap()
               for i in range(2)]
        Vsl = [p2.enter_context(nc.sbuf_tensor(f"Vsl{i}", [128, SEQ], BF16)).ap()
               for i in range(2)]
        PT = [p2.enter_context(nc.sbuf_tensor(f"PT{i}", [128, 512], BF16)).ap()
              for i in range(2)]
        cmb = p2.enter_context(nc.sbuf_tensor("cmb", [128, 16 * BLK], BF16)).ap()
        aoT = p2.enter_context(nc.sbuf_tensor("aoT", [128, NH * 256], BF16)).ap()
        woS = [p2.enter_context(nc.sbuf_tensor(f"woS{i}", [128, HT * 128],
                                               BF16)).ap() for i in range(2)]
        qs = [p2.enter_context(nc.sbuf_tensor(f"qs{i}", [128, 256], BF16)).ap()
              for i in range(2)]
        self.avf = p2.enter_context(nc.sbuf_tensor("avf", [128, 256], F32)).ap()
        self.boS = p2.enter_context(nc.sbuf_tensor("boS", [128, DT], F32)).ap()
        self.opbf = p2.enter_context(nc.sbuf_tensor("opbf", [128, 256],
                                                    BF16)).ap()
        self.Ksl, self.Vsl, self.PT, self.cmb, self.aoT, self.woS = \
            Ksl, Vsl, PT, cmb, aoT, woS
        self.qs = qs
        self.hseq = 0
        self.slast = {}

        j_t, j_s, j_v = self.tM.n, self.sS.n, self.vS.n
        self.p2_join = (j_t, j_s, j_v)
        self.p2_jdO = self.store_snapshot()
        prev_ao = None

        @blk.sync
        def _(sy):
            sy.wait_ge(self.sS.h, j_s)
            for sv in self.p2_jdO:
                self.dmaw(sy, sv)
            tok["bo_ld"] = self.dma(sy, self.boS[:, :], self.boP, self.sWC)

        for slot in range(NSLOT):
            nkb = 16 * (slot + 1)

            @blk.gpsimd
            def _(g, slot=slot):
                if slot > 0:
                    g.wait_ge(self.tM.h, tok[f"att_done_{4 * slot - 1}"])
                else:
                    g.wait_ge(self.tM.h, j_t)
                    g.wait_ge(self.sS.h, j_s)
                tok[f"cmask_{slot}"] = self.dma(g, cmb[:, :],
                                                self.cmask[slot, :, :], self.sWS)
                self.mat_gpsimd_service(g)

            for kv in range(KVH):
                gidx = slot * KVH + kv
                par = gidx % 2

                @blk.sync
                def _(sy, kv=kv, gidx=gidx, par=par, nkb=nkb):
                    wt = tok.get(f"att_done_{gidx - 2}")
                    if wt:
                        sy.wait_ge(self.tM.h, wt)
                    else:
                        sy.wait_ge(self.tM.h, j_t)
                        sy.wait_ge(self.sS.h, j_s)
                        for sv in self.p2_jdO:
                            self.dmaw(sy, sv)
                    self.dma(sy, Ksl[par][:, 0:128 * nkb],
                             self.d_K[kv, :, 0:128 * nkb], self.sKV[par])
                    tok[f"kvload_{gidx}"] = self.dma(
                        sy, Vsl[par][:, 0:128 * nkb],
                        self.d_V[kv, :, 0:128 * nkb], self.sKV[par])

                @blk.vector
                def _(v, gidx=gidx, par=par, nkb=nkb):
                    self.dmaw(v, tok[f"kvload_{gidx}"])
                    self.vchain(v, lambda: v.reduce_sum(
                        self.ksum[:, :], Ksl[par][:, 0:128 * nkb], axis=AX.X))
                    self.vchain(v, lambda: v.tensor_scalar(
                        self.negmean[:, :], self.ksum[:, :],
                        -1.0 / (128 * nkb), None, ALU.mult))
                    tok[f"kshift_{gidx}"] = self.vchain(
                        v, lambda: v.tensor_scalar(
                            Ksl[par][:, 0:128 * nkb], Ksl[par][:, 0:128 * nkb],
                            self.negmean[:, :], None, ALU.add))

                for hh in range(4):
                    h = 4 * kv + hh
                    prev_ao = self.emit_attention_head(
                        blk, slot, kv, h, gidx, par, nkb, prev_ao)
                    @blk.gpsimd
                    def _(g):
                        self.mat_gpsimd_service(g)

                    if MAT_PUMP_P2[slot]:
                        self.mat_pump_drv(blk, MAT_PUMP_P2[slot])
                    if MAT_SG_P2:
                        self.mat_pump_sg(blk, MAT_SG_P2)
                tok[f"att_done_{gidx}"] = self.tM.n

            self.emit_attn_outproj(blk, slot)

        p2.close()
        self.mat_stage2(blk)
        self.emit_ffn(blk, self.d_X[0], self.d_X[1], 1)

        self.emit_attn_b(blk)

        self.emit_ffn(blk, self.d_X[2], None, 2)

        if DEBUG_TAPS:
            @blk.gpsimd
            def _(g):
                for sv in self.store_snapshot():
                    self.dmaw(g, sv)
                for (dst, srcap) in [
                        (self.t_x1, self.d_X[0]), (self.t_x2, self.d_X[1]),
                        (self.t_x3, self.d_X[2]),
                        (self.t_wg, self.d_Wg[0, :, :]),
                        (self.t_wd, self.d_Wd[0, :, :]),
                        (self.t_wu, self.d_Wu[0, :, :]),
                        (self.t_k, self.d_K[0, :, 0:1024]),
                        (self.t_v, self.d_V[0, :, 0:1024]),
                        (self.t_q, self.d_Q[0, :, 0:1024])]:
                    self.dma(g, dst, srcap, self.sFX)

        @blk.sync
        def _(sy):
            for sv in self.store_snapshot():
                self.dmaw(sy, sv)

        blk_cm.__exit__(None, None, None)
        self.es.close()
        return nc

    def emit_proj_supertile(self, blk, su):
        tok = self.tok
        own = su >= NSUP
        if own:
            return
        src = self.lnT[su % 2]
        groups = [("k", f) for f in range(4)] + [("v", i) for i in range(4)]

        for gi in range(0, len(groups), 2):
            pair = groups[gi:gi + 2]

            @blk.tensor
            def _(tn, pair=pair, su=su, gi=gi, src=src, own=own):
                tn.wait_ge(self.vS.h, tok[f"p1_tpev_{4 * su + 3}"])
                if gi == 0 and su == 0:
                    self.dmaw(tn, tok["w_kv"])
                for pi, (kind, f) in enumerate(pair):
                    psd = (self.psC, self.psD)[pi]
                    wt = tok.get(f"pj_ev_{su}_{gi - 2 + pi}")
                    if wt is None and su > 0:
                        wt = tok.get(f"pj_ev_{su - 1}_{len(groups) - 2 + pi}")
                    if wt:
                        tn.wait_ge(self.sS.h, wt)
                    for ht in range(HT):
                        if kind == "k":
                            lhs = self.wkTb[:, 512 * ht + 128 * f:
                                            512 * ht + 128 * (f + 1)]
                            rhs = src[:, 512 * ht:512 * (ht + 1)]
                        else:
                            lhs = src[:, 512 * ht + 128 * f:512 * ht + 128 * (f + 1)]
                            rhs = self.wvTb[:, 512 * ht:512 * (ht + 1)]
                        ins = tn.matmul(psd[:, :], lhs, rhs, start=(ht == 0),
                                        stop=(ht == HT - 1))
                    ins.then_inc(self.tM.h, 1)
                    self.tM.n += 1
                    tok[f"pj_mm_{su}_{gi + pi}"] = self.tM.n

            @blk.scalar
            def _(s, pair=pair, su=su, gi=gi):
                for pi, (kind, f) in enumerate(pair):
                    psd = (self.psC, self.psD)[pi]
                    s.wait_ge(self.tM.h, tok[f"pj_mm_{su}_{gi + pi}"])
                    st = self.kvst[(gi + pi) % 2]
                    wt = tok.get(f"pj_st_{su}_{gi - 2 + pi}")
                    if wt is None and su > 0:
                        wt = tok.get(f"pj_st_{su - 1}_{gi + pi + 6}")
                    if wt:
                        self.dmaw(s, wt)
                    tok[f"pj_ev_{su}_{gi + pi}"] = self.sind(
                        s, lambda st=st, psd=psd: s.copy(st[:, :], psd[:, :]))

            if True:
                @blk.sync
                def _(sy, pair=pair, su=su, gi=gi):
                    for pi, (kind, f) in enumerate(pair):
                        sy.wait_ge(self.sS.h, tok[f"pj_ev_{su}_{gi + pi}"])
                        st = self.kvst[(gi + pi) % 2]
                        sem = self.sST[(gi + pi) % 2]
                        if kind == "k":
                            tok[f"pj_st_{su}_{gi + pi}"] = self.dma(
                                sy, self.d_K[f, :, 512 * su:512 * (su + 1)],
                                st[:, :], sem)
                        else:
                            t = 4 * su + f
                            tok[f"pj_st_{su}_{gi + pi}"] = self.dma(
                                sy,
                                self.d_V[:, :, 128 * t:128 * (t + 1)].rearrange(
                                    "kv p f -> p kv f"),
                                st[:, :].rearrange("p (kv f) -> p kv f", f=128),
                                sem)



    def emit_q_pass(self, blk, wqTb):
        tok = self.tok
        for q in range(4):
            if q > 0:
                @blk.gpsimd
                def _(g, q=q):
                    g.wait_ge(self.tM.h, tok[f"q_mm_{q - 1}_last"])
                    tok[f"wq_q{q}"] = self.dma(
                        g, wqTb[:, :].rearrange("p (h f) -> p h f", f=512),
                        self.wqT[:, 512 * q:512 * (q + 1)].rearrange(
                            "(h p) f -> p h f", p=128), self.sWS)
                    self.mat_gpsimd_service(g)
            for su2 in range(2):
                su = NSUP + su2
                for jr in range(2):
                    @blk.tensor
                    def _(tn, q=q, su2=su2, su=su, jr=jr):
                        if su2 == 0 and jr == 0:
                            self.dmaw(tn, tok[f"wq_q{q}"])
                            if q == 0:
                                tn.wait_ge(self.vS.h,
                                           tok[f"p1_tpev_{4 * (NSUP + 1) + 3}"])
                                tn.wait_ge(self.sS.h,
                                           tok[f"pj_ev_{NSUP - 1}_7"])
                        src = self.lnT[su % 2]
                        for pi in range(2):
                            j = 2 * jr + pi
                            psd = (self.psC, self.psD)[pi]
                            wt = tok.get(f"qp_ev_{q}_{su2}_{jr - 1}_{pi}")
                            if wt is None:
                                wt = tok.get(f"qp_ev_{q}_{su2 - 1}_{1}_{pi}")
                            if wt is None:
                                wt = tok.get(f"qp_ev_{q - 1}_1_1_{pi}")
                            if wt:
                                tn.wait_ge(self.sS.h, wt)
                            for ht in range(HT):
                                lhs = wqTb[:, 512 * ht + 128 * j:
                                           512 * ht + 128 * (j + 1)]
                                rhs = src[:, 512 * ht:512 * (ht + 1)]
                                ins = tn.matmul(psd[:, :], lhs, rhs,
                                                start=(ht == 0),
                                                stop=(ht == HT - 1))
                            ins.then_inc(self.tM.h, 1)
                            self.tM.n += 1
                            tok[f"qp_mm_{q}_{su2}_{jr}_{pi}"] = self.tM.n
                        tok[f"q_mm_{q}_last"] = self.tM.n

                    @blk.scalar
                    def _(s, q=q, su2=su2, jr=jr):
                        for pi in range(2):
                            j = 2 * jr + pi
                            psd = (self.psC, self.psD)[pi]
                            s.wait_ge(self.tM.h, tok[f"qp_mm_{q}_{su2}_{jr}_{pi}"])
                            st = self.kvst[pi]
                            wt = tok.get(f"qp_st_{q}_{su2}_{jr - 1}_{pi}")
                            if wt is None:
                                wt = tok.get(f"qp_st_{q}_{su2 - 1}_1_{pi}")
                            if wt is None:
                                wt = tok.get(f"qp_st_{q - 1}_1_1_{pi}")
                            if wt is None:
                                wt = tok.get(f"pj_st_{NSUP - 1}_{6 + pi}")
                            if wt:
                                self.dmaw(s, wt)
                            tok[f"qp_ev_{q}_{su2}_{jr}_{pi}"] = self.sind(
                                s, lambda st=st, psd=psd: s.copy(st[:, :], psd[:, :]))

                    @blk.sync
                    def _(sy, q=q, su2=su2, jr=jr):
                        for pi in range(2):
                            j = 2 * jr + pi
                            h = 4 * q + j
                            sy.wait_ge(self.sS.h, tok[f"qp_ev_{q}_{su2}_{jr}_{pi}"])
                            tok[f"qp_st_{q}_{su2}_{jr}_{pi}"] = self.dma(
                                sy, self.d_Q[h, :, 512 * su2:512 * (su2 + 1)],
                                self.kvst[pi][:, :], self.sST[pi])

    def emit_attention_head(self, blk, slot, kv, h, gidx, par, nkb, prev_ao):
        tok = self.tok
        npair = nkb // 2
        tail0 = npair - 8
        Ksl, Vsl, PT, cmb, aoT = (self.Ksl, self.Vsl, self.PT, self.cmb,
                                  self.aoT)
        hseq = self.hseq
        self.hseq += 1
        qb = self.qs[hseq % 2]

        @blk.sync
        def _(sy):
            wt = self.slast.get(hseq - 2)
            if wt:
                sy.wait_ge(self.tM.h, wt)
            elif hseq < 2:
                for sv in self.p2_jdO:
                    self.dmaw(sy, sv)
            tok[f"qld_{hseq}"] = self.dma(
                sy, qb[:, :], self.d_Q[h, :, 256 * slot:256 * (slot + 1)],
                self.sQ2[hseq % 2])

        base = self.tM.n

        def s_tok(p):
            return base + (1 if p == 0 else 2 * p)

        def a_tok(p):
            return base + (2 * npair if p == npair - 1 else 3 + 2 * p)

        bc_tok = base + 2 * npair + 1

        exp_toks = []

        @blk.scalar
        def _(s):
            for p in range(npair):
                s.wait_ge(self.tM.h, s_tok(p))
                ps = (self.psA, self.psB)[p % 2]
                ins = s.activation(PT[p % 2][:, :], ps[:, :], AF.Exp)
                ins.then_inc(self.sS.h, 1)
                self.sS.n += 1
                exp_toks.append(self.sS.n)

        msk_toks = {}
        r_tok = [None]
        av_tok = [None]
        ao_tok = [None]

        @blk.gpsimd
        def _(g):
            for p in range(tail0, npair):
                g.wait_ge(self.sS.h, exp_toks[p])
                m = p - tail0
                for half in range(2):
                    g.wait_ge(self.gS.h, self.gS.n)
                    g.tensor_tensor(
                        PT[p % 2][:, 256 * half:256 * (half + 1)],
                        PT[p % 2][:, 256 * half:256 * (half + 1)],
                        cmb[:, 256 * (2 * m + half):256 * (2 * m + half + 1)],
                        ALU.mult).then_inc(self.gS.h, 1)
                    self.gS.n += 1
                msk_toks[p] = self.gS.n

        @blk.vector
        def _(v):
            v.wait_ge(self.tM.h, a_tok(npair - 1))
            r_tok[0] = self.vchain(v, lambda: v.reciprocal(self.rL[:, :],
                                                           self.psL))
            av_tok[0] = self.vchain(v, lambda: v.tensor_copy(
                self.avf[:, :], self.psC[:, 0:256]))
            v.wait_ge(self.tM.h, bc_tok)
            ao_tok[0] = self.vchain(v, lambda: v.tensor_tensor(
                aoT[:, 256 * h:256 * (h + 1)], self.avf[:, :],
                self.psD[:, 0:256], ALU.mult))
            tok[f"ao_{slot}_{h}"] = ao_tok[0]

        @blk.tensor
        def _(tn):
            if prev_ao is None:
                tn.wait_ge(self.sS.h, self.p2_join[1])
                tn.wait_ge(self.vS.h, self.p2_join[2])

            def s_pair(p):
                ps = (self.psA, self.psB)[p % 2]
                if p == 0:
                    tn.wait_ge(self.vS.h, tok[f"kshift_{gidx}"])
                    self.dmaw(tn, tok[f"kvload_{gidx}"])
                    self.dmaw(tn, tok[f"qld_{hseq}"])
                    if h == 4 * kv:
                        self.dmaw(tn, tok[f"cmask_{slot}"])
                if p >= 2:
                    tn.wait_ge(self.sS.h, exp_toks[p - 2])
                for half in range(2):
                    kb = 2 * p + half
                    ins = tn.matmul(ps[:, 256 * half:256 * (half + 1)],
                                    Ksl[par][:, 128 * kb:128 * (kb + 1)],
                                    qb[:, :], start=True, stop=True)
                ins.then_inc(self.tM.h, 1)
                self.tM.n += 1
                assert self.tM.n == s_tok(p), (self.tM.n, s_tok(p))

            def lav_pair(p):
                if p == 0 and prev_ao is not None:
                    tn.wait_ge(self.vS.h, prev_ao)
                if p == 0 and h == 0 and slot > 0:
                    tn.wait_ge(self.sS.h, tok[f"op_ev_{slot - 1}_{DT - 1}"])
                if p >= tail0:
                    tn.wait_ge(self.gS.h, msk_toks[p])
                else:
                    tn.wait_ge(self.sS.h, exp_toks[p])
                for half in range(2):
                    kb = 2 * p + half
                    tn.matmul(self.psL, self.onesb[:, :],
                              PT[p % 2][:, 256 * half:256 * (half + 1)],
                              start=(kb == 0), stop=(kb == nkb - 1))
                    ins = tn.matmul(self.psC[:, 0:256],
                                    Vsl[par][:, 128 * kb:128 * (kb + 1)],
                                    PT[p % 2][:, 256 * half:256 * (half + 1)],
                                    start=(kb == 0), stop=(kb == nkb - 1))
                ins.then_inc(self.tM.h, 1)
                self.tM.n += 1
                assert self.tM.n == a_tok(p), (self.tM.n, a_tok(p))

            s_pair(0)
            if npair > 1:
                s_pair(1)
            for p in range(2, npair):
                lav_pair(p - 2)
                s_pair(p)
            if npair > 1:
                lav_pair(npair - 2)
            lav_pair(npair - 1)
            tn.wait_ge(self.vS.h, r_tok[0])
            ins = tn.matmul(self.psD[:, 0:256], self.ones1f[:, :],
                            self.rL[:, :], start=True, stop=True)
            ins.then_inc(self.tM.h, 1)
            self.tM.n += 1
            assert self.tM.n == bc_tok

        self.slast[hseq] = base + (1 if npair == 1 else 2 * (npair - 1))
        return ao_tok[0]

    def emit_attn_outproj(self, blk, slot):
        tok = self.tok

        @blk.sync
        def _(sy, slot=slot):
            for c in range(2):
                wt = tok.get(f"x1st_{slot - 1}_{c}")
                if wt:
                    self.dmaw(sy, wt)
                if slot == 0:
                    sy.wait_ge(self.sS.h, self.p2_join[1])
                tok[f"xown_{slot}_{c}"] = self.dma(
                    sy, self.xb[c][:, :],
                    self.x_own[256 * slot + 128 * c:256 * slot + 128 * (c + 1),
                               :], self.sX[c])

        for d in range(DT):
            @blk.gpsimd
            def _(g, d=d, slot=slot):
                wt = tok.get(f"op_mm_{slot}_{d - 1}")
                if wt is None and slot > 0:
                    wt = tok.get(f"op_mm_{slot - 1}_{DT - 1}")
                if wt:
                    g.wait_ge(self.tM.h, wt)
                elif slot == 0 and d < 2:
                    for sv in self.p2_jdO:
                        self.dmaw(g, sv)
                tok[f"op_w_{slot}_{d}"] = self.dma(
                    g, self.woS[d % 2][:, :], self.woTr[d, :, :],
                    self.sWO[d % 2])

            @blk.tensor
            def _(tn, d=d, slot=slot):
                self.dmaw(tn, tok[f"op_w_{slot}_{d}"])
                if d == 0:
                    tn.wait_ge(self.vS.h, tok[f"ao_{slot}_{15}"])
                wt = tok.get(f"op_ev_{slot}_{d - 1}")
                if wt:
                    tn.wait_ge(self.sS.h, wt)
                for h in range(NH):
                    ins = tn.matmul(self.psF[:, 0:256],
                                    self.woS[d % 2][:, 128 * h:128 * (h + 1)],
                                    self.aoT[:, 256 * h:256 * (h + 1)],
                                    start=(h == 0), stop=(h == NH - 1))
                ins.then_inc(self.tM.h, 1)
                self.tM.n += 1
                tok[f"op_mm_{slot}_{d}"] = self.tM.n

            @blk.scalar
            def _(s, d=d, slot=slot):
                s.wait_ge(self.tM.h, tok[f"op_mm_{slot}_{d}"])
                if slot == 0 and d == 0:
                    self.dmaw(s, tok["bo_ld"])
                wt = tok.get(f"op_tp_{slot}_{d - 1}")
                if wt:
                    s.wait_ge(self.tM.h, wt)
                tok[f"op_ev_{slot}_{d}"] = self.sind(
                    s, lambda d=d: s.activation(
                        self.opbf[:, :], self.psF[:, 0:256], AF.Identity,
                        bias=self.boS[:, d:d + 1]))

            @blk.tensor
            def _(tn, d=d, slot=slot):
                tn.wait_ge(self.sS.h, tok[f"op_ev_{slot}_{d}"])
                wt = tok.get(f"op_add_{slot}_{d - 2}")
                if wt:
                    tn.wait_ge(self.vS.h, wt)
                for c in range(2):
                    ins = tn.matmul(self.psT[d % 2][:, 128 * c:128 * (c + 1)],
                                    self.opbf[:, 128 * c:128 * (c + 1)],
                                    self.identb[:, :], start=True, stop=True)
                ins.then_inc(self.tM.h, 1)
                self.tM.n += 1
                tok[f"op_tp_{slot}_{d}"] = self.tM.n

            @blk.vector
            def _(v, d=d, slot=slot):
                v.wait_ge(self.tM.h, tok[f"op_tp_{slot}_{d}"])
                if d == 0:
                    self.dmaw(v, tok[f"xown_{slot}_0"])
                    self.dmaw(v, tok[f"xown_{slot}_1"])
                for c in range(2):
                    self.vind(v, lambda d=d, c=c: v.tensor_tensor(
                        self.xb[c][:, 128 * d:128 * (d + 1)],
                        self.xb[c][:, 128 * d:128 * (d + 1)],
                        self.psT[d % 2][:, 128 * c:128 * (c + 1)], ALU.add))
                tok[f"op_add_{slot}_{d}"] = self.vS.n

        @blk.sync
        def _(sy, slot=slot):
            sy.wait_ge(self.vS.h, tok[f"op_add_{slot}_{DT - 1}"])
            for c in range(2):
                tok[f"x1st_{slot}_{c}"] = self.dma(
                    sy, self.d_X[0][256 * slot + 128 * c:
                                    256 * slot + 128 * (c + 1), :],
                    self.xb[c][:, :], self.sXR[c])
        self.mat_pump_section(blk, 6, nsg=4)

    def emit_ffn(self, blk, xin_d, xout_d, phase):
        tok = self.tok
        nc = self.nc
        pre = f"f{phase}"
        pfh = ExitStack()
        hT = pfh.enter_context(nc.sbuf_tensor(f"hT_{phase}",
                                              [128, NIT * 1024], BF16)).ap()
        pf1 = ExitStack()
        ln2T = pf1.enter_context(nc.sbuf_tensor(f"ln2T_{phase}",
                                                [128, HT * 1024], BF16)).ap()
        wgS = [pf1.enter_context(nc.sbuf_tensor(f"wgS{i}_{phase}",
                                                [128, HT * 128], BF16)).ap()
               for i in range(2)]
        wuS = [pf1.enter_context(nc.sbuf_tensor(f"wuS{i}_{phase}",
                                                [128, HT * 128], BF16)).ap()
               for i in range(2)]
        gsb = [pf1.enter_context(nc.sbuf_tensor(f"gsb{i}_{phase}",
                                                [128, 512], BF16)).ap()
               for i in range(2)]
        jt, js, jv = self.tM.n, self.sS.n, self.vS.n

        def xsrc(t):
            return xin_d[128 * t:128 * (t + 1), :]

        fw = [(self.tM, jt), (self.sS, js), (self.vS, jv)]
        if phase == 1:
            fw.append(tok["x1st_3_0"])
            fw.append(tok["x1st_3_1"])
            fw.extend(self.mat["stage2_guard"])
        else:
            fw.append((self.sFX, self.sFX.n))
        for t in range(8):
            self.emit_ln_tile(blk, pre, t, xsrc, ln2T, 1024,
                              fw if t < 2 else [])
            if phase == 1:
                self.mat_pump_section(blk, MAT_PUMP_FFN, nsg=6)
        ln_done = self.sS.n

        for it in range(NIT):
            pair = it % 2

            @blk.sync
            def _(sy, it=it, pair=pair):
                wt = tok.get(f"{pre}_gu_{it - 2}_1")
                if wt:
                    sy.wait_ge(self.tM.h, wt)
                if it == 0 and phase == 1:
                    for sv in self.mat_store_tok(2 * HT - 1):
                        self.dmaw(sy, sv)
                self.dma(sy, wgS[pair][:, :], self.d_Wg[it, :, :],
                         self.sWU[pair])
                tok[f"{pre}_wgu_{it}"] = self.dma(
                    sy, wuS[pair][:, :], self.d_Wu[it, :, :], self.sWU[pair])

            @blk.tensor
            def _(tn, it=it, pair=pair):
                self.dmaw(tn, tok[f"{pre}_wgu_{it}"])
                if it == 0:
                    tn.wait_ge(self.sS.h, ln_done)
                    tn.wait_ge(self.vS.h, tok[f"{pre}_tpev_7"])
                for q2 in range(2):
                    psg = (self.psC, self.psE)[q2]
                    psq = (self.psD, self.psF)[q2]
                    wt = tok.get(f"{pre}_sil_{it - 1}_{q2}")
                    if wt:
                        tn.wait_ge(self.sS.h, wt)
                    wt2 = tok.get(f"{pre}_hml_{it - 1}_{q2}")
                    if wt2:
                        tn.wait_ge(self.vS.h, wt2)
                    for ht in range(HT):
                        tn.matmul(psg[:, :],
                                  wgS[pair][:, 128 * ht:128 * (ht + 1)],
                                  ln2T[:, 1024 * ht + 512 * q2:
                                       1024 * ht + 512 * (q2 + 1)],
                                  start=(ht == 0), stop=(ht == HT - 1))
                    for ht in range(HT):
                        ins = tn.matmul(psq[:, :],
                                        wuS[pair][:, 128 * ht:128 * (ht + 1)],
                                        ln2T[:, 1024 * ht + 512 * q2:
                                             1024 * ht + 512 * (q2 + 1)],
                                        start=(ht == 0), stop=(ht == HT - 1))
                    ins.then_inc(self.tM.h, 1)
                    self.tM.n += 1
                    tok[f"{pre}_gu_{it}_{q2}"] = self.tM.n

            @blk.scalar
            def _(s, it=it):
                for q2 in range(2):
                    psg = (self.psC, self.psE)[q2]
                    s.wait_ge(self.tM.h, tok[f"{pre}_gu_{it}_{q2}"])
                    wt = tok.get(f"{pre}_hml_{it - 1}_{q2}")
                    if wt:
                        s.wait_ge(self.vS.h, wt)
                    tok[f"{pre}_sil_{it}_{q2}"] = self.sind(
                        s, lambda q2=q2, psg=psg: s.activation(
                            gsb[q2][:, :], psg[:, :], AF.Silu))

            @blk.vector
            def _(v, it=it):
                for q2 in range(2):
                    psq = (self.psD, self.psF)[q2]
                    v.wait_ge(self.sS.h, tok[f"{pre}_sil_{it}_{q2}"])
                    self.vind(v, lambda it=it, q2=q2, psq=psq: v.tensor_tensor(
                        hT[:, 1024 * it + 512 * q2:1024 * it + 512 * (q2 + 1)],
                        gsb[q2][:, :], psq[:, :], ALU.mult))
                    tok[f"{pre}_hml_{it}_{q2}"] = self.vS.n

            if phase == 1:
                self.mat_pump_section(blk, MAT_PUMP_FFN, nsg=4)

        if phase == 1:
            self.mat_drain(blk)
            self.mat_stack.close()
        gu_done_t = self.tM.n
        hml_done_v = self.vS.n
        pf1.close()

        pf2 = ExitStack()
        xst = [pf2.enter_context(nc.sbuf_tensor(f"xst{i}_{phase}",
                                                [128, D], F32, side="right")).ap()
               for i in range(8)]
        wdS = [pf2.enter_context(nc.sbuf_tensor(f"wdS{i}_{phase}",
                                                [128, NIT * 128], BF16)).ap()
               for i in range(2)]

        @blk.sync
        def _(sy):
            sy.wait_ge(self.tM.h, gu_done_t)
            sy.wait_ge(self.vS.h, hml_done_v)
            if phase == 1:
                for sv in self.mat_store_tok(len(self.mat_jobs) - 1):
                    self.dmaw(sy, sv)
            for t in range(8):
                self.dma(sy, xst[t][:, :], xin_d[128 * t:128 * (t + 1), :],
                         self.sXT)
            tok[f"{pre}_xst_ld"] = (self.sXT, self.sXT.n)

        for d in range(DT):
            pair = d % 2

            @blk.sync
            def _(sy, d=d, pair=pair):
                wt = tok.get(f"{pre}_dn_{d - 2}_1")
                if wt:
                    sy.wait_ge(self.tM.h, wt)
                tok[f"{pre}_wd_{d}"] = self.dma(
                    sy, wdS[pair][:, :], self.d_Wd[d, :, :], self.sWD[pair])

            @blk.tensor
            def _(tn, d=d, pair=pair):
                self.dmaw(tn, tok[f"{pre}_wd_{d}"])
                if d == 0:
                    tn.wait_ge(self.vS.h, tok[f"{pre}_tpev_7"])
                for q2 in range(2):
                    psd = (self.psA, self.psB)[q2]
                    wt = tok.get(f"{pre}_dnev_{d - 1}_{q2}")
                    if wt:
                        tn.wait_ge(self.sS.h, wt)
                    for it in range(NIT):
                        ins = tn.matmul(
                            psd[:, :], wdS[pair][:, 128 * it:128 * (it + 1)],
                            hT[:, 1024 * it + 512 * q2:1024 * it + 512 * (q2 + 1)],
                            start=(it == 0), stop=(it == NIT - 1))
                    ins.then_inc(self.tM.h, 1)
                    self.tM.n += 1
                    tok[f"{pre}_dn_{d}_{q2}"] = self.tM.n

            @blk.scalar
            def _(s, d=d):
                for q2 in range(2):
                    psd = (self.psA, self.psB)[q2]
                    s.wait_ge(self.tM.h, tok[f"{pre}_dn_{d}_{q2}"])
                    wt = tok.get(f"{pre}_dtp_{d - 1}_{q2}")
                    if wt:
                        s.wait_ge(self.tM.h, wt)
                    tok[f"{pre}_dnev_{d}_{q2}"] = self.sind(
                        s, lambda q2=q2, psd=psd: s.copy(
                            self.dnbf[q2][:, :], psd[:, :]))

            @blk.tensor
            def _(tn, d=d):
                for q2 in range(2):
                    tn.wait_ge(self.sS.h, tok[f"{pre}_dnev_{d}_{q2}"])
                    wt = tok.get(f"{pre}_dadd_{d - 1}_{q2}")
                    if wt:
                        tn.wait_ge(self.vS.h, wt)
                    for c in range(4):
                        ins = tn.matmul(
                            self.psT[q2][:, 128 * c:128 * (c + 1)],
                            self.dnbf[q2][:, 128 * c:128 * (c + 1)],
                            self.identb[:, :], start=True, stop=True)
                    ins.then_inc(self.tM.h, 1)
                    self.tM.n += 1
                    tok[f"{pre}_dtp_{d}_{q2}"] = self.tM.n

            @blk.vector
            def _(v, d=d):
                if d == 0:
                    self.dmaw(v, tok[f"{pre}_xst_ld"])
                for q2 in range(2):
                    v.wait_ge(self.tM.h, tok[f"{pre}_dtp_{d}_{q2}"])
                    for c in range(4):
                        t = 4 * q2 + c
                        self.vind(v, lambda t=t, d=d, q2=q2, c=c: v.tensor_tensor(
                            xst[t][:, 128 * d:128 * (d + 1)],
                            xst[t][:, 128 * d:128 * (d + 1)],
                            self.psT[q2][:, 128 * c:128 * (c + 1)], ALU.add))
                    tok[f"{pre}_dadd_{d}_{q2}"] = self.vS.n

        @blk.sync
        def _(sy):
            dst = xout_d if xout_d is not None else self.y_own
            for t in range(8):
                sy.wait_ge(self.vS.h, tok[f"{pre}_dadd_{DT - 1}_{t // 4}"])
                tok[f"{pre}_xst_{t}"] = self.dma(
                    sy, dst[128 * t:128 * (t + 1), :], xst[t][:, :], self.sFX)

        pf2.close()
        pfh.close()

    def emit_attn_b(self, blk):
        tok = self.tok
        nc = self.nc
        pre = "b"
        pb = ExitStack()
        ln3T = pb.enter_context(nc.sbuf_tensor("ln3T", [128, HT * 1024],
                                               BF16)).ap()
        wvbS = pb.enter_context(nc.sbuf_tensor("wvbS", [128, HT * 512],
                                               BF16)).ap()
        wpA = pb.enter_context(nc.sbuf_tensor("wpA", [128, KVH * D], BF16)).ap()
        wpB = pb.enter_context(nc.sbuf_tensor("wpB", [128, KVH * D], BF16)).ap()
        vbT = pb.enter_context(nc.sbuf_tensor("vbT", [128, KVH * 1024],
                                              BF16)).ap()
        xst = [pb.enter_context(nc.sbuf_tensor(f"bxst{i}", [128, D], F32,
                                               side="right")).ap()
               for i in range(8)]

        jt, js, jv = self.tM.n, self.sS.n, self.vS.n

        @blk.gpsimd
        def _(g):
            g.wait_ge(self.tM.h, jt)
            g.wait_ge(self.sS.h, js)
            g.wait_ge(self.vS.h, jv)
            self.dma(g, wvbS[:, :].rearrange("p (h f) -> p h f", f=512),
                     self.wvbT.rearrange("(h p) f -> p h f", p=128), self.sWS)
            tok["wvb_ld"] = (self.sWS, self.sWS.n)
            wp4 = self.wpT.rearrange("(kv j p) d -> j p kv d", kv=4, p=128)
            tok["wp_0"] = self.dma(
                g, wpA[:, :].rearrange("p (kv d) -> p kv d", d=D),
                wp4[0, :, :, :], self.sWS)

        for j in range(1, 4):
            @blk.gpsimd
            def _(g, j=j):
                if j > 1:
                    g.wait_ge(self.vS.h, tok[f"wp_add_{j - 1}"])
                tok[f"wp_l{j}"] = self.dma(
                    g, wpB[:, :].rearrange("p (kv d) -> p kv d", d=D),
                    self.wpT.rearrange("(kv j p) d -> j p kv d",
                                       kv=4, p=128)[j, :, :, :], self.sWS)

            @blk.vector
            def _(v, j=j):
                self.dmaw(v, tok[f"wp_l{j}"])
                tok[f"wp_add_{j}"] = self.vchain(v, lambda: v.tensor_tensor(
                    wpA[:, :], wpA[:, :], wpB[:, :], ALU.add))

        def xsrc(t):
            return self.d_X[1][128 * t:128 * (t + 1), :]

        fw = [(self.tM, jt), (self.sS, js), (self.vS, jv),
              (self.sFX, self.sFX.n)]
        for t in range(8):
            self.emit_ln_tile(blk, pre, t, xsrc, ln3T, 1024,
                              fw if t < 2 else [])
        ln_done = self.sS.n

        for f in range(4):
            @blk.tensor
            def _(tn, f=f):
                if f == 0:
                    tn.wait_ge(self.sS.h, ln_done)
                    tn.wait_ge(self.vS.h, tok[f"{pre}_tpev_7"])
                    self.dmaw(tn, (self.sWS, self.sWS.n))
                for q2 in range(2):
                    psd = (self.psC, self.psD)[q2]
                    wt = tok.get(f"vb_ev_{f - 1}_{q2}")
                    if wt:
                        tn.wait_ge(self.sS.h, wt)
                    for ht in range(HT):
                        ins = tn.matmul(psd[:, :],
                                        wvbS[:, 512 * ht + 128 * f:
                                             512 * ht + 128 * (f + 1)],
                                        ln3T[:, 1024 * ht + 512 * q2:
                                             1024 * ht + 512 * (q2 + 1)],
                                        start=(ht == 0), stop=(ht == HT - 1))
                    ins.then_inc(self.tM.h, 1)
                    self.tM.n += 1
                    tok[f"vb_mm_{f}_{q2}"] = self.tM.n

            @blk.scalar
            def _(s, f=f):
                for q2 in range(2):
                    psd = (self.psC, self.psD)[q2]
                    s.wait_ge(self.tM.h, tok[f"vb_mm_{f}_{q2}"])
                    dst = vbT[:, 1024 * f + 512 * q2:1024 * f + 512 * (q2 + 1)]
                    tok[f"vb_ev_{f}_{q2}"] = self.sind(
                        s, lambda dst=dst, psd=psd: s.copy(dst, psd[:, :]))

        @blk.sync
        def _(sy):
            sy.wait_ge(self.sS.h, tok["vb_ev_3_1"])
            for t in range(8):
                self.dma(sy, xst[t][:, :],
                         self.d_X[1][128 * t:128 * (t + 1), :], self.sXT)
            tok["b_xst_ld"] = (self.sXT, self.sXT.n)

        for d in range(DT):
            @blk.tensor
            def _(tn, d=d):
                if d == 0:
                    tn.wait_ge(self.vS.h, tok["wp_add_3"])
                    tn.wait_ge(self.sS.h, tok["vb_ev_3_1"])
                for q2 in range(2):
                    psd = (self.psA, self.psB)[q2]
                    wt = tok.get(f"b_dnev_{d - 1}_{q2}")
                    if wt:
                        tn.wait_ge(self.sS.h, wt)
                    for f in range(4):
                        ins = tn.matmul(psd[:, :],
                                        wpA[:, D * f + 128 * d:
                                            D * f + 128 * (d + 1)],
                                        vbT[:, 1024 * f + 512 * q2:
                                            1024 * f + 512 * (q2 + 1)],
                                        start=(f == 0), stop=(f == 3))
                    ins.then_inc(self.tM.h, 1)
                    self.tM.n += 1
                    tok[f"b_dn_{d}_{q2}"] = self.tM.n

            @blk.scalar
            def _(s, d=d):
                for q2 in range(2):
                    psd = (self.psA, self.psB)[q2]
                    s.wait_ge(self.tM.h, tok[f"b_dn_{d}_{q2}"])
                    wt = tok.get(f"b_dtp_{d - 1}_{q2}")
                    if wt:
                        s.wait_ge(self.tM.h, wt)
                    tok[f"b_dnev_{d}_{q2}"] = self.sind(
                        s, lambda q2=q2, psd=psd: s.copy(
                            self.dnbf[q2][:, :], psd[:, :]))

            @blk.tensor
            def _(tn, d=d):
                for q2 in range(2):
                    tn.wait_ge(self.sS.h, tok[f"b_dnev_{d}_{q2}"])
                    wt = tok.get(f"b_dadd_{d - 1}_{q2}")
                    if wt:
                        tn.wait_ge(self.vS.h, wt)
                    for c in range(4):
                        ins = tn.matmul(
                            self.psT[q2][:, 128 * c:128 * (c + 1)],
                            self.dnbf[q2][:, 128 * c:128 * (c + 1)],
                            self.identb[:, :], start=True, stop=True)
                    ins.then_inc(self.tM.h, 1)
                    self.tM.n += 1
                    tok[f"b_dtp_{d}_{q2}"] = self.tM.n

            @blk.vector
            def _(v, d=d):
                if d == 0:
                    self.dmaw(v, tok["b_xst_ld"])
                for q2 in range(2):
                    v.wait_ge(self.tM.h, tok[f"b_dtp_{d}_{q2}"])
                    for c in range(4):
                        t = 4 * q2 + c
                        self.vind(v, lambda t=t, d=d, q2=q2, c=c: v.tensor_tensor(
                            xst[t][:, 128 * d:128 * (d + 1)],
                            xst[t][:, 128 * d:128 * (d + 1)],
                            self.psT[q2][:, 128 * c:128 * (c + 1)], ALU.add))
                    tok[f"b_dadd_{d}_{q2}"] = self.vS.n

        @blk.sync
        def _(sy):
            for t in range(8):
                sy.wait_ge(self.vS.h, tok[f"b_dadd_{DT - 1}_{t // 4}"])
                tok[f"b_xst_{t}"] = self.dma(
                    sy, self.d_X[2][128 * t:128 * (t + 1), :], xst[t][:, :],
                    self.sFX)

        pb.close()


_prog_cache = {}


def _get_prog():
    if "nc" not in _prog_cache:
        b = Builder()
        _prog_cache["nc"] = b.build()
    return _prog_cache["nc"]


def kernel(L0_attn_a_hidden_states, L0_attn_a_Wq, L0_attn_a_Wk, L0_attn_a_Wv,
           L0_attn_a_Wo, L0_attn_a_bo, L0_attn_b_weight_qkv,
           L0_attn_b_weight_proj, L0_ffn_lut, L0_ffn_gate_idx, L0_ffn_up_idx,
           L0_ffn_down_idx):
    f32 = np.float32
    x = np.ascontiguousarray(np.asarray(L0_attn_a_hidden_states, f32)
                             .reshape(SEQ, D))
    wqT = np.ascontiguousarray(np.asarray(L0_attn_a_Wq, f32).T)
    wkT = np.ascontiguousarray(np.asarray(L0_attn_a_Wk, f32).T)
    wvT = np.ascontiguousarray(np.asarray(L0_attn_a_Wv, f32).T)
    wo = np.asarray(L0_attn_a_Wo, f32)
    woTr = np.ascontiguousarray(
        wo.reshape(DT, 128, NH, 128).transpose(0, 3, 2, 1)).reshape(
            DT, 128, NH * 128)
    boP = np.ascontiguousarray(np.asarray(L0_attn_a_bo, f32)
                               .reshape(DT, 128).T)
    qkv = np.asarray(L0_attn_b_weight_qkv, f32)
    wvb = np.concatenate([qkv[768 * g + 640:768 * (g + 1)] for g in range(4)],
                         axis=0)
    wvbT = np.ascontiguousarray(wvb.T)
    wpT = np.ascontiguousarray(np.asarray(L0_attn_b_weight_proj, f32).T)
    lutp = np.ascontiguousarray(
        np.broadcast_to(np.asarray(L0_ffn_lut, f32), (128, 16)))
    gidxT = np.ascontiguousarray(np.asarray(L0_ffn_gate_idx, np.int32).T)
    uidxT = np.ascontiguousarray(np.asarray(L0_ffn_up_idx, np.int32).T)
    didxT = np.ascontiguousarray(np.asarray(L0_ffn_down_idx, np.int32).T)
    ident = np.eye(128, dtype=f32)

    pp = np.arange(128)[:, None]
    cq = np.arange(BLK)[None, :]
    cmasks = []
    for c in range(NCORES):
        cm = np.zeros((NSLOT, 128, 16 * BLK), f32)
        for k in range(NSLOT):
            for m in range(16):
                valid = (m * 128 + pp) <= (2 * c * 128 + cq)
                cm[k, :, BLK * m:BLK * (m + 1)] = valid.astype(f32)
        cmasks.append(cm)

    nc = _get_prog()
    in_maps = []
    for c in range(NCORES):
        rows = np.concatenate([np.arange(256 * (8 * k + c), 256 * (8 * k + c) + 256)
                               for k in range(NSLOT)])
        in_maps.append(dict(
            x_full=x, x_own=np.ascontiguousarray(x[rows]), wqT=wqT, wkT=wkT,
            wvT=wvT, woTr=woTr, boP=boP, wvbT=wvbT, wpT=wpT, lutp=lutp,
            gidxT=gidxT, uidxT=uidxT, didxT=didxT, cmask=cmasks[c],
            ident=ident))
    res = run_bass_kernel_spmd(nc, in_maps, core_ids=list(range(NCORES)))
    _prog_cache["taps"] = res.results
    y = np.empty((SEQ, D), f32)
    for c in range(NCORES):
        yo = np.asarray(res.results[c]["y_own"])
        for k in range(NSLOT):
            b = 8 * k + c
            y[256 * b:256 * (b + 1)] = yo[256 * k:256 * (k + 1)]
    return y.reshape(1, SEQ, D)


if __name__ == "__main__":
    b = Builder()
    nc = b.build()
    print("program built OK")
